# version 25
# baseline (speedup 1.0000x reference)
import contextlib
import numpy as np
import ml_dtypes
import concourse.bass as bass
import concourse.mybir as mybir
from concourse.bass_utils import run_bass_kernel_spmd

F32 = mybir.dt.float32
BF16 = mybir.dt.bfloat16
I32 = mybir.dt.int32
U16 = mybir.dt.uint16
FP8 = mybir.dt.float8e4
OP = mybir.AluOpType
AF = mybir.ActivationFunctionType
AX = mybir.AxisListType.X

MAGIC_E2M1 = float(np.float32(1.5 * 2 ** 22))
ACT_CLIP = float(np.float32(15.0 / 128.0))
EPS_ATT = 1e-6
EPS_LN = 1e-5
ENGS = ("sync", "vector", "scalar", "tensor", "gpsimd")
USE_2Q_A = False
USE_2Q_B = False
USE_2Q_F = True
USE_2Q_FP = False


class CFG:
    def __init__(self, full=True):
        if full:
            self.NC, self.T, self.D, self.I, self.H = 8, 2048, 2048, 8192, 16
            self.TBLK, self.SB = 512, 256
        else:
            self.NC, self.T, self.D, self.I, self.H = 2, 512, 512, 1024, 4
            self.TBLK, self.SB = 256, 256
        self.DH, self.M, self.QB = 128, 64, 32
        self.HP = self.H // 2
        self.nD = self.D // 128
        self.nB = self.T // self.TBLK
        self.nTT = self.TBLK // 128
        self.nSB = self.T // self.SB
        self.nST = self.SB // 128
        self.nI = self.I // 128
        self.OSW = min(512, self.I, self.I // self.NC)
        self.nOS = self.I // self.OSW
        self.WSH = self.D // self.NC
        self.FSH = self.I // self.NC
        self.DSH = self.D // self.NC
        assert self.H * self.DH == self.D
        self.GF_OFF, off = {}, 0
        for nm in ("qT", "kT", "vT", "oT"):
            self.GF_OFF[nm] = off
            off += self.D * self.WSH
        self.GFN = off
        self.GB_OFF, off = {}, 0
        for nm, sz in (("gq", self.D * self.FSH), ("uq", self.D * self.FSH),
                       ("dq", self.I * self.DSH)):
            self.GB_OFF[nm] = off
            off += sz
        self.GBN = off


class Sem:
    def __init__(self, h):
        self.h, self.n = h, 0

    def inc(self, k=1):
        self.n += k
        return self.n


class _CountInst:
    def then_inc(self, *a, **k):
        return self


class _CountMock:

    def __init__(self):
        self.count = 0

    def wait_ge(self, *a, **k):
        return None

    def __getattr__(self, name):
        def f(*a, **k):
            self.count += 1
            return _CountInst()
        return f


class _VecProxy:

    _PASS = ("wait_ge",)

    def __init__(self, e, semh, n0):
        object.__setattr__(self, "_e", e)
        object.__setattr__(self, "_semh", semh)
        object.__setattr__(self, "_n", n0)

    def __getattr__(self, name):
        a = getattr(self._e, name)
        if name in self._PASS or not callable(a):
            return a

        def f(*args, **kw):
            if self._n > 0:
                self._e.wait_ge(self._semh, self._n)
            inst = a(*args, **kw)
            inst.then_inc(self._semh, 1)
            object.__setattr__(self, "_n", self._n + 1)
            return inst
        return f


class Prog:

    def __init__(self, S):
        self.q = {e: [] for e in ENGS}
        self.sv = S["sv"]
        self.v0 = S["sv"].n

    def on(self, eng):
        def deco(fn):
            self.q[eng].append(fn)
            if eng == "vector":
                m = _CountMock()
                fn(m)
                self.sv.inc(m.count)
        return deco

    def wait(self, eng, sem):
        v = sem.n
        if v > 0:
            self.q[eng].append(lambda e, _s=sem.h, _v=v: e.wait_ge(_s, _v))

    def wait_val(self, eng, sem, v):
        if v > 0:
            self.q[eng].append(lambda e, _s=sem.h, _v=v: e.wait_ge(_s, _v))

    def play(self, nc):
        with nc.Block() as blk:
            for eng in ENGS:
                fns = self.q[eng]
                if not fns:
                    continue
                deco = getattr(blk, eng)
                if eng == "vector":
                    def body(e, _fns=fns, _n0=self.v0):
                        px = _VecProxy(e, self.sv.h, _n0)
                        for f in _fns:
                            f(px)
                else:
                    def body(e, _fns=fns):
                        for f in _fns:
                            f(e)
                deco(body)


def barrier(p, sems, engs=ENGS):
    for eng in engs:
        for s in sems:
            p.wait(eng, s)


SEM_NAMES = ["sg", "sv", "sa", "st", "sp", "scc",
             "sco", "sl", "sl1", "stf", "stb",
             "w0", "w1", "x0", "x1", "r0", "r1", "f0", "f1",
             "gu0", "gu1", "wd0", "wd1", "sxq", "skv", "ssp", "so",
             "si0", "si1", "sj0", "sj1"]


def build_nc(c):
    nc = bass.Bass("TRN2", target_bir_lowering=False)
    io = {}
    io["x_own"] = nc.dram_tensor("x_own", [c.T, c.D], F32, kind="ExternalInput").ap()
    for nm in ("wq_s", "wk_s", "wv_s", "wo_s"):
        io[nm] = nc.dram_tensor(nm, [c.WSH, c.D], F32, kind="ExternalInput").ap()
    io["wg_s"] = nc.dram_tensor("wg_s", [c.FSH, c.D], F32, kind="ExternalInput").ap()
    io["wu_s"] = nc.dram_tensor("wu_s", [c.FSH, c.D], F32, kind="ExternalInput").ap()
    io["wd_s"] = nc.dram_tensor("wd_s", [c.DSH, c.I], F32, kind="ExternalInput").ap()
    io["biases"] = nc.dram_tensor("biases", [4, c.D], F32, kind="ExternalInput").ap()
    io["wmaps"] = nc.dram_tensor("wmaps", [2 * c.M, c.DH], F32, kind="ExternalInput").ap()
    io["idf"] = nc.dram_tensor("idf", [128, 128], F32, kind="ExternalInput").ap()
    io["idb"] = nc.dram_tensor("idb", [128, 128], BF16, kind="ExternalInput").ap()
    io["out"] = nc.dram_tensor("out", [c.T, c.D], F32, kind="ExternalOutput").ap()

    io["gin_f"] = nc.dram_tensor("gin_f", [c.GFN], F32)
    _asp = dict(addr_space="Shared") if c.NC > 4 else {}
    io["gout_f"] = nc.dram_tensor("gout_f", [c.NC * c.GFN], F32, **_asp)
    io["gin_b"] = nc.dram_tensor("gin_b", [c.GBN], BF16)
    io["gout_b"] = nc.dram_tensor("gout_b", [c.NC * c.GBN], BF16, **_asp)
    nKB = (c.HP + 1) // 2
    KVW = 512 * nKB + 16
    io["kv_bin"] = nc.dram_tensor("kv_bin", [128, KVW], F32)
    io["kv_bout"] = nc.dram_tensor("kv_bout", [128, KVW], F32)
    io["x1_sp"] = nc.dram_tensor("x1_sp", [c.T, c.D], F32)
    io["x1q_sp"] = nc.dram_tensor("x1q_sp", [c.T, c.D], BF16)
    io["iq_sp"] = nc.dram_tensor("iq_sp", [c.I, c.T], BF16)

    with contextlib.ExitStack() as es:
        S = {nm: Sem(es.enter_context(nc.semaphore(nm))) for nm in SEM_NAMES}
        _build(nc, c, io, S)
    return nc


def _build(nc, c, io, S):
    sg, sv, sa, st, sp, scc = (S[k] for k in ("sg", "sv", "sa", "st", "sp", "scc"))
    nKB = (c.HP + 1) // 2
    KVW = 512 * nKB + 16

    pers = [
        nc.sbuf_tensor("idf_t", [128, 128], F32),
        nc.sbuf_tensor("idb_t", [128, 128], BF16),
        nc.sbuf_tensor("bias_t", [128, 4 * c.nD], F32),
        nc.sbuf_tensor("wmT", [128, 128], F32),
        nc.sbuf_tensor("ones_f", [128, 8], F32),
        nc.sbuf_tensor("kvacc", [128, KVW], F32),
    ]
    hp_ = [t.__enter__() for t in pers]
    idf_t, idb_t, bias_t, wmT, ones_f, kvacc = hp_

    def gfw(nm, oc):
        rk, r = oc // (c.WSH // 128), oc % (c.WSH // 128)
        base = rk * c.GFN + c.GF_OFF[nm] + r * c.D * 128
        return io["gout_f"][base:base + c.D * 128].rearrange(
            "(p dd o) -> p dd o", p=128, o=128)

    _phase_w(nc, c, io, S, idf_t, idb_t, bias_t, wmT, ones_f, kvacc)

    with (
        nc.sbuf_tensor("QM", [128, c.HP * c.T], F32) as QM,
    ):
        _phase_a(nc, c, io, S, gfw, idf_t, bias_t, wmT, ones_f, kvacc, QM, nKB)

        p = Prog(S)
        skv = S["skv"]
        barrier(p, [sv], ["sync"])

        @p.on("sync")
        def _(e):
            e.dma_start(out=io["kv_bin"][:, :], in_=kvacc[:, :]).then_inc(skv.h, 16)
        skv.inc(16)
        p.wait("gpsimd", skv)

        @p.on("gpsimd")
        def _(e):
            e.collective_compute(
                "AllReduce", OP.add,
                replica_groups=[[2 * i, 2 * i + 1] for i in range(c.NC // 2)],
                ins=[io["kv_bin"].ap().opt()], outs=[io["kv_bout"].ap().opt()],
            ).then_inc(scc.h, 1)
        scc.inc(1)
        p.wait("gpsimd", scc)

        @p.on("gpsimd")
        def _(e):
            e.dma_start(out=kvacc[:, :], in_=io["kv_bout"][:, :]).then_inc(sg.h, 16)
        sg.inc(16)
        p.play(nc)

        _phase_b(nc, c, io, S, gfw, idf_t, idb_t, bias_t, kvacc, QM, nKB)

    _phase_f(nc, c, io, S, idb_t)

    for t in reversed(pers):
        t.__exit__(None, None, None)


def _act_quant(p, c, x_ap, W, tl, out_bf):
    nb = W // 32

    @p.on("vector")
    def _(v):
        x3 = x_ap.rearrange("p (b k) -> p b k", k=32)
        v.tensor_reduce(out=tl["amax"][:, 0:nb].bitcast(F32), in_=x3, axis=AX,
                        op=OP.max, apply_absolute_value=True)
        v.tensor_scalar(out=tl["sc8"][:, 0:nb], in0=tl["amax"][:, 0:nb],
                        scalar1=23, scalar2=0xFF,
                        op0=OP.logical_shift_right, op1=OP.bitwise_and)
        v.tensor_scalar(out=tl["inv"][:, 0:nb], in0=tl["sc8"][:, 0:nb],
                        scalar1=-1.0, scalar2=250.0, op0=OP.mult, op1=OP.add)
        v.tensor_scalar(out=tl["inv"][:, 0:nb], in0=tl["inv"][:, 0:nb],
                        scalar1=23, scalar2=None, op0=OP.logical_shift_left)
        v.tensor_scalar(out=tl["m64"][:, 0:nb], in0=tl["sc8"][:, 0:nb],
                        scalar1=4.0, scalar2=None, op0=OP.add)
        v.tensor_scalar(out=tl["m64"][:, 0:nb], in0=tl["m64"][:, 0:nb],
                        scalar1=23, scalar2=None, op0=OP.logical_shift_left)
        v.tensor_tensor(out=tl["z"][:, 0:W].rearrange("p (b k) -> p b k", k=32),
                        in0=x3,
                        in1=tl["inv"][:, 0:nb].bitcast(F32).unsqueeze(2)
                        .broadcast_to([128, nb, 32]), op=OP.mult)
        v.tensor_scalar(out=tl["z"][:, 0:W], in0=tl["z"][:, 0:W],
                        scalar1=ACT_CLIP, scalar2=-ACT_CLIP,
                        op0=OP.min, op1=OP.max)
        v.tensor_copy(tl["q8"][:, 0:W], tl["z"][:, 0:W])
        v.tensor_tensor(out=out_bf[:, 0:W].rearrange("p (b k) -> p b k", k=32),
                        in0=tl["q8"][:, 0:W].rearrange("p (b k) -> p b k", k=32),
                        in1=tl["m64"][:, 0:nb].bitcast(F32).unsqueeze(2)
                        .broadcast_to([128, nb, 32]), op=OP.mult)


def _wgt_quant(p, c, w_ap, W, tl, out_bf):
    nb = W // 32

    @p.on("vector")
    def _(v):
        w3 = w_ap.rearrange("p (b k) -> p b k", k=32)
        v.tensor_reduce(out=tl["amax"][:, 0:nb].bitcast(F32), in_=w3, axis=AX,
                        op=OP.max, apply_absolute_value=True)
        v.tensor_scalar(out=tl["sc8"][:, 0:nb], in0=tl["amax"][:, 0:nb],
                        scalar1=23, scalar2=0xFF,
                        op0=OP.logical_shift_right, op1=OP.bitwise_and)
        v.tensor_scalar(out=tl["scl"][:, 0:nb], in0=tl["sc8"][:, 0:nb],
                        scalar1=-2.0, scalar2=None, op0=OP.add)
        v.tensor_scalar(out=tl["scl"][:, 0:nb], in0=tl["scl"][:, 0:nb],
                        scalar1=23, scalar2=None, op0=OP.logical_shift_left)
        v.tensor_scalar(out=tl["clp"][:, 0:nb], in0=tl["sc8"][:, 0:nb],
                        scalar1=23, scalar2=0x400000,
                        op0=OP.logical_shift_left, op1=OP.bitwise_or)
        v.tensor_scalar(out=tl["nclp"][:, 0:nb], in0=tl["clp"][:, 0:nb].bitcast(F32),
                        scalar1=-1.0, scalar2=None, op0=OP.mult)
        v.tensor_scalar(out=tl["ti"][:, 0:W], in0=w_ap.bitcast(I32),
                        scalar1=0x7F800000, scalar2=None, op0=OP.bitwise_and)
        v.tensor_tensor(out=tl["p2"][:, 0:W].rearrange("p (b k) -> p b k", k=32),
                        in0=tl["ti"][:, 0:W].bitcast(F32)
                        .rearrange("p (b k) -> p b k", k=32),
                        in1=tl["scl"][:, 0:nb].bitcast(F32).unsqueeze(2)
                        .broadcast_to([128, nb, 32]), op=OP.max)
        v.tensor_scalar(out=tl["bt"][:, 0:W], in0=tl["p2"][:, 0:W],
                        scalar1=MAGIC_E2M1, scalar2=None, op0=OP.mult)
        v.tensor_tensor(out=tl["p2"][:, 0:W], in0=w_ap, in1=tl["bt"][:, 0:W],
                        op=OP.add)
        v.tensor_tensor(out=tl["p2"][:, 0:W], in0=tl["p2"][:, 0:W],
                        in1=tl["bt"][:, 0:W], op=OP.subtract)
        v.tensor_tensor(out=tl["p2"][:, 0:W].rearrange("p (b k) -> p b k", k=32),
                        in0=tl["p2"][:, 0:W].rearrange("p (b k) -> p b k", k=32),
                        in1=tl["clp"][:, 0:nb].bitcast(F32).unsqueeze(2)
                        .broadcast_to([128, nb, 32]), op=OP.min)
        v.tensor_tensor(out=out_bf[:, 0:W].rearrange("p (b k) -> p b k", k=32),
                        in0=tl["p2"][:, 0:W].rearrange("p (b k) -> p b k", k=32),
                        in1=tl["nclp"][:, 0:nb].unsqueeze(2)
                        .broadcast_to([128, nb, 32]), op=OP.max)


def _phase_w(nc, c, io, S, idf_t, idb_t, bias_t, wmT, ones_f, kvacc):
    sv, sa, st, sp, scc = (S[k] for k in ("sv", "sa", "st", "sp", "scc"))
    sco, sl, sl1 = S["sco"], S["sl"], S["sl1"]
    stf, stb = S["stf"], S["stb"]
    _ES = contextlib.ExitStack()
    WJ = min(4096, c.I)
    wstage = _ES.enter_context(nc.sbuf_tensor("wstage", [128, 2 * WJ], F32))
    wbf = _ES.enter_context(nc.sbuf_tensor("wbf", [128, 2 * WJ], BF16))
    wbf2 = _ES.enter_context(nc.sbuf_tensor("wbf2", [128, 2 * WJ], BF16))
    wTf = _ES.enter_context(nc.sbuf_tensor("wTf", [128, c.D], F32))
    wTb = _ES.enter_context(nc.sbuf_tensor("wTb", [128, WJ], BF16))
    q_amax = _ES.enter_context(nc.sbuf_tensor("q_amax", [128, WJ // 32], I32))
    q_sc8 = _ES.enter_context(nc.sbuf_tensor("q_sc8", [128, WJ // 32], I32))
    q_scl = _ES.enter_context(nc.sbuf_tensor("q_scl", [128, WJ // 32], I32))
    q_clp = _ES.enter_context(nc.sbuf_tensor("q_clp", [128, WJ // 32], I32))
    q_nclp = _ES.enter_context(nc.sbuf_tensor("q_nclp", [128, WJ // 32], F32))
    q_ti = _ES.enter_context(nc.sbuf_tensor("q_ti", [128, WJ], I32))
    q_p2 = _ES.enter_context(nc.sbuf_tensor("q_p2", [128, WJ], F32))
    q_bt = _ES.enter_context(nc.sbuf_tensor("q_bt", [128, WJ], F32))
    wm_st = _ES.enter_context(nc.sbuf_tensor("wm_st", [128, 128], F32))
    psw0 = _ES.enter_context(nc.psum_tensor("psw0", [128, 512], F32))
    psw1 = _ES.enter_context(nc.psum_tensor("psw1", [128, 1024], BF16))
    tl = dict(amax=q_amax, sc8=q_sc8, scl=q_scl, clp=q_clp, nclp=q_nclp,
              ti=q_ti, p2=q_p2, bt=q_bt)
    p = Prog(S)

    @p.on("sync")
    def _(e):
        e.dma_start(out=idf_t[:, :], in_=io["idf"][:, :]).then_inc(sco.h, 16)
        e.dma_start(out=idb_t[:, :], in_=io["idb"][:, :]).then_inc(sco.h, 16)
        for m in range(4):
            e.dma_start(out=bias_t[:, m * c.nD:(m + 1) * c.nD],
                        in_=io["biases"][m, :].rearrange("(oc p) -> p oc", p=128),
                        allow_slow_non_contiguous=True).then_inc(sco.h, 16)
        e.dma_start(out=wm_st[:, :], in_=io["wmaps"][:, :]).then_inc(sco.h, 16)
    sco.inc(16 * 7)
    p.wait("scalar", sco)

    @p.on("scalar")
    def _(e):
        e.activation(out=ones_f[:, :], in_=wm_st[:, 0:8], func=AF.Copy,
                     scale=0.0, bias=1.0).then_inc(sa.h, 1)
    sa.inc()
    p.wait("vector", sco)

    @p.on("vector")
    def _(e):
        e.memset(kvacc[:, :], 0.0)
    p.wait("tensor", sco)

    @p.on("tensor")
    def _(e):
        e.matmul(psw0[:, 0:128], wm_st[:, :], idf_t[:, :], is_transpose=True,
                 start=True, stop=True).then_inc(st.h, 1)
    st.inc()
    p.wait("scalar", st)

    @p.on("scalar")
    def _(e):
        e.activation(out=wmT[:, :], in_=psw0[:, 0:128],
                     func=AF.Copy).then_inc(sa.h, 1)
    sa.inc()

    p.wait("tensor", sa)

    jobs = []
    for nm, srcnm in (("qT", "wq_s"), ("kT", "wk_s"), ("vT", "wv_s"),
                      ("oT", "wo_s")):
        for r in range(c.WSH // 128):
            jobs.append(("attn", nm, srcnm, r, 0, c.D))
    for nm, srcnm in (("gq", "wg_s"), ("uq", "wu_s")):
        for r in range(c.FSH // 128):
            jobs.append(("ffn", nm, srcnm, r, 0, c.D))
    for r in range(c.DSH // 128):
        for c0 in range(0, c.I, WJ):
            jobs.append(("ffn", "dq", "wd_s", r, c0, WJ))

    wst_free = [(0, 0), (0, 0)]
    wbf_free = [0, 0]
    wbf2_free = [0, 0]
    wTf_free = 0
    wTb_free = 0
    attn_done = False
    for jidx, (kind, nm, srcnm, r, c0, W) in enumerate(jobs):
        nT = W // 128
        slot = jidx % 2
        sb0 = slot * WJ
        slsem = sl if slot == 0 else sl1
        st_t, sv_t = wst_free[slot]
        p.wait_val("sync", st, st_t)
        p.wait_val("sync", sv, sv_t)

        @p.on("sync")
        def _(e, _s=srcnm, _r=r, _c0=c0, _W=W, _b=sb0, _sem=slsem):
            e.dma_start(out=wstage[:, _b:_b + _W],
                        in_=io[_s][_r * 128:(_r + 1) * 128, _c0:_c0 + _W]
                        ).then_inc(_sem.h, 16)
        slsem.inc(16)
        p.wait("vector", slsem)
        p.wait("tensor", slsem)

        if kind == "attn":
            for j in range(nT):
                p.wait("tensor", sv)

                @p.on("tensor")
                def _(e, _j=j, _b=sb0):
                    e.matmul(psw0[:, 0:128],
                             wstage[:, _b + _j * 128:_b + (_j + 1) * 128],
                             idf_t[:, :], is_transpose=True,
                             start=True, stop=True).then_inc(st.h, 1)
                st.inc()
                if j == 0:
                    p.wait_val("vector", stf, wTf_free)
                p.wait("vector", st)

                @p.on("vector")
                def _(e, _j=j):
                    e.tensor_copy(wTf[:, _j * 128:(_j + 1) * 128], psw0[:, 0:128])
            wst_free[slot] = (st.n, 0)
            p.wait("sync", sv)

            @p.on("sync")
            def _(e, _nm=nm, _r=r):
                base = c.GF_OFF[_nm] + _r * c.D * 128
                e.dma_start(out=io["gin_f"][base:base + c.D * 128]
                            .rearrange("(p x) -> p x", p=128),
                            in_=wTf[:, :]).then_inc(stf.h, 16)
            stf.inc(16)
            wTf_free = stf.n
        else:
            if not attn_done:
                attn_done = True
                p.wait("gpsimd", stf)

                @p.on("gpsimd")
                def _(e):
                    e.collective_compute(
                        "AllGather", OP.bypass,
                        replica_groups=[list(range(c.NC))],
                        ins=[io["gin_f"].ap().opt()],
                        outs=[io["gout_f"].ap().opt()],
                    ).then_inc(scc.h, 1)
                scc.inc(1)
            p.wait_val("vector", sp, wbf_free[slot])
            _wgt_quant(p, c, wstage[:, sb0:sb0 + W], W, tl, wbf[:, sb0:sb0 + W])
            wst_free[slot] = (0, sv.n)
            p.wait("gpsimd", sv)
            p.wait_val("gpsimd", st, wbf2_free[slot])

            @p.on("gpsimd")
            def _(e, _W=W, _b=sb0):
                e.tensor_copy(wbf2[:, _b:_b + _W].bitcast(U16),
                              wbf[:, _b:_b + _W].bitcast(U16)).then_inc(sp.h, 1)
            sp.inc()
            wbf_free[slot] = sp.n
            p.wait("tensor", sp)
            for j in range(nT):
                p.wait("tensor", sa)
                if j == 0:
                    p.wait_val("scalar", stb, wTb_free)

                @p.on("tensor")
                def _(e, _j=j, _b=sb0):
                    e.matmul(psw1[:, 0:128],
                             wbf2[:, _b + _j * 128:_b + (_j + 1) * 128],
                             idb_t[:, :], is_transpose=True,
                             start=True, stop=True).then_inc(st.h, 1)
                st.inc()
                p.wait("scalar", st)

                @p.on("scalar")
                def _(e, _j=j):
                    e.activation(out=wTb[:, _j * 128:(_j + 1) * 128],
                                 in_=psw1[:, 0:128], func=AF.Copy).then_inc(sa.h, 1)
                sa.inc()
            wbf2_free[slot] = st.n
            p.wait("sync", sa)

            @p.on("sync")
            def _(e, _nm=nm, _r=r, _c0=c0, _W=W):
                width = c.FSH if _nm in ("gq", "uq") else c.DSH
                rows = c.D if _nm in ("gq", "uq") else c.I
                base = c.GB_OFF[_nm]
                piece = io["gin_b"][base:base + rows * width].rearrange(
                    "(dd pp w) -> pp dd w", pp=128, w=width)
                pp = piece[:, _c0 // 128:(_c0 + _W) // 128,
                           _r * 128:(_r + 1) * 128]
                e.dma_start(out=pp,
                            in_=wTb[:, 0:_W].rearrange("p (dd o) -> p dd o", o=128)
                            ).then_inc(stb.h, 16)
            stb.inc(16)
            wTb_free = stb.n

    p.wait("gpsimd", stb)

    @p.on("gpsimd")
    def _(e):
        e.collective_compute(
            "AllGather", OP.bypass, replica_groups=[list(range(c.NC))],
            ins=[io["gin_b"].ap().opt()], outs=[io["gout_b"].ap().opt()],
        ).then_inc(scc.h, 1)
    scc.inc(1)
    p.play(nc)
    _ES.close()


def _phase_a(nc, c, io, S, gfw, idf_t, bias_t, wmT, ones_f, kvacc, QM, nKB):
    sv, sa, st, sp, scc = (S[k] for k in ("sv", "sa", "st", "sp", "scc"))
    TB, nTT, nD, H, HP = c.TBLK, c.nTT, c.nD, c.H, c.HP
    _ES = contextlib.ExitStack()
    xst = _ES.enter_context(nc.sbuf_tensor("xst", [128, 2 * c.D], F32))
    xf = _ES.enter_context(nc.sbuf_tensor("xf", [128, nD * TB], F32))
    pf = _ES.enter_context(nc.sbuf_tensor("pf", [128, nD * TB], F32))
    wc = _ES.enter_context(nc.sbuf_tensor("wc", [128, 2 * nD * 128], F32))
    kmB = _ES.enter_context(nc.sbuf_tensor("kmB", [128, nTT * H * 64], F32))
    vt = _ES.enter_context(nc.sbuf_tensor("vt", [128, 2 * c.D], F32))
    etmp1 = _ES.enter_context(nc.sbuf_tensor("etmp1", [128, 1024], F32))
    etmp2 = etmp1
    psA = _ES.enter_context(nc.psum_tensor("psA", [128, 512], F32))
    psB = _ES.enter_context(nc.psum_tensor("psB", [128, 512], F32))
    psTr = _ES.enter_context(nc.psum_tensor("psTr", [128, 512], F32))
    psKS = _ES.enter_context(nc.psum_tensor("psKS", [128, 512], F32))
    psKV = [_ES.enter_context(nc.psum_tensor("psKV%d" % i, [128, 512], F32))
            for i in range(nKB)]

    p = Prog(S)
    barrier(p, [S["sco"], sv, sa, st, sp], ENGS)

    ast = dict(vt_free=[0, 0], x_t={}, xst_free={0: 0, 1: 0})
    for tb in range(c.nB):
        _attn_block(p, c, io, S, tb, gfw, idf_t, bias_t, wmT, ones_f, kvacc,
                    QM, ast, xst, xf, pf, wc, kmB, vt, etmp1, etmp2,
                    psA, psB, psTr, psKS, psKV, nKB)

    barrier(p, [S["w0"], S["w1"], S["x0"], S["x1"], sv, sa, st, sp], ENGS)
    p.play(nc)
    _ES.close()


def _a1_loads(p, c, io, S, tb, ast, xst, tts):
    st = S["st"]
    t0 = tb * c.TBLK
    for tt in tts:
        slot = tt % 2
        sx = S["x0"] if slot == 0 else S["x1"]
        p.wait_val("sync", st, ast["xst_free"][slot])

        @p.on("sync")
        def _(e, _tt=tt, _sx=sx, _slot=slot, _t0=t0):
            e.dma_start(out=xst[:, _slot * c.D:(_slot + 1) * c.D],
                        in_=io["x_own"][_t0 + _tt * 128: _t0 + (_tt + 1) * 128, :]
                        ).then_inc(_sx.h, 16)
        sx.inc(16)
        ast["x_t"][(tb, tt)] = sx.n


def _attn_block(p, c, io, S, tb, gfw, idf_t, bias_t, wmT, ones_f, kvacc, QM,
                ast, xst, xf, pf, wc, kmB, vt, etmp1, etmp2,
                psA, psB, psTr, psKS, psKV, nKB):
    sv, sa, st, sp, scc = (S[k] for k in ("sv", "sa", "st", "sp", "scc"))
    TB, nTT, nD, H, HP = c.TBLK, c.nTT, c.nD, c.H, c.HP
    t0 = tb * TB

    a1_free = {0: sa.n, 1: sa.n}
    for tt in range(nTT):
        if (tb, tt) not in ast["x_t"]:
            _a1_loads(p, c, io, S, tb, ast, xst, [tt])
        slot = tt % 2
        sx = S["x0"] if slot == 0 else S["x1"]
        p.wait_val("tensor", sx, ast["x_t"].pop((tb, tt)))
        p.wait("tensor", sv)
        for dd in range(nD):
            tb2 = dd % 2
            trb = psTr if tb2 == 0 else psKS
            p.wait_val("tensor", sa, a1_free[tb2])

            @p.on("tensor")
            def _(e, _dd=dd, _trb=trb, _slot=slot):
                e.matmul(_trb[:, 0:128],
                         xst[:, _slot * c.D + _dd * 128:
                             _slot * c.D + (_dd + 1) * 128],
                         idf_t[:, :], is_transpose=True,
                         start=True, stop=True).then_inc(st.h, 1)
            st.inc()
            p.wait("scalar", st)

            @p.on("scalar")
            def _(e, _tt=tt, _dd=dd, _trb=trb):
                col = _dd * TB + _tt * 128
                e.activation(out=xf[:, col:col + 128], in_=_trb[:, 0:128],
                             func=AF.Copy).then_inc(sa.h, 1)
            sa.inc()
            a1_free[tb2] = sa.n
        ast["xst_free"][slot] = st.n

    def fm_proj(w_nm, bidx, dest):
        bank_free = {0: (sa.n, sv.n), 1: (sa.n, sv.n)}
        wbuf_free = {0: st.n, 1: st.n}
        for oc in range(nD):
            wb = oc % 2
            sw = S["w0"] if wb == 0 else S["w1"]
            qeng = "sync" if (wb == 0 or not USE_2Q_A) else "scalar"
            p.wait_val(qeng, st, wbuf_free[wb])
            if tb == 0 and oc <= 1 and w_nm == "qT":
                p.wait_val(qeng, scc, 1)

            @p.on(qeng)
            def _(e, _nm=w_nm, _oc=oc, _wb=wb, _sw=sw):
                e.dma_start(out=wc[:, _wb * nD * 128:(_wb + 1) * nD * 128]
                            .rearrange("p (dd o) -> p dd o", o=128),
                            in_=gfw(_nm, _oc)).then_inc(_sw.h, 16)
            sw.inc(16)
            ps = psA if wb == 0 else psB
            p.wait("tensor", sw)
            sa_t, sv_t = bank_free[wb]
            p.wait_val("tensor", sa, sa_t)
            p.wait_val("tensor", sv, sv_t)

            @p.on("tensor")
            def _(e, _wb=wb, _ps=ps):
                for dd in range(nD):
                    w = wc[:, _wb * nD * 128 + dd * 128: _wb * nD * 128 + (dd + 1) * 128]
                    _li = e.matmul(_ps[:, 0:TB], w, xf[:, dd * TB:(dd + 1) * TB],
                                   start=(dd == 0), stop=(dd == nD - 1))
                _li.then_inc(st.h, 1)
            st.inc()
            wbuf_free[wb] = st.n
            p.wait("scalar", st)

            @p.on("scalar")
            def _(e, _oc=oc, _ps=ps, _bi=bidx):
                e.activation(out=dest[:, _oc * TB:(_oc + 1) * TB],
                             in_=_ps[:, 0:TB], func=AF.Identity,
                             bias=bias_t[:, _bi * c.nD + _oc: _bi * c.nD + _oc + 1]
                             ).then_inc(sa.h, 1)
            sa.inc()
            bank_free[wb] = (sa.n, 0)

    fm_proj("qT", 0, pf)
    a3_free = {0: (sv.n, sa.n), 1: (sv.n, sa.n)}
    for h in range(H):
        hp, par = h // 2, h % 2
        ps = psA if par == 0 else psB
        rows = slice(0, 64) if par == 0 else slice(64, 128)
        tpos = None if par == 0 else (0, 64)
        sv_t, sa_t = a3_free[par]
        p.wait_val("tensor", sv, sv_t)
        p.wait_val("tensor", sa, sa_t)

        @p.on("tensor")
        def _(e, _h=h, _ps=ps, _rows=rows, _tpos=tpos):
            q = pf[:, _h * TB:(_h + 1) * TB]
            e.matmul(_ps[_rows, 0:TB], wmT[:, 0:64], q, start=True, stop=True,
                     tile_position=_tpos).then_inc(st.h, 1)
        st.inc()
        p.wait("vector", st)

        eb = par * TB

        @p.on("vector")
        def _(e, _ps=ps, _rows=rows, _eb=eb):
            e.tensor_scalar(out=etmp1[_rows, _eb:_eb + TB], in0=_ps[_rows, 0:TB],
                            scalar1=0.0, scalar2=None, op0=OP.min)
        p.wait("scalar", sv)

        @p.on("scalar")
        def _(e, _rows=rows, _eb=eb):
            e.activation(out=etmp2[_rows, _eb:_eb + TB], in_=etmp1[_rows, _eb:_eb + TB],
                         func=AF.Exp).then_inc(sa.h, 1)
        sa.inc()
        p.wait("vector", sa)

        @p.on("vector")
        def _(e, _ps=ps, _rows=rows, _hp=hp, _eb=eb):
            e.scalar_tensor_tensor(out=QM[_rows, _hp * c.T + t0: _hp * c.T + t0 + TB],
                                   in0=_ps[_rows, 0:TB],
                                   scalar=0.0, in1=etmp2[_rows, _eb:_eb + TB],
                                   op0=OP.max, op1=OP.add)
        a3_free[par] = (sv.n, 0)

    fm_proj("kT", 1, pf)
    nhalf = max(H // 8, 1)
    nh = min(H, 8)
    a5_free = {0: (sv.n, sa.n), 1: (sv.n, sa.n)}
    for tt in range(nTT):
        for half in range(nhalf):
            par5 = (tt * nhalf + half) % 2
            ps = psA if par5 == 0 else psB
            eb = par5 * 512
            sv_t, sa_t = a5_free[par5]
            p.wait_val("tensor", sv, sv_t)
            p.wait_val("tensor", sa, sa_t)

            @p.on("tensor")
            def _(e, _tt=tt, _half=half, _ps=ps):
                for hh in range(nh):
                    h = _half * 8 + hh
                    k_ = pf[:, h * TB + _tt * 128: h * TB + (_tt + 1) * 128]
                    _li = e.matmul(_ps[:, hh * 64:(hh + 1) * 64], k_,
                                   wmT[:, 64:128], start=(hh == 0),
                                   stop=(hh == nh - 1), skip_group_check=True)
                _li.then_inc(st.h, 1)
            st.inc()
            W = nh * 64
            p.wait("vector", st)

            @p.on("vector")
            def _(e, _ps=ps, _eb=eb):
                e.tensor_scalar(out=etmp1[:, _eb:_eb + W], in0=_ps[:, 0:W],
                                scalar1=0.0, scalar2=None, op0=OP.min)
            p.wait("scalar", sv)

            @p.on("scalar")
            def _(e, _eb=eb):
                e.activation(out=etmp2[:, _eb:_eb + W], in_=etmp1[:, _eb:_eb + W],
                             func=AF.Exp).then_inc(sa.h, 1)
            sa.inc()
            p.wait("vector", sa)
            cb = tt * H * 64 + half * 8 * 64

            @p.on("vector")
            def _(e, _ps=ps, _cb=cb, _eb=eb):
                e.scalar_tensor_tensor(out=kmB[:, _cb:_cb + W], in0=_ps[:, 0:W],
                                       scalar=0.0, in1=etmp2[:, _eb:_eb + W],
                                       op0=OP.max, op1=OP.add)
            a5_free[par5] = (sv.n, 0)

    fm_proj("vT", 2, pf)

    if tb + 1 < c.nB:
        _a1_loads(p, c, io, S, tb + 1, ast, xst, list(range(min(2, c.nTT))))

    for tt in range(nTT):
        vs = tt % 2
        first_evac = True
        for dd in range(nD):
            p.wait("tensor", sa)

            @p.on("tensor")
            def _(e, _tt=tt, _dd=dd):
                e.matmul(psTr[:, 0:128],
                         pf[:, _dd * TB + _tt * 128: _dd * TB + (_tt + 1) * 128],
                         idf_t[:, :], is_transpose=True,
                         start=True, stop=True).then_inc(st.h, 1)
            st.inc()
            if first_evac:
                first_evac = False
                p.wait_val("scalar", st, ast["vt_free"][vs])
            p.wait("scalar", st)

            @p.on("scalar")
            def _(e, _dd=dd, _vs=vs):
                e.activation(out=vt[:, _vs * c.D + _dd * 128: _vs * c.D + (_dd + 1) * 128],
                             in_=psTr[:, 0:128], func=AF.Copy).then_inc(sa.h, 1)
            sa.inc()
        vt_ready = sa.n

        p.wait_val("tensor", sa, vt_ready)
        p.wait("tensor", sv)

        @p.on("tensor")
        def _(e, _tt=tt, _tb=tb, _vs=vs):
            for hp in range(HP):
                bank = psKV[hp // 2]
                cb0 = (hp % 2) * 256
                km = kmB[:, _tt * H * 64 + hp * 128: _tt * H * 64 + hp * 128 + 128]
                vA = vt[:, _vs * c.D + (2 * hp) * 128: _vs * c.D + (2 * hp + 1) * 128]
                vB = vt[:, _vs * c.D + (2 * hp + 1) * 128: _vs * c.D + (2 * hp + 2) * 128]
                first = _tt == 0 and hp % 2 == 0
                _li = e.matmul(bank[:, cb0:cb0 + 128], km, vA, start=first,
                         stop=False, skip_group_check=True)
                _li = e.matmul(bank[:, cb0 + 128:cb0 + 256], km, vB, start=False,
                         stop=False, skip_group_check=True)
                _li = e.matmul(psKS[:, hp:hp + 1], km, ones_f[:, 0:1],
                         start=(_tt == 0 and hp == 0), stop=False,
                         skip_group_check=True)
            _li.then_inc(st.h, 1)
        st.inc()
        ast["vt_free"][vs] = st.n

    p.wait("vector", st)

    @p.on("vector")
    def _(e):
        for b in range(nKB):
            cc = b * 512
            w = min(512, (HP - 2 * b) * 256)
            e.tensor_tensor(out=kvacc[:, cc:cc + w], in0=psKV[b][:, 0:w],
                            in1=kvacc[:, cc:cc + w], op=OP.add)
        cc = nKB * 512
        e.tensor_tensor(out=kvacc[:, cc:cc + HP], in0=psKS[:, 0:HP],
                        in1=kvacc[:, cc:cc + HP], op=OP.add)


def _phase_b(nc, c, io, S, gfw, idf_t, idb_t, bias_t, kvacc, QM, nKB):
    sv, sa, st, sp = (S[k] for k in ("sv", "sa", "st", "sp"))
    sg = S["sg"]
    TB, nTT, nD, H, HP = c.TBLK, c.nTT, c.nD, c.H, c.HP
    _ES = contextlib.ExitStack()
    kv2 = _ES.enter_context(nc.sbuf_tensor("kv2", [128, H * 128], F32))
    ks2 = _ES.enter_context(nc.sbuf_tensor("ks2", [128, 2 * HP], F32))
    rinvT = _ES.enter_context(nc.sbuf_tensor("rinvT", [128, c.nTT * H], F32))
    oat = _ES.enter_context(nc.sbuf_tensor("oat", [128, H * TB], F32))
    oa_f = _ES.enter_context(nc.sbuf_tensor("oa_f", [128, 1024], F32))
    resid = _ES.enter_context(nc.sbuf_tensor("resid", [128, nTT * c.D], F32))
    wc2 = _ES.enter_context(nc.sbuf_tensor("wc2", [128, 2 * nD * 128], F32))
    yf = _ES.enter_context(nc.sbuf_tensor("yf", [128, 2 * TB], F32))
    xres = _ES.enter_context(nc.sbuf_tensor("xres", [128, 256], F32))
    x1w = _ES.enter_context(nc.sbuf_tensor("x1w", [128, c.D], F32))
    lnm = _ES.enter_context(nc.sbuf_tensor("lnm", [128, 8], F32))
    x1q = _ES.enter_context(nc.sbuf_tensor("x1q", [128, c.D], BF16))
    x1q2 = _ES.enter_context(nc.sbuf_tensor("x1q2", [128, c.D], BF16))
    b_amax = _ES.enter_context(nc.sbuf_tensor("b_amax", [128, c.D // 32], I32))
    b_sc8 = _ES.enter_context(nc.sbuf_tensor("b_sc8", [128, c.D // 32], I32))
    b_inv = _ES.enter_context(nc.sbuf_tensor("b_inv", [128, c.D // 32], I32))
    b_m64 = _ES.enter_context(nc.sbuf_tensor("b_m64", [128, c.D // 32], I32))
    b_z = _ES.enter_context(nc.sbuf_tensor("b_z", [128, c.D], F32))
    b_q8 = _ES.enter_context(nc.sbuf_tensor("b_q8", [128, c.D], FP8))
    psA2 = _ES.enter_context(nc.psum_tensor("psA2", [128, 512], F32))
    psB2 = _ES.enter_context(nc.psum_tensor("psB2", [128, 512], F32))
    psN = _ES.enter_context(nc.psum_tensor("psN", [128, 512], F32))
    psT2 = _ES.enter_context(nc.psum_tensor("psT2", [128, 512], F32))
    psT2b = _ES.enter_context(nc.psum_tensor("psT2b", [128, 1024], BF16))
    tlB = dict(amax=b_amax, sc8=b_sc8, inv=b_inv, m64=b_m64, z=b_z, q8=b_q8)

    p = Prog(S)
    barrier(p, [sg], ENGS)

    @p.on("vector")
    def _(e):
        e.memset(kv2[:, :], 0.0)
        e.memset(ks2[:, :], 0.0)
        ks = nKB * 512
        for hp in range(HP):
            b, j = hp // 2, hp % 2
            cc = b * 512 + j * 256
            e.tensor_copy(kv2[0:64, (2 * hp) * 128:(2 * hp + 1) * 128],
                          kvacc[0:64, cc:cc + 128])
            e.tensor_copy(kv2[64:128, (2 * hp + 1) * 128:(2 * hp + 2) * 128],
                          kvacc[64:128, cc + 128:cc + 256])
            e.tensor_copy(ks2[0:64, 2 * hp:2 * hp + 1],
                          kvacc[0:64, ks + hp:ks + hp + 1])
            e.tensor_copy(ks2[64:128, 2 * hp + 1:2 * hp + 2],
                          kvacc[64:128, ks + hp:ks + hp + 1])

    bst = dict(oa_free={0: 0, 1: 0}, psN_free=S["sv"].n,
               psAB_free={0: [0, 0], 1: [0, 0]}, psT2_free=[0, 0],
               pend_ln1=[])
    for tb in range(c.nB):
        _attn_out_block(p, c, io, S, tb, gfw, idf_t, idb_t, bias_t, QM,
                        kv2, ks2, rinvT, oat, oa_f, resid, wc2, yf, xres,
                        x1w, lnm, x1q, x1q2, tlB, bst,
                        psA2, psB2, psN, psT2, psT2b)

    barrier(p, [S["w0"], S["w1"], S["r0"], S["r1"], S["ssp"], sg, sv, sa,
                st, sp], ENGS)
    p.play(nc)
    _ES.close()


def _attn_out_block(p, c, io, S, tb, gfw, idf_t, idb_t, bias_t, QM,
                    kv2, ks2, rinvT, oat, oa_f, resid, wc2, yf, xres,
                    x1w, lnm, x1q, x1q2, tlB, bst,
                    psA2, psB2, psN, psT2, psT2b):
    sv, sa, st, sp = (S[k] for k in ("sv", "sa", "st", "sp"))
    TB, nTT, nD, H, HP = c.TBLK, c.nTT, c.nD, c.H, c.HP
    t0 = tb * TB
    ssp = S["ssp"]

    pend_tr = None
    for tt in range(nTT):
        p.wait_val("tensor", sv, bst["psN_free"])

        @p.on("tensor")
        def _(e, _tt=tt):
            for hp in range(HP):
                cc = hp * c.T + t0 + _tt * 128
                _li = e.matmul(psN[:, 2 * hp:2 * hp + 2], QM[:, cc:cc + 128],
                               ks2[:, 2 * hp:2 * hp + 2], start=(hp == 0),
                               stop=(hp == HP - 1), skip_group_check=True)
            _li.then_inc(st.h, 1)
        st.inc()
        p.wait("vector", st)

        @p.on("vector")
        def _(e, _tt=tt):
            e.tensor_scalar(out=rinvT[:, _tt * H:(_tt + 1) * H], in0=psN[:, 0:H],
                            scalar1=EPS_ATT, scalar2=None, op0=OP.add)
            e.reciprocal(out=rinvT[:, _tt * H:(_tt + 1) * H],
                         in_=rinvT[:, _tt * H:(_tt + 1) * H])
        bst["psN_free"] = sv.n

        GW = min(4, H)
        ps_free = bst["psAB_free"]
        oa_free = bst["oa_free"]
        for grp in range(H // GW):
            par = grp % 2
            ob = par * 512
            ps = psA2 if par == 0 else psB2
            sv_t, sa_t = ps_free[par]
            p.wait_val("tensor", sv, sv_t)
            p.wait_val("tensor", sa, sa_t)

            @p.on("tensor")
            def _(e, _g=grp, _ps=ps, _tt=tt):
                for j in range(GW):
                    h = _g * GW + j
                    cc = (h // 2) * c.T + t0 + _tt * 128
                    _li = e.matmul(_ps[:, j * 128:(j + 1) * 128],
                                   QM[:, cc:cc + 128],
                                   kv2[:, h * 128:(h + 1) * 128],
                                   start=True, stop=True, skip_group_check=True)
                _li.then_inc(st.h, 1)
            st.inc()
            if pend_tr is not None:
                pend_tr()
                pend_tr = None
            mm_t = st.n
            p.wait_val("vector", st, mm_t)
            p.wait_val("vector", st, oa_free[par])

            @p.on("vector")
            def _(e, _ps=ps, _tt=tt, _g=grp, _ob=ob):
                e.tensor_tensor(
                    out=oa_f[:, _ob:_ob + GW * 128].rearrange(
                        "p (j q) -> p j q", q=128),
                    in0=_ps[:, 0:GW * 128].rearrange("p (j q) -> p j q", q=128),
                    in1=rinvT[:, _tt * H + _g * GW: _tt * H + (_g + 1) * GW]
                    .unsqueeze(2).broadcast_to([128, GW, 128]),
                    op=OP.mult)
            ps_free[par][0] = sv.n
            dv_t = sv.n

            def emit_tr(_g=grp, _tt=tt, _ob=ob, _par=par, _dt=dv_t):
                p.wait_val("tensor", sv, _dt)
                for j in range(GW):
                    p.wait_val("tensor", sa, bst["psT2_free"][0])
                    p.wait_val("tensor", sv, bst["psT2_free"][1])

                    @p.on("tensor")
                    def _(e, _j=j):
                        e.matmul(psT2[:, 0:128],
                                 oa_f[:, _ob + _j * 128:_ob + (_j + 1) * 128],
                                 idf_t[:, :], is_transpose=True,
                                 start=True, stop=True).then_inc(st.h, 1)
                    st.inc()
                    p.wait("scalar", st)

                    @p.on("scalar")
                    def _(e, _j=j):
                        col = (_g * GW + _j) * TB + _tt * 128
                        e.activation(out=oat[:, col:col + 128], in_=psT2[:, 0:128],
                                     func=AF.Copy).then_inc(sa.h, 1)
                    sa.inc()
                    bst["psT2_free"][0] = sa.n
                oa_free[_par] = st.n
            pend_tr = emit_tr
        if bst["pend_ln1"]:
            bst["pend_ln1"].pop(0)()
    if pend_tr is not None:
        pend_tr()

    bank_free = bst["psAB_free"]
    wbuf_free = {0: st.n, 1: st.n}
    yf_free = {0: st.n, 1: st.n}
    p.wait("tensor", sa)
    for oc in range(nD):
        wb = oc % 2
        sw = S["w0"] if wb == 0 else S["w1"]
        qeng = "sync" if (wb == 0 or not USE_2Q_B) else "scalar"
        p.wait_val(qeng, st, wbuf_free[wb])

        @p.on(qeng)
        def _(e, _oc=oc, _wb=wb, _sw=sw):
            e.dma_start(out=wc2[:, _wb * nD * 128:(_wb + 1) * nD * 128]
                        .rearrange("p (dd o) -> p dd o", o=128),
                        in_=gfw("oT", _oc)).then_inc(_sw.h, 16)
        sw.inc(16)
        ps = psA2 if wb == 0 else psB2
        p.wait("tensor", sw)
        sv_t, sa_t = bank_free[wb]
        p.wait_val("tensor", sa, sa_t)
        p.wait_val("tensor", sv, sv_t)

        @p.on("tensor")
        def _(e, _wb=wb, _ps=ps):
            for dd in range(nD):
                w = wc2[:, _wb * nD * 128 + dd * 128: _wb * nD * 128 + (dd + 1) * 128]
                _li = e.matmul(_ps[:, 0:TB], w, oat[:, dd * TB:(dd + 1) * TB],
                               start=(dd == 0), stop=(dd == nD - 1))
            _li.then_inc(st.h, 1)
        st.inc()
        wbuf_free[wb] = st.n
        p.wait("scalar", st)
        p.wait_val("scalar", st, yf_free[wb])

        @p.on("scalar")
        def _(e, _oc=oc, _ps=ps, _wb=wb):
            e.activation(out=yf[:, _wb * TB:_wb * TB + TB], in_=_ps[:, 0:TB],
                         func=AF.Identity,
                         bias=bias_t[:, 3 * c.nD + _oc: 3 * c.nD + _oc + 1]
                         ).then_inc(sa.h, 1)
        sa.inc()
        bank_free[wb][1] = sa.n
        yf_t = sa.n
        for tt in range(nTT):
            sr = S["r0"] if tt % 2 == 0 else S["r1"]
            p.wait("sync", sv)

            @p.on("sync")
            def _(e, _oc=oc, _tt=tt, _sr=sr):
                e.dma_start(out=xres[:, (_tt % 2) * 128:(_tt % 2) * 128 + 128],
                            in_=io["x_own"][t0 + _tt * 128: t0 + (_tt + 1) * 128,
                                            _oc * 128:(_oc + 1) * 128]
                            ).then_inc(_sr.h, 16)
            sr.inc(16)
            p.wait_val("tensor", sa, max(bst["psT2_free"][0], yf_t))
            p.wait_val("tensor", sv, bst["psT2_free"][1])

            @p.on("tensor")
            def _(e, _tt=tt, _wb=wb):
                e.matmul(psT2[:, 0:128],
                         yf[:, _wb * TB + _tt * 128: _wb * TB + (_tt + 1) * 128],
                         idf_t[:, :], is_transpose=True,
                         start=True, stop=True).then_inc(st.h, 1)
            st.inc()
            p.wait("vector", st)
            p.wait("vector", sr)

            @p.on("vector")
            def _(e, _oc=oc, _tt=tt):
                e.scalar_tensor_tensor(
                    out=resid[:, _tt * c.D + _oc * 128: _tt * c.D + (_oc + 1) * 128],
                    in0=psT2[:, 0:128], scalar=1.0,
                    in1=xres[:, (_tt % 2) * 128:(_tt % 2) * 128 + 128],
                    op0=OP.mult, op1=OP.add)
            bst["psT2_free"][1] = sv.n
        yf_free[wb] = st.n

    for tt in range(nTT):
        def ln1_tt(tt=tt):
            p.wait("vector", ssp)
            p.wait("vector", sp)

            @p.on("vector")
            def _(e, _tt=tt):
                r = resid[:, _tt * c.D:(_tt + 1) * c.D]
                e.tensor_reduce(out=lnm[:, 0:1], in_=r, axis=AX, op=OP.add)
                e.tensor_scalar(out=lnm[:, 0:1], in0=lnm[:, 0:1],
                                scalar1=1.0 / c.D, scalar2=None, op0=OP.mult)
                e.tensor_scalar(out=x1w[:, :], in0=r, scalar1=lnm[:, 0:1],
                                scalar2=None, op0=OP.subtract)
            p.wait("scalar", sv)

            @p.on("scalar")
            def _(e):
                e.activation(out=tlB["z"][:, 0:c.D], in_=x1w[:, :], func=AF.Square,
                             accum_out=lnm[:, 1:2]).then_inc(sa.h, 1)
            sa.inc()
            p.wait("vector", sa)

            @p.on("vector")
            def _(e):
                e.tensor_scalar(out=lnm[:, 1:2], in0=lnm[:, 1:2],
                                scalar1=1.0 / c.D, scalar2=EPS_LN,
                                op0=OP.mult, op1=OP.add)
            p.wait("scalar", sv)

            @p.on("scalar")
            def _(e):
                e.activation(out=lnm[:, 2:3], in_=lnm[:, 1:2],
                             func=AF.Sqrt).then_inc(sa.h, 1)
            sa.inc()
            p.wait("vector", sa)

            @p.on("vector")
            def _(e):
                e.reciprocal(out=lnm[:, 3:4], in_=lnm[:, 2:3])
                e.tensor_scalar(out=x1w[:, :], in0=x1w[:, :], scalar1=lnm[:, 3:4],
                                scalar2=None, op0=OP.mult)
            p.wait("sync", sv)

            @p.on("sync")
            def _(e, _tt=tt):
                e.dma_start(out=io["x1_sp"][t0 + _tt * 128: t0 + (_tt + 1) * 128, :],
                            in_=x1w[:, :]).then_inc(ssp.h, 16)
            ssp.inc(16)
            _act_quant(p, c, x1w[:, :], c.D, tlB, x1q)
            p.wait("gpsimd", sv)
            p.wait("gpsimd", ssp)

            @p.on("gpsimd")
            def _(e):
                e.tensor_copy(x1q2[:, :].bitcast(U16), x1q[:, :].bitcast(U16)
                              ).then_inc(sp.h, 1)
            sp.inc()
            p.wait("sync", sp)

            @p.on("sync")
            def _(e, _tt=tt):
                e.dma_start(out=io["x1q_sp"][t0 + _tt * 128: t0 + (_tt + 1) * 128, :],
                            in_=x1q2[:, :]).then_inc(ssp.h, 16)
            ssp.inc(16)
        bst["pend_ln1"].append(ln1_tt)
    if tb == c.nB - 1:
        while bst["pend_ln1"]:
            bst["pend_ln1"].pop(0)()


def _phase_f(nc, c, io, S, idb_t):
    _phase_f1(nc, c, io, S, idb_t)
    _phase_f2(nc, c, io, S)


def _phase_f1(nc, c, io, S, idb_t):
    sv, sa, st, sp, scc = (S[k] for k in ("sv", "sa", "st", "sp", "scc"))
    sg = S["sg"]
    SB, nST, nD, nI, nOS, OSW = c.SB, c.nST, c.nD, c.nI, c.nOS, c.OSW
    TH = max(c.T // 2, 128)
    nTC = TH // 128
    _ES = contextlib.ExitStack()
    wgs = _ES.enter_context(nc.sbuf_tensor("wgs", [128, 2 * nD * OSW], BF16))
    wus = _ES.enter_context(nc.sbuf_tensor("wus", [128, 2 * nD * OSW], BF16))
    x1qTh = _ES.enter_context(nc.sbuf_tensor("x1qTh", [128, nD * TH], BF16))
    intw = _ES.enter_context(nc.sbuf_tensor("intw", [128, 2 * OSW], F32))
    iq = _ES.enter_context(nc.sbuf_tensor("iq", [128, 2 * OSW], BF16))
    iq2 = _ES.enter_context(nc.sbuf_tensor("iq2", [128, 2 * OSW], BF16))
    f_amax = _ES.enter_context(nc.sbuf_tensor("f_amax", [128, OSW // 16], I32))
    f_sc8 = _ES.enter_context(nc.sbuf_tensor("f_sc8", [128, OSW // 16], I32))
    f_inv = _ES.enter_context(nc.sbuf_tensor("f_inv", [128, OSW // 16], I32))
    f_m64 = _ES.enter_context(nc.sbuf_tensor("f_m64", [128, OSW // 16], I32))
    f_z = _ES.enter_context(nc.sbuf_tensor("f_z", [128, 2 * OSW], F32))
    f_q8 = _ES.enter_context(nc.sbuf_tensor("f_q8", [128, 2 * OSW], FP8))
    iqs = _ES.enter_context(nc.sbuf_tensor("iqs", [128, 2 * OSW], BF16))
    psb = [_ES.enter_context(nc.psum_tensor("psf%d" % i, [128, 512], F32))
           for i in range(6)]
    psT = [_ES.enter_context(nc.psum_tensor("psq%d" % i, [128, 1024], BF16))
           for i in range(2)]

    p = Prog(S)
    barrier(p, [S["ssp"], S["w0"], S["w1"], S["r0"], S["r1"], sg, sv, sa,
                st, sp], ENGS)
    p.wait("sync", scc)
    p.wait("tensor", scc)

    tls = []
    for s in range(2):
        nb = OSW // 32
        tls.append(dict(
            amax=f_amax[:, s * nb:(s + 1) * nb],
            sc8=f_sc8[:, s * nb:(s + 1) * nb],
            inv=f_inv[:, s * nb:(s + 1) * nb],
            m64=f_m64[:, s * nb:(s + 1) * nb],
            z=f_z[:, s * OSW:(s + 1) * OSW],
            q8=f_q8[:, s * OSW:(s + 1) * OSW]))

    fst = dict(ps_free=[(0, 0), (0, 0), (0, 0)],
               intw_free=[0, 0], iq_free=[0, 0], iq2_free=[0, 0],
               xh_free=0, gu_free=[0, 0], k=0, pend=[], psT_free=[0, 0],
               stg_free=[0, 0],
               wd_sem=[S["wd0"], S["wd1"]], wd_buf_free=[0, 0], wd_k=0)
    sgu = S["gu0"]
    sgu2 = S["gu1"]
    sxq = S["sxq"]

    for th in range(c.T // TH):
        ht0 = th * TH
        p.wait_val("sync", st, fst["xh_free"])
        p.wait("sync", S["ssp"])

        @p.on("sync")
        def _(e, _ht0=ht0):
            for dd in range(nD):
                e.dma_start(out=x1qTh[:, dd * TH:(dd + 1) * TH],
                            in_=io["x1q_sp"][_ht0:_ht0 + TH,
                                             dd * 128:(dd + 1) * 128],
                            transpose=True).then_inc(sxq.h, 16)
        sxq.inc(16 * nD)
        xq_t = sxq.n

        for osx in range(nOS):
            gpar = osx % 2
            gsem = sgu if gpar == 0 else sgu2
            p.wait_val("sync", st, fst["gu_free"][gpar])

            @p.on("sync")
            def _(e, _osx=osx, _gpar=gpar, _gs=gsem):
                rk = (_osx * OSW) // c.FSH
                lo = (_osx * OSW) % c.FSH
                for nm, buf in (("gq", wgs), ("uq", wus)):
                    base = rk * c.GBN + c.GB_OFF[nm]
                    piece = io["gout_b"][base:base + c.D * c.FSH].rearrange(
                        "(dd pp w) -> pp dd w", pp=128, w=c.FSH)
                    e.dma_start(
                        out=buf[:, _gpar * nD * OSW:(_gpar + 1) * nD * OSW]
                        .rearrange("p (dd o) -> p dd o", o=OSW),
                        in_=piece[:, :, lo:lo + OSW]).then_inc(_gs.h, 16)
            gsem.inc(16 * 2)

            for tc in range(nTC):
                k = fst["k"]
                kset = k % 3
                sl = k % 2
                pg, pu = psb[2 * kset], psb[2 * kset + 1]
                while len(fst["pend"]) >= 2:
                    fst["pend"].pop(0)()
                p.wait_val("tensor", sxq, xq_t)
                p.wait("tensor", gsem)
                sa_t, sv_t = fst["ps_free"][kset]
                p.wait_val("tensor", sa, sa_t)
                p.wait_val("tensor", sv, sv_t)

                @p.on("tensor")
                def _(e, _tc=tc, _gpar=gpar, _pg=pg, _pu=pu):
                    for dd in range(nD):
                        xw = x1qTh[:, dd * TH + _tc * 128: dd * TH + (_tc + 1) * 128]
                        gw = wgs[:, _gpar * nD * OSW + dd * OSW:
                                 _gpar * nD * OSW + (dd + 1) * OSW]
                        uw = wus[:, _gpar * nD * OSW + dd * OSW:
                                 _gpar * nD * OSW + (dd + 1) * OSW]
                        _li = e.matmul(_pg[:, 0:OSW], xw, gw, start=(dd == 0),
                                       stop=(dd == nD - 1))
                        _li = e.matmul(_pu[:, 0:OSW], xw, uw, start=(dd == 0),
                                       stop=(dd == nD - 1))
                    _li.then_inc(st.h, 1)
                st.inc()
                mm_t = st.n
                fst["gu_free"][gpar] = st.n

                p.wait_val("scalar", st, mm_t)
                p.wait_val("scalar", sv, fst["intw_free"][sl])

                @p.on("scalar")
                def _(e, _sl=sl, _pg=pg):
                    e.activation(out=intw[:, _sl * OSW:(_sl + 1) * OSW],
                                 in_=_pg[:, 0:OSW],
                                 func=AF.Silu).then_inc(sa.h, 1)
                sa.inc()
                silu_t = sa.n
                p.wait_val("vector", sa, silu_t)

                @p.on("vector")
                def _(e, _sl=sl, _pu=pu):
                    e.tensor_tensor(out=intw[:, _sl * OSW:(_sl + 1) * OSW],
                                    in0=intw[:, _sl * OSW:(_sl + 1) * OSW],
                                    in1=_pu[:, 0:OSW], op=OP.mult)
                fst["ps_free"][kset] = (silu_t, sv.n)

                p.wait_val("vector", sp, fst["iq_free"][sl])
                _act_quant(p, c, intw[:, sl * OSW:(sl + 1) * OSW], OSW,
                           tls[sl], iq[:, sl * OSW:(sl + 1) * OSW])
                fst["intw_free"][sl] = sv.n
                quant_t = sv.n

                p.wait_val("gpsimd", sv, quant_t)
                p.wait_val("gpsimd", st, fst["iq2_free"][sl])

                @p.on("gpsimd")
                def _(e, _sl=sl):
                    e.tensor_copy(iq2[:, _sl * OSW:(_sl + 1) * OSW].bitcast(U16),
                                  iq[:, _sl * OSW:(_sl + 1) * OSW].bitcast(U16)
                                  ).then_inc(sp.h, 1)
                sp.inc()
                fst["iq_free"][sl] = sp.n
                laund_t = sp.n

                def emit_tr(_sl=sl, _osx=osx, _tcol=ht0 + tc * 128,
                            _pslot=fst["k"] % 2, _lt=laund_t):
                    _psq = psT[_pslot]
                    _ssi = S["si0"] if _pslot == 0 else S["si1"]
                    nj = OSW // 128
                    p.wait_val("tensor", sp, _lt)
                    p.wait_val("tensor", sa, fst["psT_free"][_pslot])

                    @p.on("tensor")
                    def _(e):
                        for j in range(nj):
                            _li = e.matmul(_psq[:, j * 128:(j + 1) * 128],
                                           iq2[:, _sl * OSW + j * 128:
                                               _sl * OSW + (j + 1) * 128],
                                           idb_t[:, :], is_transpose=True,
                                           start=True, stop=True)
                        _li.then_inc(st.h, 1)
                    st.inc()
                    fst["iq2_free"][_sl] = st.n
                    p.wait_val("scalar", st, st.n)
                    p.wait_val("scalar", _ssi, fst["stg_free"][_pslot])

                    @p.on("scalar")
                    def _(e):
                        e.activation(out=iqs[:, _pslot * OSW:(_pslot + 1) * OSW],
                                     in_=_psq[:, 0:OSW],
                                     func=AF.Copy).then_inc(sa.h, 1)
                    sa.inc()
                    fst["psT_free"][_pslot] = sa.n
                    p.wait_val("sync", sa, sa.n)

                    @p.on("sync")
                    def _(e):
                        e.dma_start(
                            out=io["iq_sp"][_osx * OSW:(_osx + 1) * OSW,
                                            _tcol:_tcol + 128]
                            .rearrange("(j p) t -> p j t", p=128),
                            in_=iqs[:, _pslot * OSW:(_pslot + 1) * OSW]
                            .rearrange("p (j t) -> p j t", t=128)
                            ).then_inc(_ssi.h, 16)
                    _ssi.inc(16)
                    fst["stg_free"][_pslot] = _ssi.n
                fst["pend"].append(emit_tr)
                fst["k"] += 1
        fst["xh_free"] = st.n
    while fst["pend"]:
        fst["pend"].pop(0)()

    p.play(nc)
    _ES.close()


def _phase_f2(nc, c, io, S):
    sv, sa, st, sp = (S[k] for k in ("sv", "sa", "st", "sp"))
    SB, nST, nD, nI = c.SB, c.nST, c.nD, c.nI
    _ES = contextlib.ExitStack()
    iqT = _ES.enter_context(nc.sbuf_tensor("iqT", [128, 2 * nI * SB], BF16))
    wds = _ES.enter_context(nc.sbuf_tensor("wds", [128, 2 * c.D], BF16))
    res2 = _ES.enter_context(nc.sbuf_tensor("res2", [128, nST * c.D], F32))
    x1res = _ES.enter_context(nc.sbuf_tensor("x1res", [128, 2 * c.D], F32))
    f_lnm = _ES.enter_context(nc.sbuf_tensor("f_lnm", [128, 8], F32))
    x2w = _ES.enter_context(nc.sbuf_tensor("x2w", [128, c.D], F32))
    psb = [_ES.enter_context(nc.psum_tensor("psg%d" % i, [128, 512], F32))
           for i in range(8)]
    fst = dict(wd_sem=[S["wd0"], S["wd1"]], wd_buf_free=[0, 0], wd_k=0)
    iq_sems = [(S["sxq"], S["skv"]), (S["sj0"], S["sj1"])]
    p = Prog(S)
    barrier(p, [S["ssp"], S["si0"], S["si1"], S["sxq"], S["skv"],
                S["sj0"], S["sj1"], S["gu0"], S["gu1"], sv, sa, st, sp], ENGS)
    dn_state = dict(iqs_free=[0, 0], iqT_t=[0, 0], dn_free=(sa.n, sv.n))

    def iqt_prefetch(t0p, qs):
        pair = iq_sems[qs]

        @p.on("sync")
        def _(e, _t0=t0p, _qs=qs, _ss=pair[0]):
            e.dma_start(out=iqT[:, _qs * nI * SB:(_qs + 1) * nI * SB]
                        .rearrange("p (ii t) -> p ii t", t=SB),
                        in_=io["iq_sp"].rearrange("(ii p) t -> p ii t", p=128)
                        [:, :, _t0:_t0 + SB]).then_inc(_ss.h, 16)
        pair[0].inc(16)
        dn_state["iqT_t"][qs] = (pair[0].n, pair[1].n)

    iqt_prefetch(0, 0)
    for sb in range(c.nSB):
        t0 = sb * SB
        qs = sb % 2
        if sb + 1 < c.nSB:
            nqs = (sb + 1) % 2
            p.wait_val("sync", st, dn_state["iqs_free"][nqs])
            iqt_prefetch(t0 + SB, nqs)
        iqT_t, iqT_t2 = dn_state["iqT_t"][qs]
        sxq, sxq2 = iq_sems[qs]

        SPW = min(512, c.D)
        nSpan = c.D // SPW
        banks = psb[:nST * nSpan]
        sa_t, sv_t = dn_state["dn_free"]
        p.wait_val("tensor", sa, sa_t)
        p.wait_val("tensor", sv, sv_t)
        p.wait_val("tensor", sxq, iqT_t)
        p.wait_val("tensor", sxq2, iqT_t2)
        for ii in range(nI):
            wpar = fst["wd_k"] % 2
            swd = fst["wd_sem"][wpar]
            qeng = "sync" if (wpar == 0 or not USE_2Q_F) else "gpsimd"
            p.wait_val(qeng, st, fst["wd_buf_free"][wpar])

            @p.on(qeng)
            def _(e, _ii=ii, _wpar=wpar, _swd=swd):
                base = c.GB_OFF["dq"] + _ii * 128 * c.DSH
                view = io["gout_b"].rearrange("(rk x) -> rk x", x=c.GBN)
                ap = view[:, base:base + 128 * c.DSH].rearrange(
                    "rk (pp w) -> pp rk w", pp=128)
                e.dma_start(out=wds[:, _wpar * c.D:(_wpar + 1) * c.D]
                            .rearrange("p (rk w) -> p rk w", w=c.DSH),
                            in_=ap).then_inc(_swd.h, 16)
            swd.inc(16)
            p.wait("tensor", swd)

            @p.on("tensor")
            def _(e, _ii=ii, _wpar=wpar, _qs=qs):
                for t1 in range(nST):
                    for spn in range(nSpan):
                        bank = banks[t1 * nSpan + spn]
                        w = wds[:, _wpar * c.D + spn * SPW:
                                _wpar * c.D + (spn + 1) * SPW]
                        _li = e.matmul(bank[:, 0:SPW],
                                       iqT[:, _qs * nI * SB + _ii * SB + t1 * 128:
                                           _qs * nI * SB + _ii * SB + (t1 + 1) * 128],
                                       w, start=(_ii == 0), stop=(_ii == nI - 1),
                                       skip_group_check=True)
                _li.then_inc(st.h, 1)
            st.inc()
            fst["wd_buf_free"][wpar] = st.n
            fst["wd_k"] += 1
        dn_t = st.n
        dn_state["iqs_free"][qs] = st.n

        for t1 in range(nST):
            sf = S["f0"] if t1 % 2 == 0 else S["f1"]
            p.wait("sync", sv)

            @p.on("sync")
            def _(e, _t1=t1, _sf=sf, _t0=t0):
                e.dma_start(out=x1res[:, (_t1 % 2) * c.D:(_t1 % 2) * c.D + c.D],
                            in_=io["x1_sp"][_t0 + _t1 * 128: _t0 + (_t1 + 1) * 128, :]
                            ).then_inc(_sf.h, 16)
            sf.inc(16)
            p.wait_val("vector", st, dn_t)
            p.wait("vector", sf)

            @p.on("vector")
            def _(e, _t1=t1):
                for spn in range(nSpan):
                    bank = banks[_t1 * nSpan + spn]
                    e.scalar_tensor_tensor(
                        out=res2[:, _t1 * c.D + spn * SPW: _t1 * c.D + (spn + 1) * SPW],
                        in0=bank[:, 0:SPW], scalar=1.0,
                        in1=x1res[:, (_t1 % 2) * c.D + spn * SPW:
                                  (_t1 % 2) * c.D + (spn + 1) * SPW],
                        op0=OP.mult, op1=OP.add)
        dn_state["dn_free"] = (0, sv.n)

        so = S["so"]
        for t1 in range(nST):
            p.wait("vector", so)

            @p.on("vector")
            def _(e, _t1=t1):
                r = res2[:, _t1 * c.D:(_t1 + 1) * c.D]
                e.tensor_reduce(out=f_lnm[:, 0:1], in_=r, axis=AX, op=OP.add)
                e.tensor_scalar(out=f_lnm[:, 0:1], in0=f_lnm[:, 0:1],
                                scalar1=1.0 / c.D, scalar2=None, op0=OP.mult)
                e.tensor_scalar(out=x2w[:, :], in0=r, scalar1=f_lnm[:, 0:1],
                                scalar2=None, op0=OP.subtract)
            p.wait("scalar", sv)

            @p.on("scalar")
            def _(e, _t1=t1):
                e.activation(out=x1res[:, (_t1 % 2) * c.D:(_t1 % 2) * c.D + c.D],
                             in_=x2w[:, :],
                             func=AF.Square, accum_out=f_lnm[:, 1:2]).then_inc(sa.h, 1)
            sa.inc()
            p.wait("vector", sa)

            @p.on("vector")
            def _(e):
                e.tensor_scalar(out=f_lnm[:, 1:2], in0=f_lnm[:, 1:2],
                                scalar1=1.0 / c.D, scalar2=EPS_LN,
                                op0=OP.mult, op1=OP.add)
            p.wait("scalar", sv)

            @p.on("scalar")
            def _(e):
                e.activation(out=f_lnm[:, 2:3], in_=f_lnm[:, 1:2],
                             func=AF.Sqrt).then_inc(sa.h, 1)
            sa.inc()
            p.wait("vector", sa)

            @p.on("vector")
            def _(e):
                e.reciprocal(out=f_lnm[:, 3:4], in_=f_lnm[:, 2:3])
                e.tensor_scalar(out=x2w[:, :], in0=x2w[:, :], scalar1=f_lnm[:, 3:4],
                                scalar2=None, op0=OP.mult)
            p.wait("sync", sv)

            @p.on("sync")
            def _(e, _t1=t1, _t0=t0):
                e.dma_start(out=io["out"][_t0 + _t1 * 128: _t0 + (_t1 + 1) * 128, :],
                            in_=x2w[:, :]).then_inc(so.h, 16)
            so.inc(16)

    barrier(p, [S["so"], S["f0"], S["f1"], S["wd0"], S["wd1"], S["gu0"],
                S["sxq"], S["skv"], S["sj0"], S["sj1"], S["ssp"],
                sv, sa, st, sp], ["sync"])
    p.play(nc)
    _ES.close()


_NC_CACHE = {}


def _get_nc(full):
    if full not in _NC_CACHE:
        _NC_CACHE[full] = build_nc(CFG(full))
    return _NC_CACHE[full]


def kernel(L0_attn_hidden_states, L0_attn_wq, L0_attn_bq, L0_attn_wk,
           L0_attn_bk, L0_attn_wv, L0_attn_bv, L0_attn_wo, L0_attn_bo,
           L0_attn_wmap_q, L0_attn_wmap_k, L0_ffn_w_gate, L0_ffn_w_up,
           L0_ffn_w_down):
    x = np.ascontiguousarray(np.asarray(L0_attn_hidden_states, dtype=np.float32))
    B, Sq, D = x.shape
    c = CFG(full=True)
    assert (B, Sq, D) == (4, 4096, 2048)
    wq = np.ascontiguousarray(np.asarray(L0_attn_wq, dtype=np.float32))
    wk = np.ascontiguousarray(np.asarray(L0_attn_wk, dtype=np.float32))
    wv = np.ascontiguousarray(np.asarray(L0_attn_wv, dtype=np.float32))
    wo = np.ascontiguousarray(np.asarray(L0_attn_wo, dtype=np.float32))
    wg = np.ascontiguousarray(np.asarray(L0_ffn_w_gate, dtype=np.float32))
    wu = np.ascontiguousarray(np.asarray(L0_ffn_w_up, dtype=np.float32))
    wd = np.ascontiguousarray(np.asarray(L0_ffn_w_down, dtype=np.float32))
    biases = np.stack([np.asarray(L0_attn_bq, dtype=np.float32),
                       np.asarray(L0_attn_bk, dtype=np.float32),
                       np.asarray(L0_attn_bv, dtype=np.float32),
                       np.asarray(L0_attn_bo, dtype=np.float32)], axis=0)
    wmaps = np.concatenate([np.asarray(L0_attn_wmap_q, dtype=np.float32),
                            np.asarray(L0_attn_wmap_k, dtype=np.float32)], axis=0)
    idf = np.eye(128, dtype=np.float32)
    idb = np.eye(128, dtype=np.float32).astype(ml_dtypes.bfloat16)

    nc = _get_nc(True)
    in_maps = []
    for core in range(c.NC):
        b, half = core // 2, core % 2
        in_maps.append({
            "x_own": np.ascontiguousarray(x[b, half * c.T:(half + 1) * c.T, :]),
            "wq_s": np.ascontiguousarray(wq[core * c.WSH:(core + 1) * c.WSH, :]),
            "wk_s": np.ascontiguousarray(wk[core * c.WSH:(core + 1) * c.WSH, :]),
            "wv_s": np.ascontiguousarray(wv[core * c.WSH:(core + 1) * c.WSH, :]),
            "wo_s": np.ascontiguousarray(wo[core * c.WSH:(core + 1) * c.WSH, :]),
            "wg_s": np.ascontiguousarray(wg[core * c.FSH:(core + 1) * c.FSH, :]),
            "wu_s": np.ascontiguousarray(wu[core * c.FSH:(core + 1) * c.FSH, :]),
            "wd_s": np.ascontiguousarray(wd[core * c.DSH:(core + 1) * c.DSH, :]),
            "biases": biases, "wmaps": wmaps, "idf": idf, "idb": idb,
        })
    res = run_bass_kernel_spmd(nc, in_maps, core_ids=list(range(c.NC)))
    out = np.empty((B, Sq, D), dtype=np.float32)
    for core in range(c.NC):
        b, half = core // 2, core % 2
        out[b, half * c.T:(half + 1) * c.T, :] = res.results[core]["out"]
    return out


# version 27
# speedup vs baseline: 1.0524x; 1.0524x over previous
import contextlib
import numpy as np
import ml_dtypes
import concourse.bass as bass
import concourse.mybir as mybir
from concourse.bass_utils import run_bass_kernel_spmd

F32 = mybir.dt.float32
BF16 = mybir.dt.bfloat16
I32 = mybir.dt.int32
U16 = mybir.dt.uint16
FP8 = mybir.dt.float8e4
OP = mybir.AluOpType
AF = mybir.ActivationFunctionType
AX = mybir.AxisListType.X

MAGIC_E2M1 = float(np.float32(1.5 * 2 ** 22))
ACT_CLIP = float(np.float32(15.0 / 128.0))
EPS_ATT = 1e-6
EPS_LN = 1e-5
ENGS = ("sync", "vector", "scalar", "tensor", "gpsimd")
USE_2Q_A = False
USE_2Q_B = False
USE_2Q_F = False
USE_2Q_FP = False


class CFG:
    def __init__(self, full=True):
        if full:
            self.NC, self.T, self.D, self.I, self.H = 8, 2048, 2048, 8192, 16
            self.TBLK, self.SB = 512, 256
        else:
            self.NC, self.T, self.D, self.I, self.H = 2, 512, 512, 1024, 4
            self.TBLK, self.SB = 256, 256
        self.DH, self.M, self.QB = 128, 64, 32
        self.HP = self.H // 2
        self.nD = self.D // 128
        self.nB = self.T // self.TBLK
        self.nTT = self.TBLK // 128
        self.nSB = self.T // self.SB
        self.nST = self.SB // 128
        self.nI = self.I // 128
        self.OSW = min(512, self.I, self.I // self.NC)
        self.nOS = self.I // self.OSW
        self.WSH = self.D // self.NC
        self.FSH = self.I // self.NC
        self.DSH = self.D // self.NC
        assert self.H * self.DH == self.D
        self.GF_OFF, off = {}, 0
        for nm in ("qT", "kT", "vT", "oT"):
            self.GF_OFF[nm] = off
            off += self.D * self.WSH
        self.GFN = off
        self.GB_OFF, off = {}, 0
        for nm, sz in (("gq", self.D * self.FSH), ("uq", self.D * self.FSH),
                       ("dq", self.I * self.DSH)):
            self.GB_OFF[nm] = off
            off += sz
        self.GBN = off


class Sem:
    def __init__(self, h):
        self.h, self.n = h, 0

    def inc(self, k=1):
        self.n += k
        return self.n


class _CountInst:
    def then_inc(self, *a, **k):
        return self


class _CountMock:

    def __init__(self):
        self.count = 0

    def wait_ge(self, *a, **k):
        return None

    def __getattr__(self, name):
        def f(*a, **k):
            self.count += 1
            return _CountInst()
        return f


class _VecProxy:

    _PASS = ("wait_ge",)

    def __init__(self, e, semh, n0):
        object.__setattr__(self, "_e", e)
        object.__setattr__(self, "_semh", semh)
        object.__setattr__(self, "_n", n0)

    def __getattr__(self, name):
        a = getattr(self._e, name)
        if name in self._PASS or not callable(a):
            return a

        def f(*args, **kw):
            if self._n > 0:
                self._e.wait_ge(self._semh, self._n)
            inst = a(*args, **kw)
            inst.then_inc(self._semh, 1)
            object.__setattr__(self, "_n", self._n + 1)
            return inst
        return f


class Prog:

    def __init__(self, S):
        self.q = {e: [] for e in ENGS}
        self.sv = S["sv"]
        self.v0 = S["sv"].n

    def on(self, eng):
        def deco(fn):
            self.q[eng].append(fn)
            if eng == "vector":
                m = _CountMock()
                fn(m)
                self.sv.inc(m.count)
        return deco

    def wait(self, eng, sem):
        v = sem.n
        if v > 0:
            self.q[eng].append(lambda e, _s=sem.h, _v=v: e.wait_ge(_s, _v))

    def wait_val(self, eng, sem, v):
        if v > 0:
            self.q[eng].append(lambda e, _s=sem.h, _v=v: e.wait_ge(_s, _v))

    def play(self, nc):
        with nc.Block() as blk:
            for eng in ENGS:
                fns = self.q[eng]
                if not fns:
                    continue
                deco = getattr(blk, eng)
                if eng == "vector":
                    def body(e, _fns=fns, _n0=self.v0):
                        px = _VecProxy(e, self.sv.h, _n0)
                        for f in _fns:
                            f(px)
                else:
                    def body(e, _fns=fns):
                        for f in _fns:
                            f(e)
                deco(body)


def barrier(p, sems, engs=ENGS):
    for eng in engs:
        for s in sems:
            p.wait(eng, s)


SEM_NAMES = ["sg", "sv", "sa", "st", "sp", "scc",
             "sco", "sl", "sl1", "stf", "stb",
             "w0", "w1", "x0", "x1", "r0", "r1", "f0", "f1",
             "gu0", "gu1", "wd0", "wd1", "sxq", "skv", "ssp", "so",
             "si0", "si1", "sj0", "sj1"]


def build_nc(c):
    nc = bass.Bass("TRN2", target_bir_lowering=False)
    io = {}
    io["x_own"] = nc.dram_tensor("x_own", [c.T, c.D], F32, kind="ExternalInput").ap()
    for nm in ("wq_s", "wk_s", "wv_s", "wo_s"):
        io[nm] = nc.dram_tensor(nm, [c.WSH, c.D], F32, kind="ExternalInput").ap()
    io["wg_s"] = nc.dram_tensor("wg_s", [c.FSH, c.D], F32, kind="ExternalInput").ap()
    io["wu_s"] = nc.dram_tensor("wu_s", [c.FSH, c.D], F32, kind="ExternalInput").ap()
    io["wd_s"] = nc.dram_tensor("wd_s", [c.DSH, c.I], F32, kind="ExternalInput").ap()
    io["biases"] = nc.dram_tensor("biases", [4, c.D], F32, kind="ExternalInput").ap()
    io["wmaps"] = nc.dram_tensor("wmaps", [2 * c.M, c.DH], F32, kind="ExternalInput").ap()
    io["idf"] = nc.dram_tensor("idf", [128, 128], F32, kind="ExternalInput").ap()
    io["idb"] = nc.dram_tensor("idb", [128, 128], BF16, kind="ExternalInput").ap()
    io["out"] = nc.dram_tensor("out", [c.T, c.D], F32, kind="ExternalOutput").ap()

    io["gin_f"] = nc.dram_tensor("gin_f", [c.GFN], F32)
    _asp = dict(addr_space="Shared") if c.NC > 4 else {}
    io["gout_f"] = nc.dram_tensor("gout_f", [c.NC * c.GFN], F32, **_asp)
    io["gin_b"] = nc.dram_tensor("gin_b", [c.GBN], BF16)
    io["gout_b"] = nc.dram_tensor("gout_b", [c.NC * c.GBN], BF16, **_asp)
    nKB = (c.HP + 1) // 2
    KVW = 512 * nKB + 16
    io["kv_bin"] = nc.dram_tensor("kv_bin", [128, KVW], F32)
    io["kv_bout"] = nc.dram_tensor("kv_bout", [128, KVW], F32)
    io["x1_sp"] = nc.dram_tensor("x1_sp", [c.T, c.D], F32)
    io["x1q_sp"] = nc.dram_tensor("x1q_sp", [c.T, c.D], BF16)
    io["iq_sp"] = nc.dram_tensor("iq_sp", [c.I, c.T], BF16)

    with contextlib.ExitStack() as es:
        S = {nm: Sem(es.enter_context(nc.semaphore(nm))) for nm in SEM_NAMES}
        _build(nc, c, io, S)
    return nc


def _build(nc, c, io, S):
    sg, sv, sa, st, sp, scc = (S[k] for k in ("sg", "sv", "sa", "st", "sp", "scc"))
    nKB = (c.HP + 1) // 2
    KVW = 512 * nKB + 16

    pers = [
        nc.sbuf_tensor("idf_t", [128, 128], F32),
        nc.sbuf_tensor("idb_t", [128, 128], BF16),
        nc.sbuf_tensor("bias_t", [128, 4 * c.nD], F32),
        nc.sbuf_tensor("wmT", [128, 128], F32),
        nc.sbuf_tensor("ones_f", [128, 8], F32),
        nc.sbuf_tensor("kvacc", [128, KVW], F32),
    ]
    hp_ = [t.__enter__() for t in pers]
    idf_t, idb_t, bias_t, wmT, ones_f, kvacc = hp_

    def gfw(nm, oc):
        rk, r = oc // (c.WSH // 128), oc % (c.WSH // 128)
        base = rk * c.GFN + c.GF_OFF[nm] + r * c.D * 128
        return io["gout_f"][base:base + c.D * 128].rearrange(
            "(p dd o) -> p dd o", p=128, o=128)

    _phase_w(nc, c, io, S, idf_t, idb_t, bias_t, wmT, ones_f, kvacc)

    with (
        nc.sbuf_tensor("QM", [128, c.HP * c.T], F32) as QM,
    ):
        _phase_a(nc, c, io, S, gfw, idf_t, bias_t, wmT, ones_f, kvacc, QM, nKB)

        p = Prog(S)
        skv = S["skv"]
        barrier(p, [sv], ["sync"])

        @p.on("sync")
        def _(e):
            e.dma_start(out=io["kv_bin"][:, :], in_=kvacc[:, :]).then_inc(skv.h, 16)
        skv.inc(16)
        p.wait("gpsimd", skv)

        @p.on("gpsimd")
        def _(e):
            e.collective_compute(
                "AllReduce", OP.add,
                replica_groups=[[2 * i, 2 * i + 1] for i in range(c.NC // 2)],
                ins=[io["kv_bin"].ap().opt()], outs=[io["kv_bout"].ap().opt()],
            ).then_inc(scc.h, 1)
        scc.inc(1)
        p.wait("gpsimd", scc)

        @p.on("gpsimd")
        def _(e):
            e.dma_start(out=kvacc[:, :], in_=io["kv_bout"][:, :]).then_inc(sg.h, 16)
        sg.inc(16)
        p.play(nc)

        _phase_b(nc, c, io, S, gfw, idf_t, idb_t, bias_t, kvacc, QM, nKB)

    _phase_f(nc, c, io, S, idb_t)

    for t in reversed(pers):
        t.__exit__(None, None, None)


def _act_quant(p, c, x_ap, W, tl, out_bf):
    nb = W // 32

    @p.on("vector")
    def _(v):
        x3 = x_ap.rearrange("p (b k) -> p b k", k=32)
        v.tensor_reduce(out=tl["amax"][:, 0:nb].bitcast(F32), in_=x3, axis=AX,
                        op=OP.max, apply_absolute_value=True)
        v.tensor_scalar(out=tl["sc8"][:, 0:nb], in0=tl["amax"][:, 0:nb],
                        scalar1=23, scalar2=0xFF,
                        op0=OP.logical_shift_right, op1=OP.bitwise_and)
        v.tensor_scalar(out=tl["inv"][:, 0:nb], in0=tl["sc8"][:, 0:nb],
                        scalar1=-1.0, scalar2=250.0, op0=OP.mult, op1=OP.add)
        v.tensor_scalar(out=tl["inv"][:, 0:nb], in0=tl["inv"][:, 0:nb],
                        scalar1=23, scalar2=None, op0=OP.logical_shift_left)
        v.tensor_scalar(out=tl["m64"][:, 0:nb], in0=tl["sc8"][:, 0:nb],
                        scalar1=4.0, scalar2=None, op0=OP.add)
        v.tensor_scalar(out=tl["m64"][:, 0:nb], in0=tl["m64"][:, 0:nb],
                        scalar1=23, scalar2=None, op0=OP.logical_shift_left)
        v.tensor_tensor(out=tl["z"][:, 0:W].rearrange("p (b k) -> p b k", k=32),
                        in0=x3,
                        in1=tl["inv"][:, 0:nb].bitcast(F32).unsqueeze(2)
                        .broadcast_to([128, nb, 32]), op=OP.mult)
        v.tensor_scalar(out=tl["z"][:, 0:W], in0=tl["z"][:, 0:W],
                        scalar1=ACT_CLIP, scalar2=-ACT_CLIP,
                        op0=OP.min, op1=OP.max)
        v.tensor_copy(tl["q8"][:, 0:W], tl["z"][:, 0:W])
        v.tensor_tensor(out=out_bf[:, 0:W].rearrange("p (b k) -> p b k", k=32),
                        in0=tl["q8"][:, 0:W].rearrange("p (b k) -> p b k", k=32),
                        in1=tl["m64"][:, 0:nb].bitcast(F32).unsqueeze(2)
                        .broadcast_to([128, nb, 32]), op=OP.mult)


def _wgt_quant(p, c, w_ap, W, tl, out_bf):
    nb = W // 32

    @p.on("vector")
    def _(v):
        w3 = w_ap.rearrange("p (b k) -> p b k", k=32)
        v.tensor_reduce(out=tl["amax"][:, 0:nb].bitcast(F32), in_=w3, axis=AX,
                        op=OP.max, apply_absolute_value=True)
        v.tensor_scalar(out=tl["sc8"][:, 0:nb], in0=tl["amax"][:, 0:nb],
                        scalar1=23, scalar2=0xFF,
                        op0=OP.logical_shift_right, op1=OP.bitwise_and)
        v.tensor_scalar(out=tl["scl"][:, 0:nb], in0=tl["sc8"][:, 0:nb],
                        scalar1=-2.0, scalar2=None, op0=OP.add)
        v.tensor_scalar(out=tl["scl"][:, 0:nb], in0=tl["scl"][:, 0:nb],
                        scalar1=23, scalar2=None, op0=OP.logical_shift_left)
        v.tensor_scalar(out=tl["clp"][:, 0:nb], in0=tl["sc8"][:, 0:nb],
                        scalar1=23, scalar2=0x400000,
                        op0=OP.logical_shift_left, op1=OP.bitwise_or)
        v.tensor_scalar(out=tl["nclp"][:, 0:nb], in0=tl["clp"][:, 0:nb].bitcast(F32),
                        scalar1=-1.0, scalar2=None, op0=OP.mult)
        v.tensor_scalar(out=tl["ti"][:, 0:W], in0=w_ap.bitcast(I32),
                        scalar1=0x7F800000, scalar2=None, op0=OP.bitwise_and)
        v.tensor_tensor(out=tl["p2"][:, 0:W].rearrange("p (b k) -> p b k", k=32),
                        in0=tl["ti"][:, 0:W].bitcast(F32)
                        .rearrange("p (b k) -> p b k", k=32),
                        in1=tl["scl"][:, 0:nb].bitcast(F32).unsqueeze(2)
                        .broadcast_to([128, nb, 32]), op=OP.max)
        v.tensor_scalar(out=tl["bt"][:, 0:W], in0=tl["p2"][:, 0:W],
                        scalar1=MAGIC_E2M1, scalar2=None, op0=OP.mult)
        v.tensor_tensor(out=tl["p2"][:, 0:W], in0=w_ap, in1=tl["bt"][:, 0:W],
                        op=OP.add)
        v.tensor_tensor(out=tl["p2"][:, 0:W], in0=tl["p2"][:, 0:W],
                        in1=tl["bt"][:, 0:W], op=OP.subtract)
        v.tensor_tensor(out=tl["p2"][:, 0:W].rearrange("p (b k) -> p b k", k=32),
                        in0=tl["p2"][:, 0:W].rearrange("p (b k) -> p b k", k=32),
                        in1=tl["clp"][:, 0:nb].bitcast(F32).unsqueeze(2)
                        .broadcast_to([128, nb, 32]), op=OP.min)
        v.tensor_tensor(out=out_bf[:, 0:W].rearrange("p (b k) -> p b k", k=32),
                        in0=tl["p2"][:, 0:W].rearrange("p (b k) -> p b k", k=32),
                        in1=tl["nclp"][:, 0:nb].unsqueeze(2)
                        .broadcast_to([128, nb, 32]), op=OP.max)


def _phase_w(nc, c, io, S, idf_t, idb_t, bias_t, wmT, ones_f, kvacc):
    sv, sa, st, sp, scc = (S[k] for k in ("sv", "sa", "st", "sp", "scc"))
    sco, sl, sl1 = S["sco"], S["sl"], S["sl1"]
    stf, stb = S["stf"], S["stb"]
    _ES = contextlib.ExitStack()
    WJ = min(4096, c.I)
    wstage = _ES.enter_context(nc.sbuf_tensor("wstage", [128, 2 * WJ], F32))
    wbf = _ES.enter_context(nc.sbuf_tensor("wbf", [128, 2 * WJ], BF16))
    wbf2 = _ES.enter_context(nc.sbuf_tensor("wbf2", [128, 2 * WJ], BF16))
    wTf = _ES.enter_context(nc.sbuf_tensor("wTf", [128, c.D], F32))
    wTb = _ES.enter_context(nc.sbuf_tensor("wTb", [128, WJ], BF16))
    q_amax = _ES.enter_context(nc.sbuf_tensor("q_amax", [128, WJ // 32], I32))
    q_sc8 = _ES.enter_context(nc.sbuf_tensor("q_sc8", [128, WJ // 32], I32))
    q_scl = _ES.enter_context(nc.sbuf_tensor("q_scl", [128, WJ // 32], I32))
    q_clp = _ES.enter_context(nc.sbuf_tensor("q_clp", [128, WJ // 32], I32))
    q_nclp = _ES.enter_context(nc.sbuf_tensor("q_nclp", [128, WJ // 32], F32))
    q_ti = _ES.enter_context(nc.sbuf_tensor("q_ti", [128, WJ], I32))
    q_p2 = _ES.enter_context(nc.sbuf_tensor("q_p2", [128, WJ], F32))
    q_bt = _ES.enter_context(nc.sbuf_tensor("q_bt", [128, WJ], F32))
    wm_st = _ES.enter_context(nc.sbuf_tensor("wm_st", [128, 128], F32))
    psw0 = _ES.enter_context(nc.psum_tensor("psw0", [128, 512], F32))
    psw1 = _ES.enter_context(nc.psum_tensor("psw1", [128, 1024], BF16))
    tl = dict(amax=q_amax, sc8=q_sc8, scl=q_scl, clp=q_clp, nclp=q_nclp,
              ti=q_ti, p2=q_p2, bt=q_bt)
    p = Prog(S)

    @p.on("sync")
    def _(e):
        e.dma_start(out=idf_t[:, :], in_=io["idf"][:, :]).then_inc(sco.h, 16)
        e.dma_start(out=idb_t[:, :], in_=io["idb"][:, :]).then_inc(sco.h, 16)
        for m in range(4):
            e.dma_start(out=bias_t[:, m * c.nD:(m + 1) * c.nD],
                        in_=io["biases"][m, :].rearrange("(oc p) -> p oc", p=128),
                        allow_slow_non_contiguous=True).then_inc(sco.h, 16)
        e.dma_start(out=wm_st[:, :], in_=io["wmaps"][:, :]).then_inc(sco.h, 16)
    sco.inc(16 * 7)
    p.wait("scalar", sco)

    @p.on("scalar")
    def _(e):
        e.activation(out=ones_f[:, :], in_=wm_st[:, 0:8], func=AF.Copy,
                     scale=0.0, bias=1.0).then_inc(sa.h, 1)
    sa.inc()
    p.wait("vector", sco)

    @p.on("vector")
    def _(e):
        e.memset(kvacc[:, :], 0.0)
    p.wait("tensor", sco)

    @p.on("tensor")
    def _(e):
        e.matmul(psw0[:, 0:128], wm_st[:, :], idf_t[:, :], is_transpose=True,
                 start=True, stop=True).then_inc(st.h, 1)
    st.inc()
    p.wait("scalar", st)

    @p.on("scalar")
    def _(e):
        e.activation(out=wmT[:, :], in_=psw0[:, 0:128],
                     func=AF.Copy).then_inc(sa.h, 1)
    sa.inc()

    p.wait("tensor", sa)

    jobs = []
    for nm, srcnm in (("qT", "wq_s"), ("kT", "wk_s"), ("vT", "wv_s"),
                      ("oT", "wo_s")):
        for r in range(c.WSH // 128):
            jobs.append(("attn", nm, srcnm, r, 0, c.D))
    for nm, srcnm in (("gq", "wg_s"), ("uq", "wu_s")):
        for r in range(c.FSH // 128):
            jobs.append(("ffn", nm, srcnm, r, 0, c.D))
    for r in range(c.DSH // 128):
        for c0 in range(0, c.I, WJ):
            jobs.append(("ffn", "dq", "wd_s", r, c0, WJ))

    wst_free = [(0, 0), (0, 0)]
    wbf_free = [0, 0]
    wbf2_free = [0, 0]
    wTf_free = 0
    wTb_free = 0
    attn_done = False
    for jidx, (kind, nm, srcnm, r, c0, W) in enumerate(jobs):
        nT = W // 128
        slot = jidx % 2
        sb0 = slot * WJ
        slsem = sl if slot == 0 else sl1
        st_t, sv_t = wst_free[slot]
        p.wait_val("sync", st, st_t)
        p.wait_val("sync", sv, sv_t)

        @p.on("sync")
        def _(e, _s=srcnm, _r=r, _c0=c0, _W=W, _b=sb0, _sem=slsem):
            e.dma_start(out=wstage[:, _b:_b + _W],
                        in_=io[_s][_r * 128:(_r + 1) * 128, _c0:_c0 + _W]
                        ).then_inc(_sem.h, 16)
        slsem.inc(16)
        p.wait("vector", slsem)
        p.wait("tensor", slsem)

        if kind == "attn":
            ev_free = sv.n
            GA = min(4, nT)
            for j in range(nT):
                if j % GA == 0:
                    p.wait_val("tensor", sv, ev_free)

                @p.on("tensor")
                def _(e, _j=j, _b=sb0, _GA=GA):
                    _li = e.matmul(psw0[:, (_j % _GA) * 128:(_j % _GA) * 128 + 128],
                                   wstage[:, _b + _j * 128:_b + (_j + 1) * 128],
                                   idf_t[:, :], is_transpose=True,
                                   start=True, stop=True, skip_group_check=True)
                    if _j % _GA == _GA - 1:
                        _li.then_inc(st.h, 1)
                if j % GA == GA - 1:
                    st.inc()
                    if j == GA - 1:
                        p.wait_val("vector", stf, wTf_free)
                    p.wait("vector", st)

                    @p.on("vector")
                    def _(e, _j=j, _GA=GA):
                        e.tensor_copy(wTf[:, (_j - _GA + 1) * 128:(_j + 1) * 128],
                                      psw0[:, 0:_GA * 128])
                    ev_free = sv.n
            wst_free[slot] = (st.n, 0)
            p.wait("sync", sv)

            @p.on("sync")
            def _(e, _nm=nm, _r=r):
                base = c.GF_OFF[_nm] + _r * c.D * 128
                e.dma_start(out=io["gin_f"][base:base + c.D * 128]
                            .rearrange("(p x) -> p x", p=128),
                            in_=wTf[:, :]).then_inc(stf.h, 16)
            stf.inc(16)
            wTf_free = stf.n
        else:
            if not attn_done:
                attn_done = True
                p.wait("gpsimd", stf)

                @p.on("gpsimd")
                def _(e):
                    e.collective_compute(
                        "AllGather", OP.bypass,
                        replica_groups=[list(range(c.NC))],
                        ins=[io["gin_f"].ap().opt()],
                        outs=[io["gout_f"].ap().opt()],
                    ).then_inc(scc.h, 1)
                scc.inc(1)
            p.wait_val("vector", sp, wbf_free[slot])
            _wgt_quant(p, c, wstage[:, sb0:sb0 + W], W, tl, wbf[:, sb0:sb0 + W])
            wst_free[slot] = (0, sv.n)
            p.wait("gpsimd", sv)
            p.wait_val("gpsimd", st, wbf2_free[slot])

            @p.on("gpsimd")
            def _(e, _W=W, _b=sb0):
                e.tensor_copy(wbf2[:, _b:_b + _W].bitcast(U16),
                              wbf[:, _b:_b + _W].bitcast(U16)).then_inc(sp.h, 1)
            sp.inc()
            wbf_free[slot] = sp.n
            p.wait("tensor", sp)
            for j in range(nT):
                p.wait("tensor", sa)
                if j == 0:
                    p.wait_val("scalar", stb, wTb_free)

                @p.on("tensor")
                def _(e, _j=j, _b=sb0):
                    e.matmul(psw1[:, 0:128],
                             wbf2[:, _b + _j * 128:_b + (_j + 1) * 128],
                             idb_t[:, :], is_transpose=True,
                             start=True, stop=True).then_inc(st.h, 1)
                st.inc()
                p.wait("scalar", st)

                @p.on("scalar")
                def _(e, _j=j):
                    e.activation(out=wTb[:, _j * 128:(_j + 1) * 128],
                                 in_=psw1[:, 0:128], func=AF.Copy).then_inc(sa.h, 1)
                sa.inc()
            wbf2_free[slot] = st.n
            p.wait("sync", sa)

            @p.on("sync")
            def _(e, _nm=nm, _r=r, _c0=c0, _W=W):
                width = c.FSH if _nm in ("gq", "uq") else c.DSH
                rows = c.D if _nm in ("gq", "uq") else c.I
                base = c.GB_OFF[_nm]
                piece = io["gin_b"][base:base + rows * width].rearrange(
                    "(dd pp w) -> pp dd w", pp=128, w=width)
                pp = piece[:, _c0 // 128:(_c0 + _W) // 128,
                           _r * 128:(_r + 1) * 128]
                e.dma_start(out=pp,
                            in_=wTb[:, 0:_W].rearrange("p (dd o) -> p dd o", o=128)
                            ).then_inc(stb.h, 16)
            stb.inc(16)
            wTb_free = stb.n

    p.wait("gpsimd", stb)

    @p.on("gpsimd")
    def _(e):
        e.collective_compute(
            "AllGather", OP.bypass, replica_groups=[list(range(c.NC))],
            ins=[io["gin_b"].ap().opt()], outs=[io["gout_b"].ap().opt()],
        ).then_inc(scc.h, 1)
    scc.inc(1)
    p.play(nc)
    _ES.close()


def _phase_a(nc, c, io, S, gfw, idf_t, bias_t, wmT, ones_f, kvacc, QM, nKB):
    sv, sa, st, sp, scc = (S[k] for k in ("sv", "sa", "st", "sp", "scc"))
    TB, nTT, nD, H, HP = c.TBLK, c.nTT, c.nD, c.H, c.HP
    _ES = contextlib.ExitStack()
    xst = _ES.enter_context(nc.sbuf_tensor("xst", [128, 2 * c.D], F32))
    xf = _ES.enter_context(nc.sbuf_tensor("xf", [128, nD * TB], F32))
    pf = _ES.enter_context(nc.sbuf_tensor("pf", [128, nD * TB], F32))
    wc = _ES.enter_context(nc.sbuf_tensor("wc", [128, 2 * nD * 128], F32))
    kmB = _ES.enter_context(nc.sbuf_tensor("kmB", [128, nTT * H * 64], F32))
    vt = _ES.enter_context(nc.sbuf_tensor("vt", [128, 2 * c.D], F32))
    etmp1 = _ES.enter_context(nc.sbuf_tensor("etmp1", [128, 1024], F32))
    etmp2 = etmp1
    psA = _ES.enter_context(nc.psum_tensor("psA", [128, 512], F32))
    psB = _ES.enter_context(nc.psum_tensor("psB", [128, 512], F32))
    psTr = _ES.enter_context(nc.psum_tensor("psTr", [128, 512], F32))
    psKS = _ES.enter_context(nc.psum_tensor("psKS", [128, 512], F32))
    psKV = [_ES.enter_context(nc.psum_tensor("psKV%d" % i, [128, 512], F32))
            for i in range(nKB)]

    p = Prog(S)
    barrier(p, [S["sco"], sv, sa, st, sp], ENGS)

    ast = dict(vt_free=[0, 0], x_t={}, xst_free={0: 0, 1: 0})
    for tb in range(c.nB):
        _attn_block(p, c, io, S, tb, gfw, idf_t, bias_t, wmT, ones_f, kvacc,
                    QM, ast, xst, xf, pf, wc, kmB, vt, etmp1, etmp2,
                    psA, psB, psTr, psKS, psKV, nKB)

    barrier(p, [S["w0"], S["w1"], S["x0"], S["x1"], sv, sa, st, sp], ENGS)
    p.play(nc)
    _ES.close()


def _a1_loads(p, c, io, S, tb, ast, xst, tts):
    st = S["st"]
    t0 = tb * c.TBLK
    for tt in tts:
        slot = tt % 2
        sx = S["x0"] if slot == 0 else S["x1"]
        p.wait_val("sync", st, ast["xst_free"][slot])

        @p.on("sync")
        def _(e, _tt=tt, _sx=sx, _slot=slot, _t0=t0):
            e.dma_start(out=xst[:, _slot * c.D:(_slot + 1) * c.D],
                        in_=io["x_own"][_t0 + _tt * 128: _t0 + (_tt + 1) * 128, :]
                        ).then_inc(_sx.h, 16)
        sx.inc(16)
        ast["x_t"][(tb, tt)] = sx.n


def _attn_block(p, c, io, S, tb, gfw, idf_t, bias_t, wmT, ones_f, kvacc, QM,
                ast, xst, xf, pf, wc, kmB, vt, etmp1, etmp2,
                psA, psB, psTr, psKS, psKV, nKB):
    sv, sa, st, sp, scc = (S[k] for k in ("sv", "sa", "st", "sp", "scc"))
    TB, nTT, nD, H, HP = c.TBLK, c.nTT, c.nD, c.H, c.HP
    t0 = tb * TB

    a1_free = {0: sa.n, 1: sa.n}
    for tt in range(nTT):
        if (tb, tt) not in ast["x_t"]:
            _a1_loads(p, c, io, S, tb, ast, xst, [tt])
        slot = tt % 2
        sx = S["x0"] if slot == 0 else S["x1"]
        p.wait_val("tensor", sx, ast["x_t"].pop((tb, tt)))
        p.wait("tensor", sv)
        for dd in range(nD):
            tb2 = dd % 2
            trb = psTr if tb2 == 0 else psKS
            p.wait_val("tensor", sa, a1_free[tb2])

            @p.on("tensor")
            def _(e, _dd=dd, _trb=trb, _slot=slot):
                e.matmul(_trb[:, 0:128],
                         xst[:, _slot * c.D + _dd * 128:
                             _slot * c.D + (_dd + 1) * 128],
                         idf_t[:, :], is_transpose=True,
                         start=True, stop=True).then_inc(st.h, 1)
            st.inc()
            p.wait("scalar", st)

            @p.on("scalar")
            def _(e, _tt=tt, _dd=dd, _trb=trb):
                col = _dd * TB + _tt * 128
                e.activation(out=xf[:, col:col + 128], in_=_trb[:, 0:128],
                             func=AF.Copy).then_inc(sa.h, 1)
            sa.inc()
            a1_free[tb2] = sa.n
        ast["xst_free"][slot] = st.n

    def fm_proj(w_nm, bidx, dest):
        bank_free = {0: (sa.n, sv.n), 1: (sa.n, sv.n)}
        wbuf_free = {0: st.n, 1: st.n}
        for oc in range(nD):
            wb = oc % 2
            sw = S["w0"] if wb == 0 else S["w1"]
            qeng = "sync" if (wb == 0 or not USE_2Q_A) else "scalar"
            p.wait_val(qeng, st, wbuf_free[wb])
            if tb == 0 and oc <= 1 and w_nm == "qT":
                p.wait_val(qeng, scc, 1)

            @p.on(qeng)
            def _(e, _nm=w_nm, _oc=oc, _wb=wb, _sw=sw):
                e.dma_start(out=wc[:, _wb * nD * 128:(_wb + 1) * nD * 128]
                            .rearrange("p (dd o) -> p dd o", o=128),
                            in_=gfw(_nm, _oc)).then_inc(_sw.h, 16)
            sw.inc(16)
            ps = psA if wb == 0 else psB
            p.wait("tensor", sw)
            sa_t, sv_t = bank_free[wb]
            p.wait_val("tensor", sa, sa_t)
            p.wait_val("tensor", sv, sv_t)

            @p.on("tensor")
            def _(e, _wb=wb, _ps=ps):
                for dd in range(nD):
                    w = wc[:, _wb * nD * 128 + dd * 128: _wb * nD * 128 + (dd + 1) * 128]
                    _li = e.matmul(_ps[:, 0:TB], w, xf[:, dd * TB:(dd + 1) * TB],
                                   start=(dd == 0), stop=(dd == nD - 1))
                _li.then_inc(st.h, 1)
            st.inc()
            wbuf_free[wb] = st.n
            p.wait("scalar", st)

            @p.on("scalar")
            def _(e, _oc=oc, _ps=ps, _bi=bidx):
                e.activation(out=dest[:, _oc * TB:(_oc + 1) * TB],
                             in_=_ps[:, 0:TB], func=AF.Identity,
                             bias=bias_t[:, _bi * c.nD + _oc: _bi * c.nD + _oc + 1]
                             ).then_inc(sa.h, 1)
            sa.inc()
            bank_free[wb] = (sa.n, 0)

    fm_proj("qT", 0, pf)
    a3_free = {0: (sv.n, sa.n), 1: (sv.n, sa.n)}
    for h in range(H):
        hp, par = h // 2, h % 2
        ps = psA if par == 0 else psB
        rows = slice(0, 64) if par == 0 else slice(64, 128)
        tpos = None if par == 0 else (0, 64)
        sv_t, sa_t = a3_free[par]
        p.wait_val("tensor", sv, sv_t)
        p.wait_val("tensor", sa, sa_t)

        @p.on("tensor")
        def _(e, _h=h, _ps=ps, _rows=rows, _tpos=tpos):
            q = pf[:, _h * TB:(_h + 1) * TB]
            e.matmul(_ps[_rows, 0:TB], wmT[:, 0:64], q, start=True, stop=True,
                     tile_position=_tpos).then_inc(st.h, 1)
        st.inc()
        p.wait("vector", st)

        eb = par * TB

        @p.on("vector")
        def _(e, _ps=ps, _rows=rows, _eb=eb):
            e.tensor_scalar(out=etmp1[_rows, _eb:_eb + TB], in0=_ps[_rows, 0:TB],
                            scalar1=0.0, scalar2=None, op0=OP.min)
        p.wait("scalar", sv)

        @p.on("scalar")
        def _(e, _rows=rows, _eb=eb):
            e.activation(out=etmp2[_rows, _eb:_eb + TB], in_=etmp1[_rows, _eb:_eb + TB],
                         func=AF.Exp).then_inc(sa.h, 1)
        sa.inc()
        p.wait("vector", sa)

        @p.on("vector")
        def _(e, _ps=ps, _rows=rows, _hp=hp, _eb=eb):
            e.scalar_tensor_tensor(out=QM[_rows, _hp * c.T + t0: _hp * c.T + t0 + TB],
                                   in0=_ps[_rows, 0:TB],
                                   scalar=0.0, in1=etmp2[_rows, _eb:_eb + TB],
                                   op0=OP.max, op1=OP.add)
        a3_free[par] = (sv.n, 0)

    fm_proj("kT", 1, pf)
    nhalf = max(H // 8, 1)
    nh = min(H, 8)
    a5_free = {0: (sv.n, sa.n), 1: (sv.n, sa.n)}
    for tt in range(nTT):
        for half in range(nhalf):
            par5 = (tt * nhalf + half) % 2
            ps = psA if par5 == 0 else psB
            eb = par5 * 512
            sv_t, sa_t = a5_free[par5]
            p.wait_val("tensor", sv, sv_t)
            p.wait_val("tensor", sa, sa_t)

            @p.on("tensor")
            def _(e, _tt=tt, _half=half, _ps=ps):
                for hh in range(nh):
                    h = _half * 8 + hh
                    k_ = pf[:, h * TB + _tt * 128: h * TB + (_tt + 1) * 128]
                    _li = e.matmul(_ps[:, hh * 64:(hh + 1) * 64], k_,
                                   wmT[:, 64:128], start=(hh == 0),
                                   stop=(hh == nh - 1), skip_group_check=True)
                _li.then_inc(st.h, 1)
            st.inc()
            W = nh * 64
            p.wait("vector", st)

            @p.on("vector")
            def _(e, _ps=ps, _eb=eb):
                e.tensor_scalar(out=etmp1[:, _eb:_eb + W], in0=_ps[:, 0:W],
                                scalar1=0.0, scalar2=None, op0=OP.min)
            p.wait("scalar", sv)

            @p.on("scalar")
            def _(e, _eb=eb):
                e.activation(out=etmp2[:, _eb:_eb + W], in_=etmp1[:, _eb:_eb + W],
                             func=AF.Exp).then_inc(sa.h, 1)
            sa.inc()
            p.wait("vector", sa)
            cb = tt * H * 64 + half * 8 * 64

            @p.on("vector")
            def _(e, _ps=ps, _cb=cb, _eb=eb):
                e.scalar_tensor_tensor(out=kmB[:, _cb:_cb + W], in0=_ps[:, 0:W],
                                       scalar=0.0, in1=etmp2[:, _eb:_eb + W],
                                       op0=OP.max, op1=OP.add)
            a5_free[par5] = (sv.n, 0)

    fm_proj("vT", 2, pf)

    if tb + 1 < c.nB:
        _a1_loads(p, c, io, S, tb + 1, ast, xst, list(range(min(2, c.nTT))))

    for tt in range(nTT):
        vs = tt % 2
        first_evac = True
        for dd in range(nD):
            p.wait("tensor", sa)

            @p.on("tensor")
            def _(e, _tt=tt, _dd=dd):
                e.matmul(psTr[:, 0:128],
                         pf[:, _dd * TB + _tt * 128: _dd * TB + (_tt + 1) * 128],
                         idf_t[:, :], is_transpose=True,
                         start=True, stop=True).then_inc(st.h, 1)
            st.inc()
            if first_evac:
                first_evac = False
                p.wait_val("scalar", st, ast["vt_free"][vs])
            p.wait("scalar", st)

            @p.on("scalar")
            def _(e, _dd=dd, _vs=vs):
                e.activation(out=vt[:, _vs * c.D + _dd * 128: _vs * c.D + (_dd + 1) * 128],
                             in_=psTr[:, 0:128], func=AF.Copy).then_inc(sa.h, 1)
            sa.inc()
        vt_ready = sa.n

        p.wait_val("tensor", sa, vt_ready)
        p.wait("tensor", sv)

        @p.on("tensor")
        def _(e, _tt=tt, _tb=tb, _vs=vs):
            for hp in range(HP):
                bank = psKV[hp // 2]
                cb0 = (hp % 2) * 256
                km = kmB[:, _tt * H * 64 + hp * 128: _tt * H * 64 + hp * 128 + 128]
                vA = vt[:, _vs * c.D + (2 * hp) * 128: _vs * c.D + (2 * hp + 1) * 128]
                vB = vt[:, _vs * c.D + (2 * hp + 1) * 128: _vs * c.D + (2 * hp + 2) * 128]
                first = _tt == 0 and hp % 2 == 0
                _li = e.matmul(bank[:, cb0:cb0 + 128], km, vA, start=first,
                         stop=False, skip_group_check=True)
                _li = e.matmul(bank[:, cb0 + 128:cb0 + 256], km, vB, start=False,
                         stop=False, skip_group_check=True)
                _li = e.matmul(psKS[:, hp:hp + 1], km, ones_f[:, 0:1],
                         start=(_tt == 0 and hp == 0), stop=False,
                         skip_group_check=True)
            _li.then_inc(st.h, 1)
        st.inc()
        ast["vt_free"][vs] = st.n

    p.wait("vector", st)

    @p.on("vector")
    def _(e):
        for b in range(nKB):
            cc = b * 512
            w = min(512, (HP - 2 * b) * 256)
            e.tensor_tensor(out=kvacc[:, cc:cc + w], in0=psKV[b][:, 0:w],
                            in1=kvacc[:, cc:cc + w], op=OP.add)
        cc = nKB * 512
        e.tensor_tensor(out=kvacc[:, cc:cc + HP], in0=psKS[:, 0:HP],
                        in1=kvacc[:, cc:cc + HP], op=OP.add)


def _phase_b(nc, c, io, S, gfw, idf_t, idb_t, bias_t, kvacc, QM, nKB):
    sv, sa, st, sp = (S[k] for k in ("sv", "sa", "st", "sp"))
    sg = S["sg"]
    TB, nTT, nD, H, HP = c.TBLK, c.nTT, c.nD, c.H, c.HP
    _ES = contextlib.ExitStack()
    kv2 = _ES.enter_context(nc.sbuf_tensor("kv2", [128, H * 128], F32))
    ks2 = _ES.enter_context(nc.sbuf_tensor("ks2", [128, 2 * HP], F32))
    rinvT = _ES.enter_context(nc.sbuf_tensor("rinvT", [128, c.nTT * H], F32))
    oat = _ES.enter_context(nc.sbuf_tensor("oat", [128, H * TB], F32))
    oa_f = _ES.enter_context(nc.sbuf_tensor("oa_f", [128, 1024], F32))
    resid = _ES.enter_context(nc.sbuf_tensor("resid", [128, nTT * c.D], F32))
    wc2 = _ES.enter_context(nc.sbuf_tensor("wc2", [128, 2 * nD * 128], F32))
    yf = _ES.enter_context(nc.sbuf_tensor("yf", [128, 2 * TB], F32))
    xres = _ES.enter_context(nc.sbuf_tensor("xres", [128, 256], F32))
    x1w = _ES.enter_context(nc.sbuf_tensor("x1w", [128, c.D], F32))
    lnm = _ES.enter_context(nc.sbuf_tensor("lnm", [128, 8], F32))
    x1q = _ES.enter_context(nc.sbuf_tensor("x1q", [128, c.D], BF16))
    x1q2 = _ES.enter_context(nc.sbuf_tensor("x1q2", [128, c.D], BF16))
    b_amax = _ES.enter_context(nc.sbuf_tensor("b_amax", [128, c.D // 32], I32))
    b_sc8 = _ES.enter_context(nc.sbuf_tensor("b_sc8", [128, c.D // 32], I32))
    b_inv = _ES.enter_context(nc.sbuf_tensor("b_inv", [128, c.D // 32], I32))
    b_m64 = _ES.enter_context(nc.sbuf_tensor("b_m64", [128, c.D // 32], I32))
    b_z = _ES.enter_context(nc.sbuf_tensor("b_z", [128, c.D], F32))
    b_q8 = _ES.enter_context(nc.sbuf_tensor("b_q8", [128, c.D], FP8))
    psA2 = _ES.enter_context(nc.psum_tensor("psA2", [128, 512], F32))
    psB2 = _ES.enter_context(nc.psum_tensor("psB2", [128, 512], F32))
    psN = _ES.enter_context(nc.psum_tensor("psN", [128, 512], F32))
    psT2 = _ES.enter_context(nc.psum_tensor("psT2", [128, 512], F32))
    psT2b = _ES.enter_context(nc.psum_tensor("psT2b", [128, 1024], BF16))
    tlB = dict(amax=b_amax, sc8=b_sc8, inv=b_inv, m64=b_m64, z=b_z, q8=b_q8)

    p = Prog(S)
    barrier(p, [sg], ENGS)

    @p.on("vector")
    def _(e):
        e.memset(kv2[:, :], 0.0)
        e.memset(ks2[:, :], 0.0)
        ks = nKB * 512
        for hp in range(HP):
            b, j = hp // 2, hp % 2
            cc = b * 512 + j * 256
            e.tensor_copy(kv2[0:64, (2 * hp) * 128:(2 * hp + 1) * 128],
                          kvacc[0:64, cc:cc + 128])
            e.tensor_copy(kv2[64:128, (2 * hp + 1) * 128:(2 * hp + 2) * 128],
                          kvacc[64:128, cc + 128:cc + 256])
            e.tensor_copy(ks2[0:64, 2 * hp:2 * hp + 1],
                          kvacc[0:64, ks + hp:ks + hp + 1])
            e.tensor_copy(ks2[64:128, 2 * hp + 1:2 * hp + 2],
                          kvacc[64:128, ks + hp:ks + hp + 1])

    bst = dict(oa_free={0: 0, 1: 0}, psN_free=S["sv"].n,
               psAB_free={0: [0, 0], 1: [0, 0]}, psT2_free=[0, 0],
               pend_ln1=[])
    for tb in range(c.nB):
        _attn_out_block(p, c, io, S, tb, gfw, idf_t, idb_t, bias_t, QM,
                        kv2, ks2, rinvT, oat, oa_f, resid, wc2, yf, xres,
                        x1w, lnm, x1q, x1q2, tlB, bst,
                        psA2, psB2, psN, psT2, psT2b)

    barrier(p, [S["w0"], S["w1"], S["r0"], S["r1"], S["ssp"], sg, sv, sa,
                st, sp], ENGS)
    p.play(nc)
    _ES.close()


def _attn_out_block(p, c, io, S, tb, gfw, idf_t, idb_t, bias_t, QM,
                    kv2, ks2, rinvT, oat, oa_f, resid, wc2, yf, xres,
                    x1w, lnm, x1q, x1q2, tlB, bst,
                    psA2, psB2, psN, psT2, psT2b):
    sv, sa, st, sp = (S[k] for k in ("sv", "sa", "st", "sp"))
    TB, nTT, nD, H, HP = c.TBLK, c.nTT, c.nD, c.H, c.HP
    t0 = tb * TB
    ssp = S["ssp"]

    pend_tr = None
    for tt in range(nTT):
        p.wait_val("tensor", sv, bst["psN_free"])

        @p.on("tensor")
        def _(e, _tt=tt):
            for hp in range(HP):
                cc = hp * c.T + t0 + _tt * 128
                _li = e.matmul(psN[:, 2 * hp:2 * hp + 2], QM[:, cc:cc + 128],
                               ks2[:, 2 * hp:2 * hp + 2], start=(hp == 0),
                               stop=(hp == HP - 1), skip_group_check=True)
            _li.then_inc(st.h, 1)
        st.inc()
        p.wait("vector", st)

        @p.on("vector")
        def _(e, _tt=tt):
            e.tensor_scalar(out=rinvT[:, _tt * H:(_tt + 1) * H], in0=psN[:, 0:H],
                            scalar1=EPS_ATT, scalar2=None, op0=OP.add)
            e.reciprocal(out=rinvT[:, _tt * H:(_tt + 1) * H],
                         in_=rinvT[:, _tt * H:(_tt + 1) * H])
        bst["psN_free"] = sv.n

        GW = min(4, H)
        ps_free = bst["psAB_free"]
        oa_free = bst["oa_free"]
        for grp in range(H // GW):
            par = grp % 2
            ob = par * 512
            ps = psA2 if par == 0 else psB2
            sv_t, sa_t = ps_free[par]
            p.wait_val("tensor", sv, sv_t)
            p.wait_val("tensor", sa, sa_t)

            @p.on("tensor")
            def _(e, _g=grp, _ps=ps, _tt=tt):
                for j in range(GW):
                    h = _g * GW + j
                    cc = (h // 2) * c.T + t0 + _tt * 128
                    _li = e.matmul(_ps[:, j * 128:(j + 1) * 128],
                                   QM[:, cc:cc + 128],
                                   kv2[:, h * 128:(h + 1) * 128],
                                   start=True, stop=True, skip_group_check=True)
                _li.then_inc(st.h, 1)
            st.inc()
            if pend_tr is not None:
                pend_tr()
                pend_tr = None
            mm_t = st.n
            p.wait_val("vector", st, mm_t)
            p.wait_val("vector", st, oa_free[par])

            @p.on("vector")
            def _(e, _ps=ps, _tt=tt, _g=grp, _ob=ob):
                e.tensor_tensor(
                    out=oa_f[:, _ob:_ob + GW * 128].rearrange(
                        "p (j q) -> p j q", q=128),
                    in0=_ps[:, 0:GW * 128].rearrange("p (j q) -> p j q", q=128),
                    in1=rinvT[:, _tt * H + _g * GW: _tt * H + (_g + 1) * GW]
                    .unsqueeze(2).broadcast_to([128, GW, 128]),
                    op=OP.mult)
            ps_free[par][0] = sv.n
            dv_t = sv.n

            def emit_tr(_g=grp, _tt=tt, _ob=ob, _par=par, _dt=dv_t):
                p.wait_val("tensor", sv, _dt)
                for j in range(GW):
                    p.wait_val("tensor", sa, bst["psT2_free"][0])
                    p.wait_val("tensor", sv, bst["psT2_free"][1])

                    @p.on("tensor")
                    def _(e, _j=j):
                        e.matmul(psT2[:, 0:128],
                                 oa_f[:, _ob + _j * 128:_ob + (_j + 1) * 128],
                                 idf_t[:, :], is_transpose=True,
                                 start=True, stop=True).then_inc(st.h, 1)
                    st.inc()
                    p.wait("scalar", st)

                    @p.on("scalar")
                    def _(e, _j=j):
                        col = (_g * GW + _j) * TB + _tt * 128
                        e.activation(out=oat[:, col:col + 128], in_=psT2[:, 0:128],
                                     func=AF.Copy).then_inc(sa.h, 1)
                    sa.inc()
                    bst["psT2_free"][0] = sa.n
                oa_free[_par] = st.n
            pend_tr = emit_tr
        if bst["pend_ln1"]:
            bst["pend_ln1"].pop(0)()
    if pend_tr is not None:
        pend_tr()

    bank_free = bst["psAB_free"]
    wbuf_free = {0: st.n, 1: st.n}
    yf_free = {0: st.n, 1: st.n}
    p.wait("tensor", sa)
    for oc in range(nD):
        wb = oc % 2
        sw = S["w0"] if wb == 0 else S["w1"]
        qeng = "sync" if (wb == 0 or not USE_2Q_B) else "scalar"
        p.wait_val(qeng, st, wbuf_free[wb])

        @p.on(qeng)
        def _(e, _oc=oc, _wb=wb, _sw=sw):
            e.dma_start(out=wc2[:, _wb * nD * 128:(_wb + 1) * nD * 128]
                        .rearrange("p (dd o) -> p dd o", o=128),
                        in_=gfw("oT", _oc)).then_inc(_sw.h, 16)
        sw.inc(16)
        ps = psA2 if wb == 0 else psB2
        p.wait("tensor", sw)
        sv_t, sa_t = bank_free[wb]
        p.wait_val("tensor", sa, sa_t)
        p.wait_val("tensor", sv, sv_t)

        @p.on("tensor")
        def _(e, _wb=wb, _ps=ps):
            for dd in range(nD):
                w = wc2[:, _wb * nD * 128 + dd * 128: _wb * nD * 128 + (dd + 1) * 128]
                _li = e.matmul(_ps[:, 0:TB], w, oat[:, dd * TB:(dd + 1) * TB],
                               start=(dd == 0), stop=(dd == nD - 1))
            _li.then_inc(st.h, 1)
        st.inc()
        wbuf_free[wb] = st.n
        p.wait("scalar", st)
        p.wait_val("scalar", st, yf_free[wb])

        @p.on("scalar")
        def _(e, _oc=oc, _ps=ps, _wb=wb):
            e.activation(out=yf[:, _wb * TB:_wb * TB + TB], in_=_ps[:, 0:TB],
                         func=AF.Identity,
                         bias=bias_t[:, 3 * c.nD + _oc: 3 * c.nD + _oc + 1]
                         ).then_inc(sa.h, 1)
        sa.inc()
        bank_free[wb][1] = sa.n
        yf_t = sa.n
        for tt in range(nTT):
            sr = S["r0"] if tt % 2 == 0 else S["r1"]
            p.wait("sync", sv)

            @p.on("sync")
            def _(e, _oc=oc, _tt=tt, _sr=sr):
                e.dma_start(out=xres[:, (_tt % 2) * 128:(_tt % 2) * 128 + 128],
                            in_=io["x_own"][t0 + _tt * 128: t0 + (_tt + 1) * 128,
                                            _oc * 128:(_oc + 1) * 128]
                            ).then_inc(_sr.h, 16)
            sr.inc(16)
            p.wait_val("tensor", sa, max(bst["psT2_free"][0], yf_t))
            p.wait_val("tensor", sv, bst["psT2_free"][1])

            @p.on("tensor")
            def _(e, _tt=tt, _wb=wb):
                e.matmul(psT2[:, 0:128],
                         yf[:, _wb * TB + _tt * 128: _wb * TB + (_tt + 1) * 128],
                         idf_t[:, :], is_transpose=True,
                         start=True, stop=True).then_inc(st.h, 1)
            st.inc()
            p.wait("vector", st)
            p.wait("vector", sr)

            @p.on("vector")
            def _(e, _oc=oc, _tt=tt):
                e.scalar_tensor_tensor(
                    out=resid[:, _tt * c.D + _oc * 128: _tt * c.D + (_oc + 1) * 128],
                    in0=psT2[:, 0:128], scalar=1.0,
                    in1=xres[:, (_tt % 2) * 128:(_tt % 2) * 128 + 128],
                    op0=OP.mult, op1=OP.add)
            bst["psT2_free"][1] = sv.n
        yf_free[wb] = st.n

    for tt in range(nTT):
        def ln1_tt(tt=tt):
            p.wait("vector", ssp)
            p.wait("vector", sp)

            @p.on("vector")
            def _(e, _tt=tt):
                r = resid[:, _tt * c.D:(_tt + 1) * c.D]
                e.tensor_reduce(out=lnm[:, 0:1], in_=r, axis=AX, op=OP.add)
                e.tensor_scalar(out=lnm[:, 0:1], in0=lnm[:, 0:1],
                                scalar1=1.0 / c.D, scalar2=None, op0=OP.mult)
                e.tensor_scalar(out=x1w[:, :], in0=r, scalar1=lnm[:, 0:1],
                                scalar2=None, op0=OP.subtract)
            p.wait("scalar", sv)

            @p.on("scalar")
            def _(e):
                e.activation(out=tlB["z"][:, 0:c.D], in_=x1w[:, :], func=AF.Square,
                             accum_out=lnm[:, 1:2]).then_inc(sa.h, 1)
            sa.inc()
            p.wait("vector", sa)

            @p.on("vector")
            def _(e):
                e.tensor_scalar(out=lnm[:, 1:2], in0=lnm[:, 1:2],
                                scalar1=1.0 / c.D, scalar2=EPS_LN,
                                op0=OP.mult, op1=OP.add)
            p.wait("scalar", sv)

            @p.on("scalar")
            def _(e):
                e.activation(out=lnm[:, 2:3], in_=lnm[:, 1:2],
                             func=AF.Sqrt).then_inc(sa.h, 1)
            sa.inc()
            p.wait("vector", sa)

            @p.on("vector")
            def _(e):
                e.reciprocal(out=lnm[:, 3:4], in_=lnm[:, 2:3])
                e.tensor_scalar(out=x1w[:, :], in0=x1w[:, :], scalar1=lnm[:, 3:4],
                                scalar2=None, op0=OP.mult)
            p.wait("sync", sv)

            @p.on("sync")
            def _(e, _tt=tt):
                e.dma_start(out=io["x1_sp"][t0 + _tt * 128: t0 + (_tt + 1) * 128, :],
                            in_=x1w[:, :]).then_inc(ssp.h, 16)
            ssp.inc(16)
            _act_quant(p, c, x1w[:, :], c.D, tlB, x1q)
            p.wait("gpsimd", sv)
            p.wait("gpsimd", ssp)

            @p.on("gpsimd")
            def _(e):
                e.tensor_copy(x1q2[:, :].bitcast(U16), x1q[:, :].bitcast(U16)
                              ).then_inc(sp.h, 1)
            sp.inc()
            p.wait("sync", sp)

            @p.on("sync")
            def _(e, _tt=tt):
                e.dma_start(out=io["x1q_sp"][t0 + _tt * 128: t0 + (_tt + 1) * 128, :],
                            in_=x1q2[:, :]).then_inc(ssp.h, 16)
            ssp.inc(16)
        bst["pend_ln1"].append(ln1_tt)
    if tb == c.nB - 1:
        while bst["pend_ln1"]:
            bst["pend_ln1"].pop(0)()


def _phase_f(nc, c, io, S, idb_t):
    _phase_f1(nc, c, io, S, idb_t)
    _phase_f2(nc, c, io, S)


def _phase_f1(nc, c, io, S, idb_t):
    sv, sa, st, sp, scc = (S[k] for k in ("sv", "sa", "st", "sp", "scc"))
    sg = S["sg"]
    SB, nST, nD, nI, nOS, OSW = c.SB, c.nST, c.nD, c.nI, c.nOS, c.OSW
    TH = max(c.T // 2, 128)
    nTC = TH // 128
    _ES = contextlib.ExitStack()
    wgs = _ES.enter_context(nc.sbuf_tensor("wgs", [128, 2 * nD * OSW], BF16))
    wus = _ES.enter_context(nc.sbuf_tensor("wus", [128, 2 * nD * OSW], BF16))
    x1qTh = _ES.enter_context(nc.sbuf_tensor("x1qTh", [128, nD * TH], BF16))
    intw = _ES.enter_context(nc.sbuf_tensor("intw", [128, 2 * OSW], F32))
    iq = _ES.enter_context(nc.sbuf_tensor("iq", [128, 2 * OSW], BF16))
    iq2 = _ES.enter_context(nc.sbuf_tensor("iq2", [128, 2 * OSW], BF16))
    f_amax = _ES.enter_context(nc.sbuf_tensor("f_amax", [128, OSW // 16], I32))
    f_sc8 = _ES.enter_context(nc.sbuf_tensor("f_sc8", [128, OSW // 16], I32))
    f_inv = _ES.enter_context(nc.sbuf_tensor("f_inv", [128, OSW // 16], I32))
    f_m64 = _ES.enter_context(nc.sbuf_tensor("f_m64", [128, OSW // 16], I32))
    f_z = _ES.enter_context(nc.sbuf_tensor("f_z", [128, 2 * OSW], F32))
    f_q8 = _ES.enter_context(nc.sbuf_tensor("f_q8", [128, 2 * OSW], FP8))
    iqs = _ES.enter_context(nc.sbuf_tensor("iqs", [128, 2 * OSW], BF16))
    psb = [_ES.enter_context(nc.psum_tensor("psf%d" % i, [128, 512], F32))
           for i in range(6)]
    psT = [_ES.enter_context(nc.psum_tensor("psq%d" % i, [128, 1024], BF16))
           for i in range(2)]

    p = Prog(S)
    barrier(p, [S["ssp"], S["w0"], S["w1"], S["r0"], S["r1"], sg, sv, sa,
                st, sp], ENGS)
    p.wait("sync", scc)
    p.wait("tensor", scc)

    tls = []
    for s in range(2):
        nb = OSW // 32
        tls.append(dict(
            amax=f_amax[:, s * nb:(s + 1) * nb],
            sc8=f_sc8[:, s * nb:(s + 1) * nb],
            inv=f_inv[:, s * nb:(s + 1) * nb],
            m64=f_m64[:, s * nb:(s + 1) * nb],
            z=f_z[:, s * OSW:(s + 1) * OSW],
            q8=f_q8[:, s * OSW:(s + 1) * OSW]))

    fst = dict(ps_free=[(0, 0), (0, 0), (0, 0)],
               intw_free=[0, 0], iq_free=[0, 0], iq2_free=[0, 0],
               xh_free=0, gu_free=[0, 0], k=0, pend=[], psT_free=[0, 0],
               stg_free=[0, 0],
               wd_sem=[S["wd0"], S["wd1"]], wd_buf_free=[0, 0], wd_k=0)
    sgu = S["gu0"]
    sgu2 = S["gu1"]
    sxq = S["sxq"]

    for th in range(c.T // TH):
        ht0 = th * TH
        p.wait_val("sync", st, fst["xh_free"])
        p.wait("sync", S["ssp"])

        @p.on("sync")
        def _(e, _ht0=ht0):
            for dd in range(nD):
                e.dma_start(out=x1qTh[:, dd * TH:(dd + 1) * TH],
                            in_=io["x1q_sp"][_ht0:_ht0 + TH,
                                             dd * 128:(dd + 1) * 128],
                            transpose=True).then_inc(sxq.h, 16)
        sxq.inc(16 * nD)
        xq_t = sxq.n

        for osx in range(nOS):
            gpar = osx % 2
            gsem = sgu if gpar == 0 else sgu2
            p.wait_val("sync", st, fst["gu_free"][gpar])

            @p.on("sync")
            def _(e, _osx=osx, _gpar=gpar, _gs=gsem):
                rk = (_osx * OSW) // c.FSH
                lo = (_osx * OSW) % c.FSH
                for nm, buf in (("gq", wgs), ("uq", wus)):
                    base = rk * c.GBN + c.GB_OFF[nm]
                    piece = io["gout_b"][base:base + c.D * c.FSH].rearrange(
                        "(dd pp w) -> pp dd w", pp=128, w=c.FSH)
                    e.dma_start(
                        out=buf[:, _gpar * nD * OSW:(_gpar + 1) * nD * OSW]
                        .rearrange("p (dd o) -> p dd o", o=OSW),
                        in_=piece[:, :, lo:lo + OSW]).then_inc(_gs.h, 16)
            gsem.inc(16 * 2)

            for tc in range(nTC):
                k = fst["k"]
                kset = k % 3
                sl = k % 2
                pg, pu = psb[2 * kset], psb[2 * kset + 1]
                while len(fst["pend"]) >= 2:
                    fst["pend"].pop(0)()
                p.wait_val("tensor", sxq, xq_t)
                p.wait("tensor", gsem)
                sa_t, sv_t = fst["ps_free"][kset]
                p.wait_val("tensor", sa, sa_t)
                p.wait_val("tensor", sv, sv_t)

                @p.on("tensor")
                def _(e, _tc=tc, _gpar=gpar, _pg=pg, _pu=pu):
                    for dd in range(nD):
                        xw = x1qTh[:, dd * TH + _tc * 128: dd * TH + (_tc + 1) * 128]
                        gw = wgs[:, _gpar * nD * OSW + dd * OSW:
                                 _gpar * nD * OSW + (dd + 1) * OSW]
                        uw = wus[:, _gpar * nD * OSW + dd * OSW:
                                 _gpar * nD * OSW + (dd + 1) * OSW]
                        _li = e.matmul(_pg[:, 0:OSW], xw, gw, start=(dd == 0),
                                       stop=(dd == nD - 1))
                        _li = e.matmul(_pu[:, 0:OSW], xw, uw, start=(dd == 0),
                                       stop=(dd == nD - 1))
                    _li.then_inc(st.h, 1)
                st.inc()
                mm_t = st.n
                fst["gu_free"][gpar] = st.n

                p.wait_val("scalar", st, mm_t)
                p.wait_val("scalar", sv, fst["intw_free"][sl])

                @p.on("scalar")
                def _(e, _sl=sl, _pg=pg):
                    e.activation(out=intw[:, _sl * OSW:(_sl + 1) * OSW],
                                 in_=_pg[:, 0:OSW],
                                 func=AF.Silu).then_inc(sa.h, 1)
                sa.inc()
                silu_t = sa.n
                p.wait_val("vector", sa, silu_t)

                @p.on("vector")
                def _(e, _sl=sl, _pu=pu):
                    e.tensor_tensor(out=intw[:, _sl * OSW:(_sl + 1) * OSW],
                                    in0=intw[:, _sl * OSW:(_sl + 1) * OSW],
                                    in1=_pu[:, 0:OSW], op=OP.mult)
                fst["ps_free"][kset] = (silu_t, sv.n)

                p.wait_val("vector", sp, fst["iq_free"][sl])
                _act_quant(p, c, intw[:, sl * OSW:(sl + 1) * OSW], OSW,
                           tls[sl], iq[:, sl * OSW:(sl + 1) * OSW])
                fst["intw_free"][sl] = sv.n
                quant_t = sv.n

                p.wait_val("gpsimd", sv, quant_t)
                p.wait_val("gpsimd", st, fst["iq2_free"][sl])

                @p.on("gpsimd")
                def _(e, _sl=sl):
                    e.tensor_copy(iq2[:, _sl * OSW:(_sl + 1) * OSW].bitcast(U16),
                                  iq[:, _sl * OSW:(_sl + 1) * OSW].bitcast(U16)
                                  ).then_inc(sp.h, 1)
                sp.inc()
                fst["iq_free"][sl] = sp.n
                laund_t = sp.n

                def emit_tr(_sl=sl, _osx=osx, _tcol=ht0 + tc * 128,
                            _pslot=fst["k"] % 2, _lt=laund_t):
                    _psq = psT[_pslot]
                    _ssi = S["si0"] if _pslot == 0 else S["si1"]
                    nj = OSW // 128
                    p.wait_val("tensor", sp, _lt)
                    p.wait_val("tensor", sa, fst["psT_free"][_pslot])

                    @p.on("tensor")
                    def _(e):
                        for j in range(nj):
                            _li = e.matmul(_psq[:, j * 128:(j + 1) * 128],
                                           iq2[:, _sl * OSW + j * 128:
                                               _sl * OSW + (j + 1) * 128],
                                           idb_t[:, :], is_transpose=True,
                                           start=True, stop=True)
                        _li.then_inc(st.h, 1)
                    st.inc()
                    fst["iq2_free"][_sl] = st.n
                    p.wait_val("scalar", st, st.n)
                    p.wait_val("scalar", _ssi, fst["stg_free"][_pslot])

                    @p.on("scalar")
                    def _(e):
                        e.activation(out=iqs[:, _pslot * OSW:(_pslot + 1) * OSW],
                                     in_=_psq[:, 0:OSW],
                                     func=AF.Copy).then_inc(sa.h, 1)
                    sa.inc()
                    fst["psT_free"][_pslot] = sa.n
                    p.wait_val("sync", sa, sa.n)

                    @p.on("sync")
                    def _(e):
                        e.dma_start(
                            out=io["iq_sp"][_osx * OSW:(_osx + 1) * OSW,
                                            _tcol:_tcol + 128]
                            .rearrange("(j p) t -> p j t", p=128),
                            in_=iqs[:, _pslot * OSW:(_pslot + 1) * OSW]
                            .rearrange("p (j t) -> p j t", t=128)
                            ).then_inc(_ssi.h, 16)
                    _ssi.inc(16)
                    fst["stg_free"][_pslot] = _ssi.n
                fst["pend"].append(emit_tr)
                fst["k"] += 1
        fst["xh_free"] = st.n
    while fst["pend"]:
        fst["pend"].pop(0)()

    p.play(nc)
    _ES.close()


def _phase_f2(nc, c, io, S):
    sv, sa, st, sp = (S[k] for k in ("sv", "sa", "st", "sp"))
    SB, nST, nD, nI = c.SB, c.nST, c.nD, c.nI
    _ES = contextlib.ExitStack()
    iqT = _ES.enter_context(nc.sbuf_tensor("iqT", [128, 2 * nI * SB], BF16))
    wds = _ES.enter_context(nc.sbuf_tensor("wds", [128, 2 * c.D], BF16))
    res2 = _ES.enter_context(nc.sbuf_tensor("res2", [128, nST * c.D], F32))
    x1res = _ES.enter_context(nc.sbuf_tensor("x1res", [128, 2 * c.D], F32))
    f_lnm = _ES.enter_context(nc.sbuf_tensor("f_lnm", [128, 8], F32))
    x2w = _ES.enter_context(nc.sbuf_tensor("x2w", [128, c.D], F32))
    psb = [_ES.enter_context(nc.psum_tensor("psg%d" % i, [128, 512], F32))
           for i in range(8)]
    fst = dict(wd_sem=[S["wd0"], S["wd1"]], wd_buf_free=[0, 0], wd_k=0)
    iq_sems = [(S["sxq"], S["skv"]), (S["sj0"], S["sj1"])]
    p = Prog(S)
    barrier(p, [S["ssp"], S["si0"], S["si1"], S["sxq"], S["skv"],
                S["sj0"], S["sj1"], S["gu0"], S["gu1"], sv, sa, st, sp], ENGS)
    dn_state = dict(iqs_free=[0, 0], iqT_t=[0, 0], dn_free=(sa.n, sv.n))

    def iqt_prefetch(t0p, qs):
        pair = iq_sems[qs]

        @p.on("sync")
        def _(e, _t0=t0p, _qs=qs, _ss=pair[0]):
            e.dma_start(out=iqT[:, _qs * nI * SB:(_qs + 1) * nI * SB]
                        .rearrange("p (ii t) -> p ii t", t=SB),
                        in_=io["iq_sp"].rearrange("(ii p) t -> p ii t", p=128)
                        [:, :, _t0:_t0 + SB]).then_inc(_ss.h, 16)
        pair[0].inc(16)
        dn_state["iqT_t"][qs] = (pair[0].n, pair[1].n)

    iqt_prefetch(0, 0)
    for sb in range(c.nSB):
        t0 = sb * SB
        qs = sb % 2
        if sb + 1 < c.nSB:
            nqs = (sb + 1) % 2
            p.wait_val("sync", st, dn_state["iqs_free"][nqs])
            iqt_prefetch(t0 + SB, nqs)
        iqT_t, iqT_t2 = dn_state["iqT_t"][qs]
        sxq, sxq2 = iq_sems[qs]

        SPW = min(512, c.D)
        nSpan = c.D // SPW
        banks = psb[:nST * nSpan]
        sa_t, sv_t = dn_state["dn_free"]
        p.wait_val("tensor", sa, sa_t)
        p.wait_val("tensor", sv, sv_t)
        p.wait_val("tensor", sxq, iqT_t)
        p.wait_val("tensor", sxq2, iqT_t2)
        for ii in range(nI):
            wpar = fst["wd_k"] % 2
            swd = fst["wd_sem"][wpar]
            qeng = "sync" if (wpar == 0 or not USE_2Q_F) else "scalar"
            p.wait_val(qeng, st, fst["wd_buf_free"][wpar])

            @p.on(qeng)
            def _(e, _ii=ii, _wpar=wpar, _swd=swd):
                base = c.GB_OFF["dq"] + _ii * 128 * c.DSH
                view = io["gout_b"].rearrange("(rk x) -> rk x", x=c.GBN)
                ap = view[:, base:base + 128 * c.DSH].rearrange(
                    "rk (pp w) -> pp rk w", pp=128)
                e.dma_start(out=wds[:, _wpar * c.D:(_wpar + 1) * c.D]
                            .rearrange("p (rk w) -> p rk w", w=c.DSH),
                            in_=ap).then_inc(_swd.h, 16)
            swd.inc(16)
            p.wait("tensor", swd)

            @p.on("tensor")
            def _(e, _ii=ii, _wpar=wpar, _qs=qs):
                for t1 in range(nST):
                    for spn in range(nSpan):
                        bank = banks[t1 * nSpan + spn]
                        w = wds[:, _wpar * c.D + spn * SPW:
                                _wpar * c.D + (spn + 1) * SPW]
                        _li = e.matmul(bank[:, 0:SPW],
                                       iqT[:, _qs * nI * SB + _ii * SB + t1 * 128:
                                           _qs * nI * SB + _ii * SB + (t1 + 1) * 128],
                                       w, start=(_ii == 0), stop=(_ii == nI - 1),
                                       skip_group_check=True)
                _li.then_inc(st.h, 1)
            st.inc()
            fst["wd_buf_free"][wpar] = st.n
            fst["wd_k"] += 1
        dn_t = st.n
        dn_state["iqs_free"][qs] = st.n

        for t1 in range(nST):
            sf = S["f0"] if t1 % 2 == 0 else S["f1"]
            p.wait("sync", sv)

            @p.on("sync")
            def _(e, _t1=t1, _sf=sf, _t0=t0):
                e.dma_start(out=x1res[:, (_t1 % 2) * c.D:(_t1 % 2) * c.D + c.D],
                            in_=io["x1_sp"][_t0 + _t1 * 128: _t0 + (_t1 + 1) * 128, :]
                            ).then_inc(_sf.h, 16)
            sf.inc(16)
            p.wait_val("vector", st, dn_t)
            p.wait("vector", sf)

            @p.on("vector")
            def _(e, _t1=t1):
                for spn in range(nSpan):
                    bank = banks[_t1 * nSpan + spn]
                    e.scalar_tensor_tensor(
                        out=res2[:, _t1 * c.D + spn * SPW: _t1 * c.D + (spn + 1) * SPW],
                        in0=bank[:, 0:SPW], scalar=1.0,
                        in1=x1res[:, (_t1 % 2) * c.D + spn * SPW:
                                  (_t1 % 2) * c.D + (spn + 1) * SPW],
                        op0=OP.mult, op1=OP.add)
        dn_state["dn_free"] = (0, sv.n)

        so = S["so"]
        for t1 in range(nST):
            p.wait("vector", so)

            @p.on("vector")
            def _(e, _t1=t1):
                r = res2[:, _t1 * c.D:(_t1 + 1) * c.D]
                e.tensor_reduce(out=f_lnm[:, 0:1], in_=r, axis=AX, op=OP.add)
                e.tensor_scalar(out=f_lnm[:, 0:1], in0=f_lnm[:, 0:1],
                                scalar1=1.0 / c.D, scalar2=None, op0=OP.mult)
                e.tensor_scalar(out=x2w[:, :], in0=r, scalar1=f_lnm[:, 0:1],
                                scalar2=None, op0=OP.subtract)
            p.wait("scalar", sv)

            @p.on("scalar")
            def _(e, _t1=t1):
                e.activation(out=x1res[:, (_t1 % 2) * c.D:(_t1 % 2) * c.D + c.D],
                             in_=x2w[:, :],
                             func=AF.Square, accum_out=f_lnm[:, 1:2]).then_inc(sa.h, 1)
            sa.inc()
            p.wait("vector", sa)

            @p.on("vector")
            def _(e):
                e.tensor_scalar(out=f_lnm[:, 1:2], in0=f_lnm[:, 1:2],
                                scalar1=1.0 / c.D, scalar2=EPS_LN,
                                op0=OP.mult, op1=OP.add)
            p.wait("scalar", sv)

            @p.on("scalar")
            def _(e):
                e.activation(out=f_lnm[:, 2:3], in_=f_lnm[:, 1:2],
                             func=AF.Sqrt).then_inc(sa.h, 1)
            sa.inc()
            p.wait("vector", sa)

            @p.on("vector")
            def _(e):
                e.reciprocal(out=f_lnm[:, 3:4], in_=f_lnm[:, 2:3])
                e.tensor_scalar(out=x2w[:, :], in0=x2w[:, :], scalar1=f_lnm[:, 3:4],
                                scalar2=None, op0=OP.mult)
            p.wait("sync", sv)

            @p.on("sync")
            def _(e, _t1=t1, _t0=t0):
                e.dma_start(out=io["out"][_t0 + _t1 * 128: _t0 + (_t1 + 1) * 128, :],
                            in_=x2w[:, :]).then_inc(so.h, 16)
            so.inc(16)

    barrier(p, [S["so"], S["f0"], S["f1"], S["wd0"], S["wd1"], S["gu0"],
                S["sxq"], S["skv"], S["sj0"], S["sj1"], S["ssp"],
                sv, sa, st, sp], ["sync"])
    p.play(nc)
    _ES.close()


_NC_CACHE = {}


def _get_nc(full):
    if full not in _NC_CACHE:
        _NC_CACHE[full] = build_nc(CFG(full))
    return _NC_CACHE[full]


def kernel(L0_attn_hidden_states, L0_attn_wq, L0_attn_bq, L0_attn_wk,
           L0_attn_bk, L0_attn_wv, L0_attn_bv, L0_attn_wo, L0_attn_bo,
           L0_attn_wmap_q, L0_attn_wmap_k, L0_ffn_w_gate, L0_ffn_w_up,
           L0_ffn_w_down):
    x = np.ascontiguousarray(np.asarray(L0_attn_hidden_states, dtype=np.float32))
    B, Sq, D = x.shape
    c = CFG(full=True)
    assert (B, Sq, D) == (4, 4096, 2048)
    wq = np.ascontiguousarray(np.asarray(L0_attn_wq, dtype=np.float32))
    wk = np.ascontiguousarray(np.asarray(L0_attn_wk, dtype=np.float32))
    wv = np.ascontiguousarray(np.asarray(L0_attn_wv, dtype=np.float32))
    wo = np.ascontiguousarray(np.asarray(L0_attn_wo, dtype=np.float32))
    wg = np.ascontiguousarray(np.asarray(L0_ffn_w_gate, dtype=np.float32))
    wu = np.ascontiguousarray(np.asarray(L0_ffn_w_up, dtype=np.float32))
    wd = np.ascontiguousarray(np.asarray(L0_ffn_w_down, dtype=np.float32))
    biases = np.stack([np.asarray(L0_attn_bq, dtype=np.float32),
                       np.asarray(L0_attn_bk, dtype=np.float32),
                       np.asarray(L0_attn_bv, dtype=np.float32),
                       np.asarray(L0_attn_bo, dtype=np.float32)], axis=0)
    wmaps = np.concatenate([np.asarray(L0_attn_wmap_q, dtype=np.float32),
                            np.asarray(L0_attn_wmap_k, dtype=np.float32)], axis=0)
    idf = np.eye(128, dtype=np.float32)
    idb = np.eye(128, dtype=np.float32).astype(ml_dtypes.bfloat16)

    nc = _get_nc(True)
    in_maps = []
    for core in range(c.NC):
        b, half = core // 2, core % 2
        in_maps.append({
            "x_own": np.ascontiguousarray(x[b, half * c.T:(half + 1) * c.T, :]),
            "wq_s": np.ascontiguousarray(wq[core * c.WSH:(core + 1) * c.WSH, :]),
            "wk_s": np.ascontiguousarray(wk[core * c.WSH:(core + 1) * c.WSH, :]),
            "wv_s": np.ascontiguousarray(wv[core * c.WSH:(core + 1) * c.WSH, :]),
            "wo_s": np.ascontiguousarray(wo[core * c.WSH:(core + 1) * c.WSH, :]),
            "wg_s": np.ascontiguousarray(wg[core * c.FSH:(core + 1) * c.FSH, :]),
            "wu_s": np.ascontiguousarray(wu[core * c.FSH:(core + 1) * c.FSH, :]),
            "wd_s": np.ascontiguousarray(wd[core * c.DSH:(core + 1) * c.DSH, :]),
            "biases": biases, "wmaps": wmaps, "idf": idf, "idb": idb,
        })
    res = run_bass_kernel_spmd(nc, in_maps, core_ids=list(range(c.NC)))
    out = np.empty((B, Sq, D), dtype=np.float32)
    for core in range(c.NC):
        b, half = core // 2, core % 2
        out[b, half * c.T:(half + 1) * c.T, :] = res.results[core]["out"]
    return out


# version 31 (final)
# speedup vs baseline: 1.0906x; 1.0362x over previous
import contextlib
import numpy as np
import ml_dtypes
import concourse.bass as bass
import concourse.mybir as mybir
from concourse.bass_utils import run_bass_kernel_spmd

F32 = mybir.dt.float32
BF16 = mybir.dt.bfloat16
I32 = mybir.dt.int32
U16 = mybir.dt.uint16
FP8 = mybir.dt.float8e4
OP = mybir.AluOpType
AF = mybir.ActivationFunctionType
AX = mybir.AxisListType.X

MAGIC_E2M1 = float(np.float32(1.5 * 2 ** 22))
ACT_CLIP = float(np.float32(15.0 / 128.0))
EPS_ATT = 1e-6
EPS_LN = 1e-5
ENGS = ("sync", "vector", "scalar", "tensor", "gpsimd")
USE_2Q_A = False
USE_2Q_B = False
USE_2Q_F = False
USE_2Q_FP = False


class CFG:
    def __init__(self, full=True):
        if full:
            self.NC, self.T, self.D, self.I, self.H = 8, 2048, 2048, 8192, 16
            self.TBLK, self.SB = 512, 256
        else:
            self.NC, self.T, self.D, self.I, self.H = 2, 512, 512, 1024, 4
            self.TBLK, self.SB = 256, 256
        self.DH, self.M, self.QB = 128, 64, 32
        self.HP = self.H // 2
        self.nD = self.D // 128
        self.nB = self.T // self.TBLK
        self.nTT = self.TBLK // 128
        self.nSB = self.T // self.SB
        self.nST = self.SB // 128
        self.nI = self.I // 128
        self.OSW = min(512, self.I, self.I // self.NC)
        self.nOS = self.I // self.OSW
        self.WSH = self.D // self.NC
        self.FSH = self.I // self.NC
        self.DSH = self.D // self.NC
        assert self.H * self.DH == self.D
        self.GF_OFF, off = {}, 0
        for nm in ("qT", "kT", "vT", "oT"):
            self.GF_OFF[nm] = off
            off += self.D * self.WSH
        self.GFN = off
        self.GB_OFF, off = {}, 0
        for nm, sz in (("gq", self.D * self.FSH), ("uq", self.D * self.FSH),
                       ("dq", self.I * self.DSH)):
            self.GB_OFF[nm] = off
            off += sz
        self.GBN = off


class Sem:
    def __init__(self, h):
        self.h, self.n = h, 0

    def inc(self, k=1):
        self.n += k
        return self.n


class _CountInst:
    def then_inc(self, *a, **k):
        return self


class _CountMock:

    def __init__(self):
        self.count = 0

    def wait_ge(self, *a, **k):
        return None

    def __getattr__(self, name):
        def f(*a, **k):
            self.count += 1
            return _CountInst()
        return f


class _VecProxy:

    _PASS = ("wait_ge",)

    def __init__(self, e, semh, n0):
        object.__setattr__(self, "_e", e)
        object.__setattr__(self, "_semh", semh)
        object.__setattr__(self, "_n", n0)

    def __getattr__(self, name):
        a = getattr(self._e, name)
        if name in self._PASS or not callable(a):
            return a

        def f(*args, **kw):
            if self._n > 0:
                self._e.wait_ge(self._semh, self._n)
            inst = a(*args, **kw)
            inst.then_inc(self._semh, 1)
            object.__setattr__(self, "_n", self._n + 1)
            return inst
        return f


class Prog:

    def __init__(self, S):
        self.q = {e: [] for e in ENGS}
        self.sv = S["sv"]
        self.v0 = S["sv"].n

    def on(self, eng):
        def deco(fn):
            self.q[eng].append(fn)
            if eng == "vector":
                m = _CountMock()
                fn(m)
                self.sv.inc(m.count)
        return deco

    def wait(self, eng, sem):
        v = sem.n
        if v > 0:
            self.q[eng].append(lambda e, _s=sem.h, _v=v: e.wait_ge(_s, _v))

    def wait_val(self, eng, sem, v):
        if v > 0:
            self.q[eng].append(lambda e, _s=sem.h, _v=v: e.wait_ge(_s, _v))

    def play(self, nc):
        with nc.Block() as blk:
            for eng in ENGS:
                fns = self.q[eng]
                if not fns:
                    continue
                deco = getattr(blk, eng)
                if eng == "vector":
                    def body(e, _fns=fns, _n0=self.v0):
                        px = _VecProxy(e, self.sv.h, _n0)
                        for f in _fns:
                            f(px)
                else:
                    def body(e, _fns=fns):
                        for f in _fns:
                            f(e)
                deco(body)


def barrier(p, sems, engs=ENGS):
    for eng in engs:
        for s in sems:
            p.wait(eng, s)


SEM_NAMES = ["sg", "sv", "sa", "st", "sp", "scc",
             "sco", "sl", "sl1", "stf", "stb",
             "w0", "w1", "x0", "x1", "r0", "r1", "f0", "f1",
             "gu0", "gu1", "wd0", "wd1", "sxq", "skv", "ssp", "so",
             "si0", "si1", "sj0", "sj1"]


def build_nc(c):
    nc = bass.Bass("TRN2", target_bir_lowering=False)
    io = {}
    io["x_own"] = nc.dram_tensor("x_own", [c.T, c.D], F32, kind="ExternalInput").ap()
    for nm in ("wq_s", "wk_s", "wv_s", "wo_s"):
        io[nm] = nc.dram_tensor(nm, [c.WSH, c.D], F32, kind="ExternalInput").ap()
    io["wg_s"] = nc.dram_tensor("wg_s", [c.FSH, c.D], F32, kind="ExternalInput").ap()
    io["wu_s"] = nc.dram_tensor("wu_s", [c.FSH, c.D], F32, kind="ExternalInput").ap()
    io["wd_s"] = nc.dram_tensor("wd_s", [c.DSH, c.I], F32, kind="ExternalInput").ap()
    io["biases"] = nc.dram_tensor("biases", [4, c.D], F32, kind="ExternalInput").ap()
    io["wmaps"] = nc.dram_tensor("wmaps", [2 * c.M, c.DH], F32, kind="ExternalInput").ap()
    io["idf"] = nc.dram_tensor("idf", [128, 128], F32, kind="ExternalInput").ap()
    io["idb"] = nc.dram_tensor("idb", [128, 128], BF16, kind="ExternalInput").ap()
    io["out"] = nc.dram_tensor("out", [c.T, c.D], F32, kind="ExternalOutput").ap()

    io["gin_f"] = nc.dram_tensor("gin_f", [c.GFN], F32)
    _asp = dict(addr_space="Shared") if c.NC > 4 else {}
    io["gout_f"] = nc.dram_tensor("gout_f", [c.NC * c.GFN], F32, **_asp)
    io["gin_b"] = nc.dram_tensor("gin_b", [c.GBN], BF16)
    io["gout_b"] = nc.dram_tensor("gout_b", [c.NC * c.GBN], BF16, **_asp)
    nKB = (c.HP + 1) // 2
    KVW = 512 * nKB + 16
    io["kv_bin"] = nc.dram_tensor("kv_bin", [128, KVW], F32)
    io["kv_bout"] = nc.dram_tensor("kv_bout", [128, KVW], F32)
    io["x1_sp"] = nc.dram_tensor("x1_sp", [c.T, c.D], F32)
    io["x1q_sp"] = nc.dram_tensor("x1q_sp", [c.T, c.D], BF16)
    io["iq_sp"] = nc.dram_tensor("iq_sp", [c.I, c.T], BF16)

    with contextlib.ExitStack() as es:
        S = {nm: Sem(es.enter_context(nc.semaphore(nm))) for nm in SEM_NAMES}
        _build(nc, c, io, S)
    return nc


def _build(nc, c, io, S):
    sg, sv, sa, st, sp, scc = (S[k] for k in ("sg", "sv", "sa", "st", "sp", "scc"))
    nKB = (c.HP + 1) // 2
    KVW = 512 * nKB + 16

    pers = [
        nc.sbuf_tensor("idf_t", [128, 128], F32),
        nc.sbuf_tensor("idb_t", [128, 128], BF16),
        nc.sbuf_tensor("bias_t", [128, 4 * c.nD], F32),
        nc.sbuf_tensor("wmT", [128, 128], F32),
        nc.sbuf_tensor("ones_f", [128, 8], F32),
        nc.sbuf_tensor("kvacc", [128, KVW], F32),
    ]
    hp_ = [t.__enter__() for t in pers]
    idf_t, idb_t, bias_t, wmT, ones_f, kvacc = hp_

    def gfw(nm, oc):
        rk, r = oc // (c.WSH // 128), oc % (c.WSH // 128)
        base = rk * c.GFN + c.GF_OFF[nm] + r * c.D * 128
        return io["gout_f"][base:base + c.D * 128].rearrange(
            "(p dd o) -> p dd o", p=128, o=128)

    _phase_w(nc, c, io, S, idf_t, idb_t, bias_t, wmT, ones_f, kvacc)

    with (
        nc.sbuf_tensor("QM", [128, c.HP * c.T], F32) as QM,
    ):
        _phase_a(nc, c, io, S, gfw, idf_t, bias_t, wmT, ones_f, kvacc, QM, nKB)

        p = Prog(S)
        skv = S["skv"]
        barrier(p, [sv], ["sync"])

        @p.on("sync")
        def _(e):
            e.dma_start(out=io["kv_bin"][:, :], in_=kvacc[:, :]).then_inc(skv.h, 16)
        skv.inc(16)
        p.wait("gpsimd", skv)

        @p.on("gpsimd")
        def _(e):
            e.collective_compute(
                "AllReduce", OP.add,
                replica_groups=[[2 * i, 2 * i + 1] for i in range(c.NC // 2)],
                ins=[io["kv_bin"].ap().opt()], outs=[io["kv_bout"].ap().opt()],
            ).then_inc(scc.h, 1)
        scc.inc(1)
        p.wait("gpsimd", scc)

        @p.on("gpsimd")
        def _(e):
            e.dma_start(out=kvacc[:, :], in_=io["kv_bout"][:, :]).then_inc(sg.h, 16)
        sg.inc(16)
        p.play(nc)

        _phase_b(nc, c, io, S, gfw, idf_t, idb_t, bias_t, kvacc, QM, nKB)

    _phase_f(nc, c, io, S, idb_t)

    for t in reversed(pers):
        t.__exit__(None, None, None)


def _act_quant(p, c, x_ap, W, tl, out_bf):
    nb = W // 32

    @p.on("vector")
    def _(v):
        x3 = x_ap.rearrange("p (b k) -> p b k", k=32)
        v.tensor_reduce(out=tl["amax"][:, 0:nb].bitcast(F32), in_=x3, axis=AX,
                        op=OP.max, apply_absolute_value=True)
        v.tensor_scalar(out=tl["sc8"][:, 0:nb], in0=tl["amax"][:, 0:nb],
                        scalar1=23, scalar2=0xFF,
                        op0=OP.logical_shift_right, op1=OP.bitwise_and)
        v.tensor_scalar(out=tl["inv"][:, 0:nb], in0=tl["sc8"][:, 0:nb],
                        scalar1=-1.0, scalar2=250.0, op0=OP.mult, op1=OP.add)
        v.tensor_scalar(out=tl["inv"][:, 0:nb], in0=tl["inv"][:, 0:nb],
                        scalar1=23, scalar2=None, op0=OP.logical_shift_left)
        v.tensor_scalar(out=tl["m64"][:, 0:nb], in0=tl["sc8"][:, 0:nb],
                        scalar1=4.0, scalar2=None, op0=OP.add)
        v.tensor_scalar(out=tl["m64"][:, 0:nb], in0=tl["m64"][:, 0:nb],
                        scalar1=23, scalar2=None, op0=OP.logical_shift_left)
        v.tensor_tensor(out=tl["z"][:, 0:W].rearrange("p (b k) -> p b k", k=32),
                        in0=x3,
                        in1=tl["inv"][:, 0:nb].bitcast(F32).unsqueeze(2)
                        .broadcast_to([128, nb, 32]), op=OP.mult)
        v.tensor_scalar(out=tl["z"][:, 0:W], in0=tl["z"][:, 0:W],
                        scalar1=ACT_CLIP, scalar2=-ACT_CLIP,
                        op0=OP.min, op1=OP.max)
        v.tensor_copy(tl["q8"][:, 0:W], tl["z"][:, 0:W])
        v.tensor_tensor(out=out_bf[:, 0:W].rearrange("p (b k) -> p b k", k=32),
                        in0=tl["q8"][:, 0:W].rearrange("p (b k) -> p b k", k=32),
                        in1=tl["m64"][:, 0:nb].bitcast(F32).unsqueeze(2)
                        .broadcast_to([128, nb, 32]), op=OP.mult)


def _wgt_quant(p, c, w_ap, W, tl, out_bf):
    nb = W // 32

    @p.on("vector")
    def _(v):
        w3 = w_ap.rearrange("p (b k) -> p b k", k=32)
        v.tensor_reduce(out=tl["amax"][:, 0:nb].bitcast(F32), in_=w3, axis=AX,
                        op=OP.max, apply_absolute_value=True)
        v.tensor_scalar(out=tl["sc8"][:, 0:nb], in0=tl["amax"][:, 0:nb],
                        scalar1=23, scalar2=0xFF,
                        op0=OP.logical_shift_right, op1=OP.bitwise_and)
        v.tensor_scalar(out=tl["scl"][:, 0:nb], in0=tl["sc8"][:, 0:nb],
                        scalar1=-2.0, scalar2=None, op0=OP.add)
        v.tensor_scalar(out=tl["scl"][:, 0:nb], in0=tl["scl"][:, 0:nb],
                        scalar1=23, scalar2=None, op0=OP.logical_shift_left)
        v.tensor_scalar(out=tl["clp"][:, 0:nb], in0=tl["sc8"][:, 0:nb],
                        scalar1=23, scalar2=0x400000,
                        op0=OP.logical_shift_left, op1=OP.bitwise_or)
        v.tensor_scalar(out=tl["nclp"][:, 0:nb], in0=tl["clp"][:, 0:nb].bitcast(F32),
                        scalar1=-1.0, scalar2=None, op0=OP.mult)
        v.tensor_scalar(out=tl["ti"][:, 0:W], in0=w_ap.bitcast(I32),
                        scalar1=0x7F800000, scalar2=None, op0=OP.bitwise_and)
        v.tensor_tensor(out=tl["p2"][:, 0:W].rearrange("p (b k) -> p b k", k=32),
                        in0=tl["ti"][:, 0:W].bitcast(F32)
                        .rearrange("p (b k) -> p b k", k=32),
                        in1=tl["scl"][:, 0:nb].bitcast(F32).unsqueeze(2)
                        .broadcast_to([128, nb, 32]), op=OP.max)
        v.tensor_scalar(out=tl["bt"][:, 0:W], in0=tl["p2"][:, 0:W],
                        scalar1=MAGIC_E2M1, scalar2=None, op0=OP.mult)
        v.tensor_tensor(out=tl["p2"][:, 0:W], in0=w_ap, in1=tl["bt"][:, 0:W],
                        op=OP.add)
        v.tensor_tensor(out=tl["p2"][:, 0:W], in0=tl["p2"][:, 0:W],
                        in1=tl["bt"][:, 0:W], op=OP.subtract)
        v.tensor_tensor(out=tl["p2"][:, 0:W].rearrange("p (b k) -> p b k", k=32),
                        in0=tl["p2"][:, 0:W].rearrange("p (b k) -> p b k", k=32),
                        in1=tl["clp"][:, 0:nb].bitcast(F32).unsqueeze(2)
                        .broadcast_to([128, nb, 32]), op=OP.min)
        v.tensor_tensor(out=out_bf[:, 0:W].rearrange("p (b k) -> p b k", k=32),
                        in0=tl["p2"][:, 0:W].rearrange("p (b k) -> p b k", k=32),
                        in1=tl["nclp"][:, 0:nb].unsqueeze(2)
                        .broadcast_to([128, nb, 32]), op=OP.max)


def _phase_w(nc, c, io, S, idf_t, idb_t, bias_t, wmT, ones_f, kvacc):
    sv, sa, st, sp, scc = (S[k] for k in ("sv", "sa", "st", "sp", "scc"))
    sco, sl, sl1 = S["sco"], S["sl"], S["sl1"]
    stf, stb = S["stf"], S["stb"]
    _ES = contextlib.ExitStack()
    WJ = min(4096, c.I)
    wstage = _ES.enter_context(nc.sbuf_tensor("wstage", [128, 2 * WJ], F32))
    wbf = _ES.enter_context(nc.sbuf_tensor("wbf", [128, 2 * WJ], BF16))
    wbf2 = _ES.enter_context(nc.sbuf_tensor("wbf2", [128, 2 * WJ], BF16))
    wTf = _ES.enter_context(nc.sbuf_tensor("wTf", [128, c.D], F32))
    wTb = _ES.enter_context(nc.sbuf_tensor("wTb", [128, WJ], BF16))
    q_amax = _ES.enter_context(nc.sbuf_tensor("q_amax", [128, WJ // 32], I32))
    q_sc8 = _ES.enter_context(nc.sbuf_tensor("q_sc8", [128, WJ // 32], I32))
    q_scl = _ES.enter_context(nc.sbuf_tensor("q_scl", [128, WJ // 32], I32))
    q_clp = _ES.enter_context(nc.sbuf_tensor("q_clp", [128, WJ // 32], I32))
    q_nclp = _ES.enter_context(nc.sbuf_tensor("q_nclp", [128, WJ // 32], F32))
    q_ti = _ES.enter_context(nc.sbuf_tensor("q_ti", [128, WJ], I32))
    q_p2 = _ES.enter_context(nc.sbuf_tensor("q_p2", [128, WJ], F32))
    q_bt = _ES.enter_context(nc.sbuf_tensor("q_bt", [128, WJ], F32))
    wm_st = _ES.enter_context(nc.sbuf_tensor("wm_st", [128, 128], F32))
    psw0 = _ES.enter_context(nc.psum_tensor("psw0", [128, 512], F32))
    psw1 = _ES.enter_context(nc.psum_tensor("psw1", [128, 1024], BF16))
    tl = dict(amax=q_amax, sc8=q_sc8, scl=q_scl, clp=q_clp, nclp=q_nclp,
              ti=q_ti, p2=q_p2, bt=q_bt)
    p = Prog(S)

    @p.on("sync")
    def _(e):
        e.dma_start(out=idf_t[:, :], in_=io["idf"][:, :]).then_inc(sco.h, 16)
        e.dma_start(out=idb_t[:, :], in_=io["idb"][:, :]).then_inc(sco.h, 16)
        for m in range(4):
            e.dma_start(out=bias_t[:, m * c.nD:(m + 1) * c.nD],
                        in_=io["biases"][m, :].rearrange("(oc p) -> p oc", p=128),
                        allow_slow_non_contiguous=True).then_inc(sco.h, 16)
        e.dma_start(out=wm_st[:, :], in_=io["wmaps"][:, :]).then_inc(sco.h, 16)
    sco.inc(16 * 7)
    p.wait("scalar", sco)

    @p.on("scalar")
    def _(e):
        e.activation(out=ones_f[:, :], in_=wm_st[:, 0:8], func=AF.Copy,
                     scale=0.0, bias=1.0).then_inc(sa.h, 1)
    sa.inc()
    p.wait("vector", sco)

    @p.on("vector")
    def _(e):
        e.memset(kvacc[:, :], 0.0)
    p.wait("tensor", sco)

    @p.on("tensor")
    def _(e):
        e.matmul(psw0[:, 0:128], wm_st[:, :], idf_t[:, :], is_transpose=True,
                 start=True, stop=True).then_inc(st.h, 1)
    st.inc()
    p.wait("scalar", st)

    @p.on("scalar")
    def _(e):
        e.activation(out=wmT[:, :], in_=psw0[:, 0:128],
                     func=AF.Copy).then_inc(sa.h, 1)
    sa.inc()

    p.wait("tensor", sa)

    jobs = []
    for nm, srcnm in (("qT", "wq_s"), ("kT", "wk_s"), ("vT", "wv_s"),
                      ("oT", "wo_s")):
        for r in range(c.WSH // 128):
            jobs.append(("attn", nm, srcnm, r, 0, c.D))
    for nm, srcnm in (("gq", "wg_s"), ("uq", "wu_s")):
        for r in range(c.FSH // 128):
            jobs.append(("ffn", nm, srcnm, r, 0, c.D))
    for r in range(c.DSH // 128):
        for c0 in range(0, c.I, WJ):
            jobs.append(("ffn", "dq", "wd_s", r, c0, WJ))

    wst_free = [(0, 0), (0, 0)]
    wbf_free = [0, 0]
    wbf2_free = [0, 0]
    wTf_free = 0
    wTb_free = 0
    attn_done = False
    for jidx, (kind, nm, srcnm, r, c0, W) in enumerate(jobs):
        nT = W // 128
        slot = jidx % 2
        sb0 = slot * WJ
        slsem = sl if slot == 0 else sl1
        st_t, sv_t = wst_free[slot]
        p.wait_val("sync", st, st_t)
        p.wait_val("sync", sv, sv_t)

        @p.on("sync")
        def _(e, _s=srcnm, _r=r, _c0=c0, _W=W, _b=sb0, _sem=slsem):
            e.dma_start(out=wstage[:, _b:_b + _W],
                        in_=io[_s][_r * 128:(_r + 1) * 128, _c0:_c0 + _W]
                        ).then_inc(_sem.h, 16)
        slsem.inc(16)
        p.wait("vector", slsem)
        p.wait("tensor", slsem)

        if kind == "attn":
            for j in range(nT):
                p.wait("tensor", sv)

                @p.on("tensor")
                def _(e, _j=j, _b=sb0):
                    e.matmul(psw0[:, 0:128],
                             wstage[:, _b + _j * 128:_b + (_j + 1) * 128],
                             idf_t[:, :], is_transpose=True,
                             start=True, stop=True).then_inc(st.h, 1)
                st.inc()
                if j == 0:
                    p.wait_val("vector", stf, wTf_free)
                p.wait("vector", st)

                @p.on("vector")
                def _(e, _j=j):
                    e.tensor_copy(wTf[:, _j * 128:(_j + 1) * 128], psw0[:, 0:128])
            wst_free[slot] = (st.n, 0)
            p.wait("sync", sv)

            @p.on("sync")
            def _(e, _nm=nm, _r=r):
                base = c.GF_OFF[_nm] + _r * c.D * 128
                e.dma_start(out=io["gin_f"][base:base + c.D * 128]
                            .rearrange("(p x) -> p x", p=128),
                            in_=wTf[:, :]).then_inc(stf.h, 16)
            stf.inc(16)
            wTf_free = stf.n
        else:
            if not attn_done:
                attn_done = True
                p.wait("gpsimd", stf)

                @p.on("gpsimd")
                def _(e):
                    e.collective_compute(
                        "AllGather", OP.bypass,
                        replica_groups=[list(range(c.NC))],
                        ins=[io["gin_f"].ap().opt()],
                        outs=[io["gout_f"].ap().opt()],
                    ).then_inc(scc.h, 1)
                scc.inc(1)
            p.wait_val("vector", sp, wbf_free[slot])
            _wgt_quant(p, c, wstage[:, sb0:sb0 + W], W, tl, wbf[:, sb0:sb0 + W])
            wst_free[slot] = (0, sv.n)
            p.wait("gpsimd", sv)
            p.wait_val("gpsimd", st, wbf2_free[slot])

            @p.on("gpsimd")
            def _(e, _W=W, _b=sb0):
                e.tensor_copy(wbf2[:, _b:_b + _W].bitcast(U16),
                              wbf[:, _b:_b + _W].bitcast(U16)).then_inc(sp.h, 1)
            sp.inc()
            wbf_free[slot] = sp.n
            p.wait("tensor", sp)
            for j in range(nT):
                p.wait("tensor", sa)
                if j == 0:
                    p.wait_val("scalar", stb, wTb_free)

                @p.on("tensor")
                def _(e, _j=j, _b=sb0):
                    e.matmul(psw1[:, 0:128],
                             wbf2[:, _b + _j * 128:_b + (_j + 1) * 128],
                             idb_t[:, :], is_transpose=True,
                             start=True, stop=True).then_inc(st.h, 1)
                st.inc()
                p.wait("scalar", st)

                @p.on("scalar")
                def _(e, _j=j):
                    e.activation(out=wTb[:, _j * 128:(_j + 1) * 128],
                                 in_=psw1[:, 0:128], func=AF.Copy).then_inc(sa.h, 1)
                sa.inc()
            wbf2_free[slot] = st.n
            p.wait("sync", sa)

            @p.on("sync")
            def _(e, _nm=nm, _r=r, _c0=c0, _W=W):
                width = c.FSH if _nm in ("gq", "uq") else c.DSH
                rows = c.D if _nm in ("gq", "uq") else c.I
                base = c.GB_OFF[_nm]
                piece = io["gin_b"][base:base + rows * width].rearrange(
                    "(dd pp w) -> pp dd w", pp=128, w=width)
                pp = piece[:, _c0 // 128:(_c0 + _W) // 128,
                           _r * 128:(_r + 1) * 128]
                e.dma_start(out=pp,
                            in_=wTb[:, 0:_W].rearrange("p (dd o) -> p dd o", o=128)
                            ).then_inc(stb.h, 16)
            stb.inc(16)
            wTb_free = stb.n

    p.wait("gpsimd", stb)

    @p.on("gpsimd")
    def _(e):
        e.collective_compute(
            "AllGather", OP.bypass, replica_groups=[list(range(c.NC))],
            ins=[io["gin_b"].ap().opt()], outs=[io["gout_b"].ap().opt()],
        ).then_inc(scc.h, 1)
    scc.inc(1)
    p.play(nc)
    _ES.close()


def _phase_a(nc, c, io, S, gfw, idf_t, bias_t, wmT, ones_f, kvacc, QM, nKB):
    sv, sa, st, sp, scc = (S[k] for k in ("sv", "sa", "st", "sp", "scc"))
    TB, nTT, nD, H, HP = c.TBLK, c.nTT, c.nD, c.H, c.HP
    _ES = contextlib.ExitStack()
    xst = _ES.enter_context(nc.sbuf_tensor("xst", [128, 2 * c.D], F32))
    xf = _ES.enter_context(nc.sbuf_tensor("xf", [128, nD * TB], F32))
    pf = _ES.enter_context(nc.sbuf_tensor("pf", [128, nD * TB], F32))
    wc = _ES.enter_context(nc.sbuf_tensor("wc", [128, 2 * nD * 128], F32))
    kmB = _ES.enter_context(nc.sbuf_tensor("kmB", [128, nTT * H * 64], F32))
    vt = _ES.enter_context(nc.sbuf_tensor("vt", [128, 2 * c.D], F32))
    etmp1 = _ES.enter_context(nc.sbuf_tensor("etmp1", [128, 1024], F32))
    etmp2 = etmp1
    psA = _ES.enter_context(nc.psum_tensor("psA", [128, 512], F32))
    psB = _ES.enter_context(nc.psum_tensor("psB", [128, 512], F32))
    psTr = _ES.enter_context(nc.psum_tensor("psTr", [128, 512], F32))
    psKS = _ES.enter_context(nc.psum_tensor("psKS", [128, 512], F32))
    psKV = [_ES.enter_context(nc.psum_tensor("psKV%d" % i, [128, 512], F32))
            for i in range(nKB)]

    p = Prog(S)
    barrier(p, [S["sco"], sv, sa, st, sp], ENGS)

    ast = dict(vt_free=[0, 0], x_t={}, xst_free={0: 0, 1: 0})
    for tb in range(c.nB):
        _attn_block(p, c, io, S, tb, gfw, idf_t, bias_t, wmT, ones_f, kvacc,
                    QM, ast, xst, xf, pf, wc, kmB, vt, etmp1, etmp2,
                    psA, psB, psTr, psKS, psKV, nKB)

    barrier(p, [S["w0"], S["w1"], S["x0"], S["x1"], sv, sa, st, sp], ENGS)
    p.play(nc)
    _ES.close()


def _a1_loads(p, c, io, S, tb, ast, xst, tts):
    st = S["st"]
    t0 = tb * c.TBLK
    for tt in tts:
        slot = tt % 2
        sx = S["x0"] if slot == 0 else S["x1"]
        p.wait_val("sync", st, ast["xst_free"][slot])

        @p.on("sync")
        def _(e, _tt=tt, _sx=sx, _slot=slot, _t0=t0):
            e.dma_start(out=xst[:, _slot * c.D:(_slot + 1) * c.D],
                        in_=io["x_own"][_t0 + _tt * 128: _t0 + (_tt + 1) * 128, :]
                        ).then_inc(_sx.h, 16)
        sx.inc(16)
        ast["x_t"][(tb, tt)] = sx.n


def _attn_block(p, c, io, S, tb, gfw, idf_t, bias_t, wmT, ones_f, kvacc, QM,
                ast, xst, xf, pf, wc, kmB, vt, etmp1, etmp2,
                psA, psB, psTr, psKS, psKV, nKB):
    sv, sa, st, sp, scc = (S[k] for k in ("sv", "sa", "st", "sp", "scc"))
    TB, nTT, nD, H, HP = c.TBLK, c.nTT, c.nD, c.H, c.HP
    t0 = tb * TB

    a1_free = {0: sa.n, 1: sa.n}
    for tt in range(nTT):
        if (tb, tt) not in ast["x_t"]:
            _a1_loads(p, c, io, S, tb, ast, xst, [tt])
        slot = tt % 2
        sx = S["x0"] if slot == 0 else S["x1"]
        p.wait_val("tensor", sx, ast["x_t"].pop((tb, tt)))
        p.wait("tensor", sv)
        for dd in range(nD):
            tb2 = dd % 2
            trb = psTr if tb2 == 0 else psKS
            p.wait_val("tensor", sa, a1_free[tb2])

            @p.on("tensor")
            def _(e, _dd=dd, _trb=trb, _slot=slot):
                e.matmul(_trb[:, 0:128],
                         xst[:, _slot * c.D + _dd * 128:
                             _slot * c.D + (_dd + 1) * 128],
                         idf_t[:, :], is_transpose=True,
                         start=True, stop=True).then_inc(st.h, 1)
            st.inc()
            p.wait("scalar", st)

            @p.on("scalar")
            def _(e, _tt=tt, _dd=dd, _trb=trb):
                col = _dd * TB + _tt * 128
                e.activation(out=xf[:, col:col + 128], in_=_trb[:, 0:128],
                             func=AF.Copy).then_inc(sa.h, 1)
            sa.inc()
            a1_free[tb2] = sa.n
        ast["xst_free"][slot] = st.n

    def fm_proj(w_nm, bidx, dest):
        bank_free = {0: (sa.n, sv.n), 1: (sa.n, sv.n)}
        wbuf_free = {0: st.n, 1: st.n}
        for oc in range(nD):
            wb = oc % 2
            sw = S["w0"] if wb == 0 else S["w1"]
            qeng = "sync" if (wb == 0 or not USE_2Q_A) else "scalar"
            p.wait_val(qeng, st, wbuf_free[wb])
            if tb == 0 and oc <= 1 and w_nm == "qT":
                p.wait_val(qeng, scc, 1)

            @p.on(qeng)
            def _(e, _nm=w_nm, _oc=oc, _wb=wb, _sw=sw):
                e.dma_start(out=wc[:, _wb * nD * 128:(_wb + 1) * nD * 128]
                            .rearrange("p (dd o) -> p dd o", o=128),
                            in_=gfw(_nm, _oc)).then_inc(_sw.h, 16)
            sw.inc(16)
            ps = psA if wb == 0 else psB
            p.wait("tensor", sw)
            sa_t, sv_t = bank_free[wb]
            p.wait_val("tensor", sa, sa_t)
            p.wait_val("tensor", sv, sv_t)

            @p.on("tensor")
            def _(e, _wb=wb, _ps=ps):
                for dd in range(nD):
                    w = wc[:, _wb * nD * 128 + dd * 128: _wb * nD * 128 + (dd + 1) * 128]
                    _li = e.matmul(_ps[:, 0:TB], w, xf[:, dd * TB:(dd + 1) * TB],
                                   start=(dd == 0), stop=(dd == nD - 1))
                _li.then_inc(st.h, 1)
            st.inc()
            wbuf_free[wb] = st.n
            p.wait("scalar", st)

            @p.on("scalar")
            def _(e, _oc=oc, _ps=ps, _bi=bidx):
                e.activation(out=dest[:, _oc * TB:(_oc + 1) * TB],
                             in_=_ps[:, 0:TB], func=AF.Identity,
                             bias=bias_t[:, _bi * c.nD + _oc: _bi * c.nD + _oc + 1]
                             ).then_inc(sa.h, 1)
            sa.inc()
            bank_free[wb] = (sa.n, 0)

    fm_proj("qT", 0, pf)
    a3_free = {0: (sv.n, sa.n), 1: (sv.n, sa.n)}
    for h in range(H):
        hp, par = h // 2, h % 2
        ps = psA if par == 0 else psB
        rows = slice(0, 64) if par == 0 else slice(64, 128)
        tpos = None if par == 0 else (0, 64)
        sv_t, sa_t = a3_free[par]
        p.wait_val("tensor", sv, sv_t)
        p.wait_val("tensor", sa, sa_t)

        @p.on("tensor")
        def _(e, _h=h, _ps=ps, _rows=rows, _tpos=tpos):
            q = pf[:, _h * TB:(_h + 1) * TB]
            e.matmul(_ps[_rows, 0:TB], wmT[:, 0:64], q, start=True, stop=True,
                     tile_position=_tpos).then_inc(st.h, 1)
        st.inc()
        p.wait("vector", st)

        eb = par * TB

        @p.on("vector")
        def _(e, _ps=ps, _rows=rows, _eb=eb):
            e.tensor_scalar(out=etmp1[_rows, _eb:_eb + TB], in0=_ps[_rows, 0:TB],
                            scalar1=0.0, scalar2=None, op0=OP.min)
        p.wait("scalar", sv)

        @p.on("scalar")
        def _(e, _rows=rows, _eb=eb):
            e.activation(out=etmp2[_rows, _eb:_eb + TB], in_=etmp1[_rows, _eb:_eb + TB],
                         func=AF.Exp).then_inc(sa.h, 1)
        sa.inc()
        p.wait("vector", sa)

        @p.on("vector")
        def _(e, _ps=ps, _rows=rows, _hp=hp, _eb=eb):
            e.scalar_tensor_tensor(out=QM[_rows, _hp * c.T + t0: _hp * c.T + t0 + TB],
                                   in0=_ps[_rows, 0:TB],
                                   scalar=0.0, in1=etmp2[_rows, _eb:_eb + TB],
                                   op0=OP.max, op1=OP.add)
        a3_free[par] = (sv.n, 0)

    fm_proj("kT", 1, pf)
    nhalf = max(H // 8, 1)
    nh = min(H, 8)
    a5_free = {0: (sv.n, sa.n), 1: (sv.n, sa.n)}
    for tt in range(nTT):
        for half in range(nhalf):
            par5 = (tt * nhalf + half) % 2
            ps = psA if par5 == 0 else psB
            eb = par5 * 512
            sv_t, sa_t = a5_free[par5]
            p.wait_val("tensor", sv, sv_t)
            p.wait_val("tensor", sa, sa_t)

            @p.on("tensor")
            def _(e, _tt=tt, _half=half, _ps=ps):
                for hh in range(nh):
                    h = _half * 8 + hh
                    k_ = pf[:, h * TB + _tt * 128: h * TB + (_tt + 1) * 128]
                    _li = e.matmul(_ps[:, hh * 64:(hh + 1) * 64], k_,
                                   wmT[:, 64:128], start=(hh == 0),
                                   stop=(hh == nh - 1), skip_group_check=True)
                _li.then_inc(st.h, 1)
            st.inc()
            W = nh * 64
            p.wait("vector", st)

            @p.on("vector")
            def _(e, _ps=ps, _eb=eb):
                e.tensor_scalar(out=etmp1[:, _eb:_eb + W], in0=_ps[:, 0:W],
                                scalar1=0.0, scalar2=None, op0=OP.min)
            p.wait("scalar", sv)

            @p.on("scalar")
            def _(e, _eb=eb):
                e.activation(out=etmp2[:, _eb:_eb + W], in_=etmp1[:, _eb:_eb + W],
                             func=AF.Exp).then_inc(sa.h, 1)
            sa.inc()
            p.wait("vector", sa)
            cb = tt * H * 64 + half * 8 * 64

            @p.on("vector")
            def _(e, _ps=ps, _cb=cb, _eb=eb):
                e.scalar_tensor_tensor(out=kmB[:, _cb:_cb + W], in0=_ps[:, 0:W],
                                       scalar=0.0, in1=etmp2[:, _eb:_eb + W],
                                       op0=OP.max, op1=OP.add)
            a5_free[par5] = (sv.n, 0)

    fm_proj("vT", 2, pf)

    if tb + 1 < c.nB:
        _a1_loads(p, c, io, S, tb + 1, ast, xst, list(range(min(2, c.nTT))))

    for tt in range(nTT):
        vs = tt % 2
        first_evac = True
        for dd in range(nD):
            p.wait("tensor", sa)

            @p.on("tensor")
            def _(e, _tt=tt, _dd=dd):
                e.matmul(psTr[:, 0:128],
                         pf[:, _dd * TB + _tt * 128: _dd * TB + (_tt + 1) * 128],
                         idf_t[:, :], is_transpose=True,
                         start=True, stop=True).then_inc(st.h, 1)
            st.inc()
            if first_evac:
                first_evac = False
                p.wait_val("scalar", st, ast["vt_free"][vs])
            p.wait("scalar", st)

            @p.on("scalar")
            def _(e, _dd=dd, _vs=vs):
                e.activation(out=vt[:, _vs * c.D + _dd * 128: _vs * c.D + (_dd + 1) * 128],
                             in_=psTr[:, 0:128], func=AF.Copy).then_inc(sa.h, 1)
            sa.inc()
        vt_ready = sa.n

        p.wait_val("tensor", sa, vt_ready)
        p.wait("tensor", sv)

        @p.on("tensor")
        def _(e, _tt=tt, _tb=tb, _vs=vs):
            for hp in range(HP):
                bank = psKV[hp // 2]
                cb0 = (hp % 2) * 256
                km = kmB[:, _tt * H * 64 + hp * 128: _tt * H * 64 + hp * 128 + 128]
                vA = vt[:, _vs * c.D + (2 * hp) * 128: _vs * c.D + (2 * hp + 1) * 128]
                vB = vt[:, _vs * c.D + (2 * hp + 1) * 128: _vs * c.D + (2 * hp + 2) * 128]
                first = _tt == 0 and hp % 2 == 0
                _li = e.matmul(bank[:, cb0:cb0 + 128], km, vA, start=first,
                         stop=False, skip_group_check=True)
                _li = e.matmul(bank[:, cb0 + 128:cb0 + 256], km, vB, start=False,
                         stop=False, skip_group_check=True)
                _li = e.matmul(psKS[:, hp:hp + 1], km, ones_f[:, 0:1],
                         start=(_tt == 0 and hp == 0), stop=False,
                         skip_group_check=True)
            _li.then_inc(st.h, 1)
        st.inc()
        ast["vt_free"][vs] = st.n

    p.wait("vector", st)

    @p.on("vector")
    def _(e):
        for b in range(nKB):
            cc = b * 512
            w = min(512, (HP - 2 * b) * 256)
            e.tensor_tensor(out=kvacc[:, cc:cc + w], in0=psKV[b][:, 0:w],
                            in1=kvacc[:, cc:cc + w], op=OP.add)
        cc = nKB * 512
        e.tensor_tensor(out=kvacc[:, cc:cc + HP], in0=psKS[:, 0:HP],
                        in1=kvacc[:, cc:cc + HP], op=OP.add)


def _phase_b(nc, c, io, S, gfw, idf_t, idb_t, bias_t, kvacc, QM, nKB):
    sv, sa, st, sp = (S[k] for k in ("sv", "sa", "st", "sp"))
    sg = S["sg"]
    TB, nTT, nD, H, HP = c.TBLK, c.nTT, c.nD, c.H, c.HP
    _ES = contextlib.ExitStack()
    kv2 = _ES.enter_context(nc.sbuf_tensor("kv2", [128, H * 128], F32))
    ks2 = _ES.enter_context(nc.sbuf_tensor("ks2", [128, 2 * HP], F32))
    rinvT = _ES.enter_context(nc.sbuf_tensor("rinvT", [128, c.nTT * H], F32))
    oat = _ES.enter_context(nc.sbuf_tensor("oat", [128, H * TB], F32))
    oa_f = _ES.enter_context(nc.sbuf_tensor("oa_f", [128, 1024], F32))
    resid = _ES.enter_context(nc.sbuf_tensor("resid", [128, nTT * c.D], F32))
    wc2 = _ES.enter_context(nc.sbuf_tensor("wc2", [128, 2 * nD * 128], F32))
    yf = _ES.enter_context(nc.sbuf_tensor("yf", [128, 2 * TB], F32))
    xres = _ES.enter_context(nc.sbuf_tensor("xres", [128, 256], F32))
    x1w = _ES.enter_context(nc.sbuf_tensor("x1w", [128, c.D], F32))
    lnm = _ES.enter_context(nc.sbuf_tensor("lnm", [128, 8], F32))
    x1q = _ES.enter_context(nc.sbuf_tensor("x1q", [128, c.D], BF16))
    x1q2 = _ES.enter_context(nc.sbuf_tensor("x1q2", [128, c.D], BF16))
    b_amax = _ES.enter_context(nc.sbuf_tensor("b_amax", [128, c.D // 32], I32))
    b_sc8 = _ES.enter_context(nc.sbuf_tensor("b_sc8", [128, c.D // 32], I32))
    b_inv = _ES.enter_context(nc.sbuf_tensor("b_inv", [128, c.D // 32], I32))
    b_m64 = _ES.enter_context(nc.sbuf_tensor("b_m64", [128, c.D // 32], I32))
    b_z = _ES.enter_context(nc.sbuf_tensor("b_z", [128, c.D], F32))
    b_q8 = _ES.enter_context(nc.sbuf_tensor("b_q8", [128, c.D], FP8))
    psA2 = _ES.enter_context(nc.psum_tensor("psA2", [128, 512], F32))
    psB2 = _ES.enter_context(nc.psum_tensor("psB2", [128, 512], F32))
    psN = _ES.enter_context(nc.psum_tensor("psN", [128, 512], F32))
    psT2 = _ES.enter_context(nc.psum_tensor("psT2", [128, 512], F32))
    psT2b = _ES.enter_context(nc.psum_tensor("psT2b", [128, 1024], BF16))
    tlB = dict(amax=b_amax, sc8=b_sc8, inv=b_inv, m64=b_m64, z=b_z, q8=b_q8)

    p = Prog(S)
    barrier(p, [sg], ENGS)

    @p.on("vector")
    def _(e):
        e.memset(kv2[:, :], 0.0)
        e.memset(ks2[:, :], 0.0)
        ks = nKB * 512
        for hp in range(HP):
            b, j = hp // 2, hp % 2
            cc = b * 512 + j * 256
            e.tensor_copy(kv2[0:64, (2 * hp) * 128:(2 * hp + 1) * 128],
                          kvacc[0:64, cc:cc + 128])
            e.tensor_copy(kv2[64:128, (2 * hp + 1) * 128:(2 * hp + 2) * 128],
                          kvacc[64:128, cc + 128:cc + 256])
            e.tensor_copy(ks2[0:64, 2 * hp:2 * hp + 1],
                          kvacc[0:64, ks + hp:ks + hp + 1])
            e.tensor_copy(ks2[64:128, 2 * hp + 1:2 * hp + 2],
                          kvacc[64:128, ks + hp:ks + hp + 1])

    bst = dict(oa_free={0: 0, 1: 0}, psN_free=S["sv"].n,
               psAB_free={0: [0, 0], 1: [0, 0]}, psT2_free=[0, 0],
               pend_ln1=[])
    for tb in range(c.nB):
        _attn_out_block(p, c, io, S, tb, gfw, idf_t, idb_t, bias_t, QM,
                        kv2, ks2, rinvT, oat, oa_f, resid, wc2, yf, xres,
                        x1w, lnm, x1q, x1q2, tlB, bst,
                        psA2, psB2, psN, psT2, psT2b)

    barrier(p, [S["w0"], S["w1"], S["r0"], S["r1"], S["ssp"], sg, sv, sa,
                st, sp], ENGS)
    p.play(nc)
    _ES.close()


def _attn_out_block(p, c, io, S, tb, gfw, idf_t, idb_t, bias_t, QM,
                    kv2, ks2, rinvT, oat, oa_f, resid, wc2, yf, xres,
                    x1w, lnm, x1q, x1q2, tlB, bst,
                    psA2, psB2, psN, psT2, psT2b):
    sv, sa, st, sp = (S[k] for k in ("sv", "sa", "st", "sp"))
    TB, nTT, nD, H, HP = c.TBLK, c.nTT, c.nD, c.H, c.HP
    t0 = tb * TB
    ssp = S["ssp"]

    pend_tr = None
    for tt in range(nTT):
        p.wait_val("tensor", sv, bst["psN_free"])

        @p.on("tensor")
        def _(e, _tt=tt):
            for hp in range(HP):
                cc = hp * c.T + t0 + _tt * 128
                _li = e.matmul(psN[:, 2 * hp:2 * hp + 2], QM[:, cc:cc + 128],
                               ks2[:, 2 * hp:2 * hp + 2], start=(hp == 0),
                               stop=(hp == HP - 1), skip_group_check=True)
            _li.then_inc(st.h, 1)
        st.inc()
        p.wait("vector", st)

        @p.on("vector")
        def _(e, _tt=tt):
            e.tensor_scalar(out=rinvT[:, _tt * H:(_tt + 1) * H], in0=psN[:, 0:H],
                            scalar1=EPS_ATT, scalar2=None, op0=OP.add)
            e.reciprocal(out=rinvT[:, _tt * H:(_tt + 1) * H],
                         in_=rinvT[:, _tt * H:(_tt + 1) * H])
        bst["psN_free"] = sv.n

        GW = min(4, H)
        ps_free = bst["psAB_free"]
        oa_free = bst["oa_free"]
        for grp in range(H // GW):
            par = grp % 2
            ob = par * 512
            ps = psA2 if par == 0 else psB2
            sv_t, sa_t = ps_free[par]
            p.wait_val("tensor", sv, sv_t)
            p.wait_val("tensor", sa, sa_t)

            @p.on("tensor")
            def _(e, _g=grp, _ps=ps, _tt=tt):
                for j in range(GW):
                    h = _g * GW + j
                    cc = (h // 2) * c.T + t0 + _tt * 128
                    _li = e.matmul(_ps[:, j * 128:(j + 1) * 128],
                                   QM[:, cc:cc + 128],
                                   kv2[:, h * 128:(h + 1) * 128],
                                   start=True, stop=True, skip_group_check=True)
                _li.then_inc(st.h, 1)
            st.inc()
            if pend_tr is not None:
                pend_tr()
                pend_tr = None
            mm_t = st.n
            p.wait_val("vector", st, mm_t)
            p.wait_val("vector", st, oa_free[par])

            @p.on("vector")
            def _(e, _ps=ps, _tt=tt, _g=grp, _ob=ob):
                e.tensor_tensor(
                    out=oa_f[:, _ob:_ob + GW * 128].rearrange(
                        "p (j q) -> p j q", q=128),
                    in0=_ps[:, 0:GW * 128].rearrange("p (j q) -> p j q", q=128),
                    in1=rinvT[:, _tt * H + _g * GW: _tt * H + (_g + 1) * GW]
                    .unsqueeze(2).broadcast_to([128, GW, 128]),
                    op=OP.mult)
            ps_free[par][0] = sv.n
            dv_t = sv.n

            def emit_tr(_g=grp, _tt=tt, _ob=ob, _par=par, _dt=dv_t):
                p.wait_val("tensor", sv, _dt)
                for j in range(GW):
                    p.wait_val("tensor", sa, bst["psT2_free"][0])
                    p.wait_val("tensor", sv, bst["psT2_free"][1])

                    @p.on("tensor")
                    def _(e, _j=j):
                        e.matmul(psT2[:, 0:128],
                                 oa_f[:, _ob + _j * 128:_ob + (_j + 1) * 128],
                                 idf_t[:, :], is_transpose=True,
                                 start=True, stop=True).then_inc(st.h, 1)
                    st.inc()
                    p.wait("scalar", st)

                    @p.on("scalar")
                    def _(e, _j=j):
                        col = (_g * GW + _j) * TB + _tt * 128
                        e.activation(out=oat[:, col:col + 128], in_=psT2[:, 0:128],
                                     func=AF.Copy).then_inc(sa.h, 1)
                    sa.inc()
                    bst["psT2_free"][0] = sa.n
                oa_free[_par] = st.n
            pend_tr = emit_tr
        if bst["pend_ln1"]:
            bst["pend_ln1"].pop(0)()
    if pend_tr is not None:
        pend_tr()

    bank_free = bst["psAB_free"]
    wbuf_free = {0: st.n, 1: st.n}
    yf_free = {0: st.n, 1: st.n}
    p.wait("tensor", sa)
    for oc in range(nD):
        wb = oc % 2
        sw = S["w0"] if wb == 0 else S["w1"]
        qeng = "sync" if (wb == 0 or not USE_2Q_B) else "scalar"
        p.wait_val(qeng, st, wbuf_free[wb])

        @p.on(qeng)
        def _(e, _oc=oc, _wb=wb, _sw=sw):
            e.dma_start(out=wc2[:, _wb * nD * 128:(_wb + 1) * nD * 128]
                        .rearrange("p (dd o) -> p dd o", o=128),
                        in_=gfw("oT", _oc)).then_inc(_sw.h, 16)
        sw.inc(16)
        ps = psA2 if wb == 0 else psB2
        p.wait("tensor", sw)
        sv_t, sa_t = bank_free[wb]
        p.wait_val("tensor", sa, sa_t)
        p.wait_val("tensor", sv, sv_t)

        @p.on("tensor")
        def _(e, _wb=wb, _ps=ps):
            for dd in range(nD):
                w = wc2[:, _wb * nD * 128 + dd * 128: _wb * nD * 128 + (dd + 1) * 128]
                _li = e.matmul(_ps[:, 0:TB], w, oat[:, dd * TB:(dd + 1) * TB],
                               start=(dd == 0), stop=(dd == nD - 1))
            _li.then_inc(st.h, 1)
        st.inc()
        wbuf_free[wb] = st.n
        p.wait("scalar", st)
        p.wait_val("scalar", st, yf_free[wb])

        @p.on("scalar")
        def _(e, _oc=oc, _ps=ps, _wb=wb):
            e.activation(out=yf[:, _wb * TB:_wb * TB + TB], in_=_ps[:, 0:TB],
                         func=AF.Identity,
                         bias=bias_t[:, 3 * c.nD + _oc: 3 * c.nD + _oc + 1]
                         ).then_inc(sa.h, 1)
        sa.inc()
        bank_free[wb][1] = sa.n
        yf_t = sa.n
        for tt in range(nTT):
            sr = S["r0"] if tt % 2 == 0 else S["r1"]
            p.wait("sync", sv)

            @p.on("sync")
            def _(e, _oc=oc, _tt=tt, _sr=sr):
                e.dma_start(out=xres[:, (_tt % 2) * 128:(_tt % 2) * 128 + 128],
                            in_=io["x_own"][t0 + _tt * 128: t0 + (_tt + 1) * 128,
                                            _oc * 128:(_oc + 1) * 128]
                            ).then_inc(_sr.h, 16)
            sr.inc(16)
            p.wait_val("tensor", sa, max(bst["psT2_free"][0], yf_t))
            p.wait_val("tensor", sv, bst["psT2_free"][1])

            @p.on("tensor")
            def _(e, _tt=tt, _wb=wb):
                e.matmul(psT2[:, 0:128],
                         yf[:, _wb * TB + _tt * 128: _wb * TB + (_tt + 1) * 128],
                         idf_t[:, :], is_transpose=True,
                         start=True, stop=True).then_inc(st.h, 1)
            st.inc()
            p.wait("vector", st)
            p.wait("vector", sr)

            @p.on("vector")
            def _(e, _oc=oc, _tt=tt):
                e.scalar_tensor_tensor(
                    out=resid[:, _tt * c.D + _oc * 128: _tt * c.D + (_oc + 1) * 128],
                    in0=psT2[:, 0:128], scalar=1.0,
                    in1=xres[:, (_tt % 2) * 128:(_tt % 2) * 128 + 128],
                    op0=OP.mult, op1=OP.add)
            bst["psT2_free"][1] = sv.n
        yf_free[wb] = st.n

    for tt in range(nTT):
        def ln1_tt(tt=tt):
            p.wait("vector", ssp)
            p.wait("vector", sp)

            @p.on("vector")
            def _(e, _tt=tt):
                r = resid[:, _tt * c.D:(_tt + 1) * c.D]
                e.tensor_reduce(out=lnm[:, 0:1], in_=r, axis=AX, op=OP.add)
                e.tensor_scalar(out=lnm[:, 0:1], in0=lnm[:, 0:1],
                                scalar1=1.0 / c.D, scalar2=None, op0=OP.mult)
                e.tensor_scalar(out=x1w[:, :], in0=r, scalar1=lnm[:, 0:1],
                                scalar2=None, op0=OP.subtract)
            p.wait("scalar", sv)

            @p.on("scalar")
            def _(e):
                e.activation(out=tlB["z"][:, 0:c.D], in_=x1w[:, :], func=AF.Square,
                             accum_out=lnm[:, 1:2]).then_inc(sa.h, 1)
            sa.inc()
            p.wait("vector", sa)

            @p.on("vector")
            def _(e):
                e.tensor_scalar(out=lnm[:, 1:2], in0=lnm[:, 1:2],
                                scalar1=1.0 / c.D, scalar2=EPS_LN,
                                op0=OP.mult, op1=OP.add)
            p.wait("scalar", sv)

            @p.on("scalar")
            def _(e):
                e.activation(out=lnm[:, 2:3], in_=lnm[:, 1:2],
                             func=AF.Sqrt).then_inc(sa.h, 1)
            sa.inc()
            p.wait("vector", sa)

            @p.on("vector")
            def _(e):
                e.reciprocal(out=lnm[:, 3:4], in_=lnm[:, 2:3])
                e.tensor_scalar(out=x1w[:, :], in0=x1w[:, :], scalar1=lnm[:, 3:4],
                                scalar2=None, op0=OP.mult)
            p.wait("sync", sv)

            @p.on("sync")
            def _(e, _tt=tt):
                e.dma_start(out=io["x1_sp"][t0 + _tt * 128: t0 + (_tt + 1) * 128, :],
                            in_=x1w[:, :]).then_inc(ssp.h, 16)
            ssp.inc(16)
            _act_quant(p, c, x1w[:, :], c.D, tlB, x1q)
            p.wait("gpsimd", sv)
            p.wait("gpsimd", ssp)

            @p.on("gpsimd")
            def _(e):
                e.tensor_copy(x1q2[:, :].bitcast(U16), x1q[:, :].bitcast(U16)
                              ).then_inc(sp.h, 1)
            sp.inc()
            p.wait("sync", sp)

            @p.on("sync")
            def _(e, _tt=tt):
                e.dma_start(out=io["x1q_sp"][t0 + _tt * 128: t0 + (_tt + 1) * 128, :],
                            in_=x1q2[:, :]).then_inc(ssp.h, 16)
            ssp.inc(16)
        bst["pend_ln1"].append(ln1_tt)
    if tb == c.nB - 1:
        while bst["pend_ln1"]:
            bst["pend_ln1"].pop(0)()


def _phase_f(nc, c, io, S, idb_t):
    NRES = min(21, c.nI // 2)
    fctx = {"nres": NRES, "wres_t": 0}
    with contextlib.ExitStack() as fes:
        wres = (fes.enter_context(
            nc.sbuf_tensor("wres", [128, NRES * c.D], BF16)) if NRES else None)
        _phase_f1(nc, c, io, S, idb_t, wres, fctx)
        _phase_f2(nc, c, io, S, wres, fctx)


def _phase_f1(nc, c, io, S, idb_t, wres, fctx):
    sv, sa, st, sp, scc = (S[k] for k in ("sv", "sa", "st", "sp", "scc"))
    sg = S["sg"]
    SB, nST, nD, nI, nOS, OSW = c.SB, c.nST, c.nD, c.nI, c.nOS, c.OSW
    TH = max(c.T // 2, 128)
    nTC = TH // 128
    _ES = contextlib.ExitStack()
    wgs = _ES.enter_context(nc.sbuf_tensor("wgs", [128, 2 * nD * OSW], BF16))
    wus = _ES.enter_context(nc.sbuf_tensor("wus", [128, 2 * nD * OSW], BF16))
    x1qTh = _ES.enter_context(nc.sbuf_tensor("x1qTh", [128, nD * TH], BF16))
    intw = _ES.enter_context(nc.sbuf_tensor("intw", [128, 2 * OSW], F32))
    iq = _ES.enter_context(nc.sbuf_tensor("iq", [128, 2 * OSW], BF16))
    iq2 = _ES.enter_context(nc.sbuf_tensor("iq2", [128, 2 * OSW], BF16))
    f_amax = _ES.enter_context(nc.sbuf_tensor("f_amax", [128, OSW // 16], I32))
    f_sc8 = _ES.enter_context(nc.sbuf_tensor("f_sc8", [128, OSW // 16], I32))
    f_inv = _ES.enter_context(nc.sbuf_tensor("f_inv", [128, OSW // 16], I32))
    f_m64 = _ES.enter_context(nc.sbuf_tensor("f_m64", [128, OSW // 16], I32))
    f_z = _ES.enter_context(nc.sbuf_tensor("f_z", [128, 2 * OSW], F32))
    f_q8 = _ES.enter_context(nc.sbuf_tensor("f_q8", [128, 2 * OSW], FP8))
    iqs = _ES.enter_context(nc.sbuf_tensor("iqs", [128, 2 * OSW], BF16))
    psb = [_ES.enter_context(nc.psum_tensor("psf%d" % i, [128, 512], F32))
           for i in range(6)]
    psT = [_ES.enter_context(nc.psum_tensor("psq%d" % i, [128, 1024], BF16))
           for i in range(2)]

    def _emit_wres(p):
        stf = S["stf"]
        p.wait("scalar", S["scc"])

        @p.on("scalar")
        def _(e):
            for ii in range(fctx["nres"]):
                base = c.GB_OFF["dq"] + ii * 128 * c.DSH
                view = io["gout_b"].rearrange("(rk x) -> rk x", x=c.GBN)
                ap = view[:, base:base + 128 * c.DSH].rearrange(
                    "rk (pp w) -> pp rk w", pp=128)
                e.dma_start(out=wres[:, ii * c.D:(ii + 1) * c.D]
                            .rearrange("p (rk w) -> p rk w", w=c.DSH),
                            in_=ap).then_inc(stf.h, 16)
        stf.inc(16 * fctx["nres"])
        fctx["wres_t"] = stf.n

    p = Prog(S)
    barrier(p, [S["ssp"], S["w0"], S["w1"], S["r0"], S["r1"], sg, sv, sa,
                st, sp], ENGS)
    p.wait("sync", scc)
    p.wait("tensor", scc)
    if fctx["nres"]:
        _emit_wres(p)

    tls = []
    for s in range(2):
        nb = OSW // 32
        tls.append(dict(
            amax=f_amax[:, s * nb:(s + 1) * nb],
            sc8=f_sc8[:, s * nb:(s + 1) * nb],
            inv=f_inv[:, s * nb:(s + 1) * nb],
            m64=f_m64[:, s * nb:(s + 1) * nb],
            z=f_z[:, s * OSW:(s + 1) * OSW],
            q8=f_q8[:, s * OSW:(s + 1) * OSW]))

    fst = dict(ps_free=[(0, 0), (0, 0), (0, 0)],
               intw_free=[0, 0], iq_free=[0, 0], iq2_free=[0, 0],
               xh_free=0, gu_free=[0, 0], k=0, pend=[], psT_free=[0, 0],
               stg_free=[0, 0],
               wd_sem=[S["wd0"], S["wd1"]], wd_buf_free=[0, 0], wd_k=0)
    sgu = S["gu0"]
    sgu2 = S["gu1"]
    sxq = S["sxq"]

    for th in range(c.T // TH):
        ht0 = th * TH
        p.wait_val("sync", st, fst["xh_free"])
        p.wait("sync", S["ssp"])

        @p.on("sync")
        def _(e, _ht0=ht0):
            for dd in range(nD):
                e.dma_start(out=x1qTh[:, dd * TH:(dd + 1) * TH],
                            in_=io["x1q_sp"][_ht0:_ht0 + TH,
                                             dd * 128:(dd + 1) * 128],
                            transpose=True).then_inc(sxq.h, 16)
        sxq.inc(16 * nD)
        xq_t = sxq.n

        for osx in range(nOS):
            gpar = osx % 2
            gsem = sgu if gpar == 0 else sgu2
            p.wait_val("sync", st, fst["gu_free"][gpar])

            @p.on("sync")
            def _(e, _osx=osx, _gpar=gpar, _gs=gsem):
                rk = (_osx * OSW) // c.FSH
                lo = (_osx * OSW) % c.FSH
                for nm, buf in (("gq", wgs), ("uq", wus)):
                    base = rk * c.GBN + c.GB_OFF[nm]
                    piece = io["gout_b"][base:base + c.D * c.FSH].rearrange(
                        "(dd pp w) -> pp dd w", pp=128, w=c.FSH)
                    e.dma_start(
                        out=buf[:, _gpar * nD * OSW:(_gpar + 1) * nD * OSW]
                        .rearrange("p (dd o) -> p dd o", o=OSW),
                        in_=piece[:, :, lo:lo + OSW]).then_inc(_gs.h, 16)
            gsem.inc(16 * 2)

            for tc in range(nTC):
                k = fst["k"]
                kset = k % 3
                sl = k % 2
                pg, pu = psb[2 * kset], psb[2 * kset + 1]
                while len(fst["pend"]) >= 2:
                    fst["pend"].pop(0)()
                p.wait_val("tensor", sxq, xq_t)
                p.wait("tensor", gsem)
                sa_t, sv_t = fst["ps_free"][kset]
                p.wait_val("tensor", sa, sa_t)
                p.wait_val("tensor", sv, sv_t)

                @p.on("tensor")
                def _(e, _tc=tc, _gpar=gpar, _pg=pg, _pu=pu):
                    for dd in range(nD):
                        xw = x1qTh[:, dd * TH + _tc * 128: dd * TH + (_tc + 1) * 128]
                        gw = wgs[:, _gpar * nD * OSW + dd * OSW:
                                 _gpar * nD * OSW + (dd + 1) * OSW]
                        uw = wus[:, _gpar * nD * OSW + dd * OSW:
                                 _gpar * nD * OSW + (dd + 1) * OSW]
                        _li = e.matmul(_pg[:, 0:OSW], xw, gw, start=(dd == 0),
                                       stop=(dd == nD - 1))
                        _li = e.matmul(_pu[:, 0:OSW], xw, uw, start=(dd == 0),
                                       stop=(dd == nD - 1))
                    _li.then_inc(st.h, 1)
                st.inc()
                mm_t = st.n
                fst["gu_free"][gpar] = st.n

                p.wait_val("scalar", st, mm_t)
                p.wait_val("scalar", sv, fst["intw_free"][sl])

                @p.on("scalar")
                def _(e, _sl=sl, _pg=pg):
                    e.activation(out=intw[:, _sl * OSW:(_sl + 1) * OSW],
                                 in_=_pg[:, 0:OSW],
                                 func=AF.Silu).then_inc(sa.h, 1)
                sa.inc()
                silu_t = sa.n
                p.wait_val("vector", sa, silu_t)

                @p.on("vector")
                def _(e, _sl=sl, _pu=pu):
                    e.tensor_tensor(out=intw[:, _sl * OSW:(_sl + 1) * OSW],
                                    in0=intw[:, _sl * OSW:(_sl + 1) * OSW],
                                    in1=_pu[:, 0:OSW], op=OP.mult)
                fst["ps_free"][kset] = (silu_t, sv.n)

                p.wait_val("vector", sp, fst["iq_free"][sl])
                _act_quant(p, c, intw[:, sl * OSW:(sl + 1) * OSW], OSW,
                           tls[sl], iq[:, sl * OSW:(sl + 1) * OSW])
                fst["intw_free"][sl] = sv.n
                quant_t = sv.n

                p.wait_val("gpsimd", sv, quant_t)
                p.wait_val("gpsimd", st, fst["iq2_free"][sl])

                @p.on("gpsimd")
                def _(e, _sl=sl):
                    e.tensor_copy(iq2[:, _sl * OSW:(_sl + 1) * OSW].bitcast(U16),
                                  iq[:, _sl * OSW:(_sl + 1) * OSW].bitcast(U16)
                                  ).then_inc(sp.h, 1)
                sp.inc()
                fst["iq_free"][sl] = sp.n
                laund_t = sp.n

                def emit_tr(_sl=sl, _osx=osx, _tcol=ht0 + tc * 128,
                            _pslot=fst["k"] % 2, _lt=laund_t):
                    _psq = psT[_pslot]
                    _ssi = S["si0"] if _pslot == 0 else S["si1"]
                    nj = OSW // 128
                    p.wait_val("tensor", sp, _lt)
                    p.wait_val("tensor", sa, fst["psT_free"][_pslot])

                    @p.on("tensor")
                    def _(e):
                        for j in range(nj):
                            _li = e.matmul(_psq[:, j * 128:(j + 1) * 128],
                                           iq2[:, _sl * OSW + j * 128:
                                               _sl * OSW + (j + 1) * 128],
                                           idb_t[:, :], is_transpose=True,
                                           start=True, stop=True)
                        _li.then_inc(st.h, 1)
                    st.inc()
                    fst["iq2_free"][_sl] = st.n
                    p.wait_val("scalar", st, st.n)
                    p.wait_val("scalar", _ssi, fst["stg_free"][_pslot])

                    @p.on("scalar")
                    def _(e):
                        e.activation(out=iqs[:, _pslot * OSW:(_pslot + 1) * OSW],
                                     in_=_psq[:, 0:OSW],
                                     func=AF.Copy).then_inc(sa.h, 1)
                    sa.inc()
                    fst["psT_free"][_pslot] = sa.n
                    p.wait_val("sync", sa, sa.n)

                    @p.on("sync")
                    def _(e):
                        e.dma_start(
                            out=io["iq_sp"][_osx * OSW:(_osx + 1) * OSW,
                                            _tcol:_tcol + 128]
                            .rearrange("(j p) t -> p j t", p=128),
                            in_=iqs[:, _pslot * OSW:(_pslot + 1) * OSW]
                            .rearrange("p (j t) -> p j t", t=128)
                            ).then_inc(_ssi.h, 16)
                    _ssi.inc(16)
                    fst["stg_free"][_pslot] = _ssi.n
                fst["pend"].append(emit_tr)
                fst["k"] += 1
        fst["xh_free"] = st.n
    while fst["pend"]:
        fst["pend"].pop(0)()

    p.play(nc)
    _ES.close()


def _phase_f2(nc, c, io, S, wres, fctx):
    sv, sa, st, sp = (S[k] for k in ("sv", "sa", "st", "sp"))
    SB, nST, nD, nI = c.SB, c.nST, c.nD, c.nI
    _ES = contextlib.ExitStack()
    iqT = _ES.enter_context(nc.sbuf_tensor("iqT", [128, 2 * nI * SB], BF16))
    wds = _ES.enter_context(nc.sbuf_tensor("wds", [128, 2 * c.D], BF16))
    res2 = _ES.enter_context(nc.sbuf_tensor("res2", [128, nST * c.D], F32))
    x1res = _ES.enter_context(nc.sbuf_tensor("x1res", [128, 2 * c.D], F32))
    f_lnm = _ES.enter_context(nc.sbuf_tensor("f_lnm", [128, 8], F32))
    x2w = _ES.enter_context(nc.sbuf_tensor("x2w", [128, c.D], F32))
    psb = [_ES.enter_context(nc.psum_tensor("psg%d" % i, [128, 512], F32))
           for i in range(8)]
    fst = dict(wd_sem=[S["wd0"], S["wd1"]], wd_buf_free=[0, 0], wd_k=0)
    iq_sems = [(S["sxq"], S["skv"]), (S["sj0"], S["sj1"])]
    p = Prog(S)
    barrier(p, [S["ssp"], S["si0"], S["si1"], S["sxq"], S["skv"],
                S["sj0"], S["sj1"], S["gu0"], S["gu1"], sv, sa, st, sp], ENGS)
    dn_state = dict(iqs_free=[0, 0], iqT_t=[0, 0], dn_free=(sa.n, sv.n))

    def iqt_prefetch(t0p, qs):
        pair = iq_sems[qs]

        @p.on("sync")
        def _(e, _t0=t0p, _qs=qs, _ss=pair[0]):
            e.dma_start(out=iqT[:, _qs * nI * SB:(_qs + 1) * nI * SB]
                        .rearrange("p (ii t) -> p ii t", t=SB),
                        in_=io["iq_sp"].rearrange("(ii p) t -> p ii t", p=128)
                        [:, :, _t0:_t0 + SB]).then_inc(_ss.h, 16)
        pair[0].inc(16)
        dn_state["iqT_t"][qs] = (pair[0].n, pair[1].n)

    iqt_prefetch(0, 0)
    for sb in range(c.nSB):
        t0 = sb * SB
        qs = sb % 2
        if sb + 1 < c.nSB:
            nqs = (sb + 1) % 2
            p.wait_val("sync", st, dn_state["iqs_free"][nqs])
            iqt_prefetch(t0 + SB, nqs)
        iqT_t, iqT_t2 = dn_state["iqT_t"][qs]
        sxq, sxq2 = iq_sems[qs]

        SPW = min(512, c.D)
        nSpan = c.D // SPW
        banks = psb[:nST * nSpan]
        sa_t, sv_t = dn_state["dn_free"]
        p.wait_val("tensor", sa, sa_t)
        p.wait_val("tensor", sv, sv_t)
        p.wait_val("tensor", sxq, iqT_t)
        p.wait_val("tensor", sxq2, iqT_t2)
        nres = fctx["nres"]
        if sb == 0 and nres:
            p.wait_val("tensor", S["stf"], fctx["wres_t"])
        for ii in range(nI):
            if ii < nres:
                @p.on("tensor")
                def _(e, _ii=ii, _qs=qs):
                    for t1 in range(nST):
                        for spn in range(nSpan):
                            bank = banks[t1 * nSpan + spn]
                            w = wres[:, _ii * c.D + spn * SPW:
                                     _ii * c.D + (spn + 1) * SPW]
                            _li = e.matmul(bank[:, 0:SPW],
                                           iqT[:, _qs * nI * SB + _ii * SB + t1 * 128:
                                               _qs * nI * SB + _ii * SB + (t1 + 1) * 128],
                                           w, start=(_ii == 0), stop=(_ii == nI - 1),
                                           skip_group_check=True)
                    _li.then_inc(st.h, 1)
                st.inc()
                continue
            wpar = fst["wd_k"] % 2
            swd = fst["wd_sem"][wpar]
            qeng = "sync" if (wpar == 0 or not USE_2Q_F) else "scalar"
            p.wait_val(qeng, st, fst["wd_buf_free"][wpar])

            @p.on(qeng)
            def _(e, _ii=ii, _wpar=wpar, _swd=swd):
                base = c.GB_OFF["dq"] + _ii * 128 * c.DSH
                view = io["gout_b"].rearrange("(rk x) -> rk x", x=c.GBN)
                ap = view[:, base:base + 128 * c.DSH].rearrange(
                    "rk (pp w) -> pp rk w", pp=128)
                e.dma_start(out=wds[:, _wpar * c.D:(_wpar + 1) * c.D]
                            .rearrange("p (rk w) -> p rk w", w=c.DSH),
                            in_=ap).then_inc(_swd.h, 16)
            swd.inc(16)
            p.wait("tensor", swd)

            @p.on("tensor")
            def _(e, _ii=ii, _wpar=wpar, _qs=qs):
                for t1 in range(nST):
                    for spn in range(nSpan):
                        bank = banks[t1 * nSpan + spn]
                        w = wds[:, _wpar * c.D + spn * SPW:
                                _wpar * c.D + (spn + 1) * SPW]
                        _li = e.matmul(bank[:, 0:SPW],
                                       iqT[:, _qs * nI * SB + _ii * SB + t1 * 128:
                                           _qs * nI * SB + _ii * SB + (t1 + 1) * 128],
                                       w, start=(_ii == 0), stop=(_ii == nI - 1),
                                       skip_group_check=True)
                _li.then_inc(st.h, 1)
            st.inc()
            fst["wd_buf_free"][wpar] = st.n
            fst["wd_k"] += 1
        dn_t = st.n
        dn_state["iqs_free"][qs] = st.n

        for t1 in range(nST):
            sf = S["f0"] if t1 % 2 == 0 else S["f1"]
            p.wait("sync", sv)

            @p.on("sync")
            def _(e, _t1=t1, _sf=sf, _t0=t0):
                e.dma_start(out=x1res[:, (_t1 % 2) * c.D:(_t1 % 2) * c.D + c.D],
                            in_=io["x1_sp"][_t0 + _t1 * 128: _t0 + (_t1 + 1) * 128, :]
                            ).then_inc(_sf.h, 16)
            sf.inc(16)
            p.wait_val("vector", st, dn_t)
            p.wait("vector", sf)

            @p.on("vector")
            def _(e, _t1=t1):
                for spn in range(nSpan):
                    bank = banks[_t1 * nSpan + spn]
                    e.scalar_tensor_tensor(
                        out=res2[:, _t1 * c.D + spn * SPW: _t1 * c.D + (spn + 1) * SPW],
                        in0=bank[:, 0:SPW], scalar=1.0,
                        in1=x1res[:, (_t1 % 2) * c.D + spn * SPW:
                                  (_t1 % 2) * c.D + (spn + 1) * SPW],
                        op0=OP.mult, op1=OP.add)
        dn_state["dn_free"] = (0, sv.n)

        so = S["so"]
        for t1 in range(nST):
            p.wait("vector", so)

            @p.on("vector")
            def _(e, _t1=t1):
                r = res2[:, _t1 * c.D:(_t1 + 1) * c.D]
                e.tensor_reduce(out=f_lnm[:, 0:1], in_=r, axis=AX, op=OP.add)
                e.tensor_scalar(out=f_lnm[:, 0:1], in0=f_lnm[:, 0:1],
                                scalar1=1.0 / c.D, scalar2=None, op0=OP.mult)
                e.tensor_scalar(out=x2w[:, :], in0=r, scalar1=f_lnm[:, 0:1],
                                scalar2=None, op0=OP.subtract)
            p.wait("scalar", sv)

            @p.on("scalar")
            def _(e, _t1=t1):
                e.activation(out=x1res[:, (_t1 % 2) * c.D:(_t1 % 2) * c.D + c.D],
                             in_=x2w[:, :],
                             func=AF.Square, accum_out=f_lnm[:, 1:2]).then_inc(sa.h, 1)
            sa.inc()
            p.wait("vector", sa)

            @p.on("vector")
            def _(e):
                e.tensor_scalar(out=f_lnm[:, 1:2], in0=f_lnm[:, 1:2],
                                scalar1=1.0 / c.D, scalar2=EPS_LN,
                                op0=OP.mult, op1=OP.add)
            p.wait("scalar", sv)

            @p.on("scalar")
            def _(e):
                e.activation(out=f_lnm[:, 2:3], in_=f_lnm[:, 1:2],
                             func=AF.Sqrt).then_inc(sa.h, 1)
            sa.inc()
            p.wait("vector", sa)

            @p.on("vector")
            def _(e):
                e.reciprocal(out=f_lnm[:, 3:4], in_=f_lnm[:, 2:3])
                e.tensor_scalar(out=x2w[:, :], in0=x2w[:, :], scalar1=f_lnm[:, 3:4],
                                scalar2=None, op0=OP.mult)
            p.wait("sync", sv)

            @p.on("sync")
            def _(e, _t1=t1, _t0=t0):
                e.dma_start(out=io["out"][_t0 + _t1 * 128: _t0 + (_t1 + 1) * 128, :],
                            in_=x2w[:, :]).then_inc(so.h, 16)
            so.inc(16)

    barrier(p, [S["so"], S["f0"], S["f1"], S["wd0"], S["wd1"], S["gu0"],
                S["sxq"], S["skv"], S["sj0"], S["sj1"], S["ssp"],
                sv, sa, st, sp], ["sync"])
    p.play(nc)
    _ES.close()


_NC_CACHE = {}


def _get_nc(full):
    if full not in _NC_CACHE:
        _NC_CACHE[full] = build_nc(CFG(full))
    return _NC_CACHE[full]


def kernel(L0_attn_hidden_states, L0_attn_wq, L0_attn_bq, L0_attn_wk,
           L0_attn_bk, L0_attn_wv, L0_attn_bv, L0_attn_wo, L0_attn_bo,
           L0_attn_wmap_q, L0_attn_wmap_k, L0_ffn_w_gate, L0_ffn_w_up,
           L0_ffn_w_down):
    x = np.ascontiguousarray(np.asarray(L0_attn_hidden_states, dtype=np.float32))
    B, Sq, D = x.shape
    c = CFG(full=True)
    assert (B, Sq, D) == (4, 4096, 2048)
    wq = np.ascontiguousarray(np.asarray(L0_attn_wq, dtype=np.float32))
    wk = np.ascontiguousarray(np.asarray(L0_attn_wk, dtype=np.float32))
    wv = np.ascontiguousarray(np.asarray(L0_attn_wv, dtype=np.float32))
    wo = np.ascontiguousarray(np.asarray(L0_attn_wo, dtype=np.float32))
    wg = np.ascontiguousarray(np.asarray(L0_ffn_w_gate, dtype=np.float32))
    wu = np.ascontiguousarray(np.asarray(L0_ffn_w_up, dtype=np.float32))
    wd = np.ascontiguousarray(np.asarray(L0_ffn_w_down, dtype=np.float32))
    biases = np.stack([np.asarray(L0_attn_bq, dtype=np.float32),
                       np.asarray(L0_attn_bk, dtype=np.float32),
                       np.asarray(L0_attn_bv, dtype=np.float32),
                       np.asarray(L0_attn_bo, dtype=np.float32)], axis=0)
    wmaps = np.concatenate([np.asarray(L0_attn_wmap_q, dtype=np.float32),
                            np.asarray(L0_attn_wmap_k, dtype=np.float32)], axis=0)
    idf = np.eye(128, dtype=np.float32)
    idb = np.eye(128, dtype=np.float32).astype(ml_dtypes.bfloat16)

    nc = _get_nc(True)
    in_maps = []
    for core in range(c.NC):
        b, half = core // 2, core % 2
        in_maps.append({
            "x_own": np.ascontiguousarray(x[b, half * c.T:(half + 1) * c.T, :]),
            "wq_s": np.ascontiguousarray(wq[core * c.WSH:(core + 1) * c.WSH, :]),
            "wk_s": np.ascontiguousarray(wk[core * c.WSH:(core + 1) * c.WSH, :]),
            "wv_s": np.ascontiguousarray(wv[core * c.WSH:(core + 1) * c.WSH, :]),
            "wo_s": np.ascontiguousarray(wo[core * c.WSH:(core + 1) * c.WSH, :]),
            "wg_s": np.ascontiguousarray(wg[core * c.FSH:(core + 1) * c.FSH, :]),
            "wu_s": np.ascontiguousarray(wu[core * c.FSH:(core + 1) * c.FSH, :]),
            "wd_s": np.ascontiguousarray(wd[core * c.DSH:(core + 1) * c.DSH, :]),
            "biases": biases, "wmaps": wmaps, "idf": idf, "idb": idb,
        })
    res = run_bass_kernel_spmd(nc, in_maps, core_ids=list(range(c.NC)))
    out = np.empty((B, Sq, D), dtype=np.float32)
    for core in range(c.NC):
        b, half = core // 2, core % 2
        out[b, half * c.T:(half + 1) * c.T, :] = res.results[core]["out"]
    return out


# version 32
# speedup vs baseline: 1.1048x; 1.0130x over previous
import contextlib
import numpy as np
import ml_dtypes
import concourse.bass as bass
import concourse.mybir as mybir
from concourse.bass_utils import run_bass_kernel_spmd

F32 = mybir.dt.float32
BF16 = mybir.dt.bfloat16
I32 = mybir.dt.int32
U16 = mybir.dt.uint16
FP8 = mybir.dt.float8e4
OP = mybir.AluOpType
AF = mybir.ActivationFunctionType
AX = mybir.AxisListType.X

MAGIC_E2M1 = float(np.float32(1.5 * 2 ** 22))
ACT_CLIP = float(np.float32(15.0 / 128.0))
EPS_ATT = 1e-6
EPS_LN = 1e-5
ENGS = ("sync", "vector", "scalar", "tensor", "gpsimd")
USE_2Q_A = False
USE_2Q_B = False
USE_2Q_F = False
USE_2Q_FP = False


class CFG:
    def __init__(self, full=True):
        if full:
            self.NC, self.T, self.D, self.I, self.H = 8, 2048, 2048, 8192, 16
            self.TBLK, self.SB = 512, 256
        else:
            self.NC, self.T, self.D, self.I, self.H = 2, 512, 512, 1024, 4
            self.TBLK, self.SB = 256, 256
        self.DH, self.M, self.QB = 128, 64, 32
        self.HP = self.H // 2
        self.nD = self.D // 128
        self.nB = self.T // self.TBLK
        self.nTT = self.TBLK // 128
        self.nSB = self.T // self.SB
        self.nST = self.SB // 128
        self.nI = self.I // 128
        self.OSW = min(512, self.I, self.I // self.NC)
        self.nOS = self.I // self.OSW
        self.WSH = self.D // self.NC
        self.FSH = self.I // self.NC
        self.DSH = self.D // self.NC
        assert self.H * self.DH == self.D
        self.GF_OFF, off = {}, 0
        for nm in ("qT", "kT", "vT", "oT"):
            self.GF_OFF[nm] = off
            off += self.D * self.WSH
        self.GFN = off
        self.GB_OFF, off = {}, 0
        for nm, sz in (("gq", self.D * self.FSH), ("uq", self.D * self.FSH),
                       ("dq", self.I * self.DSH)):
            self.GB_OFF[nm] = off
            off += sz
        self.GBN = off


class Sem:
    def __init__(self, h):
        self.h, self.n = h, 0

    def inc(self, k=1):
        self.n += k
        return self.n


class _CountInst:
    def then_inc(self, *a, **k):
        return self


class _CountMock:

    def __init__(self):
        self.count = 0

    def wait_ge(self, *a, **k):
        return None

    def __getattr__(self, name):
        def f(*a, **k):
            self.count += 1
            return _CountInst()
        return f


class _VecProxy:

    _PASS = ("wait_ge",)

    def __init__(self, e, semh, n0):
        object.__setattr__(self, "_e", e)
        object.__setattr__(self, "_semh", semh)
        object.__setattr__(self, "_n", n0)

    def __getattr__(self, name):
        a = getattr(self._e, name)
        if name in self._PASS or not callable(a):
            return a

        def f(*args, **kw):
            if self._n > 0:
                self._e.wait_ge(self._semh, self._n)
            inst = a(*args, **kw)
            inst.then_inc(self._semh, 1)
            object.__setattr__(self, "_n", self._n + 1)
            return inst
        return f


class Prog:

    def __init__(self, S):
        self.q = {e: [] for e in ENGS}
        self.sv = S["sv"]
        self.v0 = S["sv"].n

    def on(self, eng):
        def deco(fn):
            self.q[eng].append(fn)
            if eng == "vector":
                m = _CountMock()
                fn(m)
                self.sv.inc(m.count)
        return deco

    def wait(self, eng, sem):
        v = sem.n
        if v > 0:
            self.q[eng].append(lambda e, _s=sem.h, _v=v: e.wait_ge(_s, _v))

    def wait_val(self, eng, sem, v):
        if v > 0:
            self.q[eng].append(lambda e, _s=sem.h, _v=v: e.wait_ge(_s, _v))

    def play(self, nc):
        with nc.Block() as blk:
            for eng in ENGS:
                fns = self.q[eng]
                if not fns:
                    continue
                deco = getattr(blk, eng)
                if eng == "vector":
                    def body(e, _fns=fns, _n0=self.v0):
                        px = _VecProxy(e, self.sv.h, _n0)
                        for f in _fns:
                            f(px)
                else:
                    def body(e, _fns=fns):
                        for f in _fns:
                            f(e)
                deco(body)


def barrier(p, sems, engs=ENGS):
    for eng in engs:
        for s in sems:
            p.wait(eng, s)


SEM_NAMES = ["sg", "sv", "sa", "st", "sp", "scc",
             "sco", "sl", "sl1", "stf", "stb",
             "w0", "w1", "x0", "x1", "r0", "r1", "f0", "f1",
             "gu0", "gu1", "wd0", "wd1", "sxq", "skv", "ssp", "so",
             "si0", "si1", "sj0", "sj1"]


def build_nc(c):
    nc = bass.Bass("TRN2", target_bir_lowering=False)
    io = {}
    io["x_own"] = nc.dram_tensor("x_own", [c.T, c.D], F32, kind="ExternalInput").ap()
    for nm in ("wq_s", "wk_s", "wv_s", "wo_s"):
        io[nm] = nc.dram_tensor(nm, [c.WSH, c.D], F32, kind="ExternalInput").ap()
    io["wg_s"] = nc.dram_tensor("wg_s", [c.FSH, c.D], F32, kind="ExternalInput").ap()
    io["wu_s"] = nc.dram_tensor("wu_s", [c.FSH, c.D], F32, kind="ExternalInput").ap()
    io["wd_s"] = nc.dram_tensor("wd_s", [c.DSH, c.I], F32, kind="ExternalInput").ap()
    io["biases"] = nc.dram_tensor("biases", [4, c.D], F32, kind="ExternalInput").ap()
    io["wmaps"] = nc.dram_tensor("wmaps", [2 * c.M, c.DH], F32, kind="ExternalInput").ap()
    io["idf"] = nc.dram_tensor("idf", [128, 128], F32, kind="ExternalInput").ap()
    io["idb"] = nc.dram_tensor("idb", [128, 128], BF16, kind="ExternalInput").ap()
    io["out"] = nc.dram_tensor("out", [c.T, c.D], F32, kind="ExternalOutput").ap()

    io["gin_f"] = nc.dram_tensor("gin_f", [c.GFN], F32)
    _asp = dict(addr_space="Shared") if c.NC > 4 else {}
    io["gout_f"] = nc.dram_tensor("gout_f", [c.NC * c.GFN], F32, **_asp)
    io["gin_b"] = nc.dram_tensor("gin_b", [c.GBN], BF16)
    io["gout_b"] = nc.dram_tensor("gout_b", [c.NC * c.GBN], BF16, **_asp)
    nKB = (c.HP + 1) // 2
    KVW = 512 * nKB + 16
    io["kv_bin"] = nc.dram_tensor("kv_bin", [128, KVW], F32)
    io["kv_bout"] = nc.dram_tensor("kv_bout", [128, KVW], F32)
    io["x1_sp"] = nc.dram_tensor("x1_sp", [c.T, c.D], F32)
    io["x1q_sp"] = nc.dram_tensor("x1q_sp", [c.T, c.D], BF16)
    io["iq_sp"] = nc.dram_tensor("iq_sp", [c.I, c.T], BF16)

    with contextlib.ExitStack() as es:
        S = {nm: Sem(es.enter_context(nc.semaphore(nm))) for nm in SEM_NAMES}
        _build(nc, c, io, S)
    return nc


def _build(nc, c, io, S):
    sg, sv, sa, st, sp, scc = (S[k] for k in ("sg", "sv", "sa", "st", "sp", "scc"))
    nKB = (c.HP + 1) // 2
    KVW = 512 * nKB + 16

    pers = [
        nc.sbuf_tensor("idf_t", [128, 128], F32),
        nc.sbuf_tensor("idb_t", [128, 128], BF16),
        nc.sbuf_tensor("bias_t", [128, 4 * c.nD], F32),
        nc.sbuf_tensor("wmT", [128, 128], F32),
        nc.sbuf_tensor("ones_f", [128, 8], F32),
        nc.sbuf_tensor("kvacc", [128, KVW], F32),
    ]
    hp_ = [t.__enter__() for t in pers]
    idf_t, idb_t, bias_t, wmT, ones_f, kvacc = hp_

    def gfw(nm, oc):
        rk, r = oc // (c.WSH // 128), oc % (c.WSH // 128)
        base = rk * c.GFN + c.GF_OFF[nm] + r * c.D * 128
        return io["gout_f"][base:base + c.D * 128].rearrange(
            "(p dd o) -> p dd o", p=128, o=128)

    _phase_w(nc, c, io, S, idf_t, idb_t, bias_t, wmT, ones_f, kvacc)

    with (
        nc.sbuf_tensor("QM", [128, c.HP * c.T], F32) as QM,
    ):
        _phase_a(nc, c, io, S, gfw, idf_t, bias_t, wmT, ones_f, kvacc, QM, nKB)

        p = Prog(S)
        skv = S["skv"]
        barrier(p, [sv], ["sync"])

        @p.on("sync")
        def _(e):
            e.dma_start(out=io["kv_bin"][:, :], in_=kvacc[:, :]).then_inc(skv.h, 16)
        skv.inc(16)
        p.wait("gpsimd", skv)

        @p.on("gpsimd")
        def _(e):
            e.collective_compute(
                "AllReduce", OP.add,
                replica_groups=[[2 * i, 2 * i + 1] for i in range(c.NC // 2)],
                ins=[io["kv_bin"].ap().opt()], outs=[io["kv_bout"].ap().opt()],
            ).then_inc(scc.h, 1)
        scc.inc(1)
        p.wait("gpsimd", scc)

        @p.on("gpsimd")
        def _(e):
            e.dma_start(out=kvacc[:, :], in_=io["kv_bout"][:, :]).then_inc(sg.h, 16)
        sg.inc(16)
        p.play(nc)

        _phase_b(nc, c, io, S, gfw, idf_t, idb_t, bias_t, kvacc, QM, nKB)

    _phase_f(nc, c, io, S, idb_t)

    for t in reversed(pers):
        t.__exit__(None, None, None)


def _act_quant(p, c, x_ap, W, tl, out_bf):
    nb = W // 32

    @p.on("vector")
    def _(v):
        x3 = x_ap.rearrange("p (b k) -> p b k", k=32)
        v.tensor_reduce(out=tl["amax"][:, 0:nb].bitcast(F32), in_=x3, axis=AX,
                        op=OP.max, apply_absolute_value=True)
        v.tensor_scalar(out=tl["sc8"][:, 0:nb], in0=tl["amax"][:, 0:nb],
                        scalar1=23, scalar2=0xFF,
                        op0=OP.logical_shift_right, op1=OP.bitwise_and)
        v.tensor_scalar(out=tl["inv"][:, 0:nb], in0=tl["sc8"][:, 0:nb],
                        scalar1=-1.0, scalar2=250.0, op0=OP.mult, op1=OP.add)
        v.tensor_scalar(out=tl["inv"][:, 0:nb], in0=tl["inv"][:, 0:nb],
                        scalar1=23, scalar2=None, op0=OP.logical_shift_left)
        v.tensor_scalar(out=tl["m64"][:, 0:nb], in0=tl["sc8"][:, 0:nb],
                        scalar1=4.0, scalar2=None, op0=OP.add)
        v.tensor_scalar(out=tl["m64"][:, 0:nb], in0=tl["m64"][:, 0:nb],
                        scalar1=23, scalar2=None, op0=OP.logical_shift_left)
        v.tensor_tensor(out=tl["z"][:, 0:W].rearrange("p (b k) -> p b k", k=32),
                        in0=x3,
                        in1=tl["inv"][:, 0:nb].bitcast(F32).unsqueeze(2)
                        .broadcast_to([128, nb, 32]), op=OP.mult)
        v.tensor_scalar(out=tl["z"][:, 0:W], in0=tl["z"][:, 0:W],
                        scalar1=ACT_CLIP, scalar2=-ACT_CLIP,
                        op0=OP.min, op1=OP.max)
        v.tensor_copy(tl["q8"][:, 0:W], tl["z"][:, 0:W])
        v.tensor_tensor(out=out_bf[:, 0:W].rearrange("p (b k) -> p b k", k=32),
                        in0=tl["q8"][:, 0:W].rearrange("p (b k) -> p b k", k=32),
                        in1=tl["m64"][:, 0:nb].bitcast(F32).unsqueeze(2)
                        .broadcast_to([128, nb, 32]), op=OP.mult)


def _wgt_quant(p, c, w_ap, W, tl, out_bf):
    nb = W // 32

    @p.on("vector")
    def _(v):
        w3 = w_ap.rearrange("p (b k) -> p b k", k=32)
        v.tensor_reduce(out=tl["amax"][:, 0:nb].bitcast(F32), in_=w3, axis=AX,
                        op=OP.max, apply_absolute_value=True)
        v.tensor_scalar(out=tl["sc8"][:, 0:nb], in0=tl["amax"][:, 0:nb],
                        scalar1=23, scalar2=0xFF,
                        op0=OP.logical_shift_right, op1=OP.bitwise_and)
        v.tensor_scalar(out=tl["scl"][:, 0:nb], in0=tl["sc8"][:, 0:nb],
                        scalar1=-2.0, scalar2=None, op0=OP.add)
        v.tensor_scalar(out=tl["scl"][:, 0:nb], in0=tl["scl"][:, 0:nb],
                        scalar1=23, scalar2=None, op0=OP.logical_shift_left)
        v.tensor_scalar(out=tl["clp"][:, 0:nb], in0=tl["sc8"][:, 0:nb],
                        scalar1=23, scalar2=0x400000,
                        op0=OP.logical_shift_left, op1=OP.bitwise_or)
        v.tensor_scalar(out=tl["nclp"][:, 0:nb], in0=tl["clp"][:, 0:nb].bitcast(F32),
                        scalar1=-1.0, scalar2=None, op0=OP.mult)
        v.tensor_scalar(out=tl["ti"][:, 0:W], in0=w_ap.bitcast(I32),
                        scalar1=0x7F800000, scalar2=None, op0=OP.bitwise_and)
        v.tensor_tensor(out=tl["p2"][:, 0:W].rearrange("p (b k) -> p b k", k=32),
                        in0=tl["ti"][:, 0:W].bitcast(F32)
                        .rearrange("p (b k) -> p b k", k=32),
                        in1=tl["scl"][:, 0:nb].bitcast(F32).unsqueeze(2)
                        .broadcast_to([128, nb, 32]), op=OP.max)
        v.tensor_scalar(out=tl["bt"][:, 0:W], in0=tl["p2"][:, 0:W],
                        scalar1=MAGIC_E2M1, scalar2=None, op0=OP.mult)
        v.tensor_tensor(out=tl["p2"][:, 0:W], in0=w_ap, in1=tl["bt"][:, 0:W],
                        op=OP.add)
        v.tensor_tensor(out=tl["p2"][:, 0:W], in0=tl["p2"][:, 0:W],
                        in1=tl["bt"][:, 0:W], op=OP.subtract)
        v.tensor_tensor(out=tl["p2"][:, 0:W].rearrange("p (b k) -> p b k", k=32),
                        in0=tl["p2"][:, 0:W].rearrange("p (b k) -> p b k", k=32),
                        in1=tl["clp"][:, 0:nb].bitcast(F32).unsqueeze(2)
                        .broadcast_to([128, nb, 32]), op=OP.min)
        v.tensor_tensor(out=out_bf[:, 0:W].rearrange("p (b k) -> p b k", k=32),
                        in0=tl["p2"][:, 0:W].rearrange("p (b k) -> p b k", k=32),
                        in1=tl["nclp"][:, 0:nb].unsqueeze(2)
                        .broadcast_to([128, nb, 32]), op=OP.max)


def _phase_w(nc, c, io, S, idf_t, idb_t, bias_t, wmT, ones_f, kvacc):
    sv, sa, st, sp, scc = (S[k] for k in ("sv", "sa", "st", "sp", "scc"))
    sco, sl, sl1 = S["sco"], S["sl"], S["sl1"]
    stf, stb = S["stf"], S["stb"]
    _ES = contextlib.ExitStack()
    WJ = min(4096, c.I)
    wstage = _ES.enter_context(nc.sbuf_tensor("wstage", [128, 2 * WJ], F32))
    wbf = _ES.enter_context(nc.sbuf_tensor("wbf", [128, 2 * WJ], BF16))
    wbf2 = _ES.enter_context(nc.sbuf_tensor("wbf2", [128, 2 * WJ], BF16))
    wTf = _ES.enter_context(nc.sbuf_tensor("wTf", [128, c.D], F32))
    wTb = _ES.enter_context(nc.sbuf_tensor("wTb", [128, WJ], BF16))
    q_amax = _ES.enter_context(nc.sbuf_tensor("q_amax", [128, WJ // 32], I32))
    q_sc8 = _ES.enter_context(nc.sbuf_tensor("q_sc8", [128, WJ // 32], I32))
    q_scl = _ES.enter_context(nc.sbuf_tensor("q_scl", [128, WJ // 32], I32))
    q_clp = _ES.enter_context(nc.sbuf_tensor("q_clp", [128, WJ // 32], I32))
    q_nclp = _ES.enter_context(nc.sbuf_tensor("q_nclp", [128, WJ // 32], F32))
    q_ti = _ES.enter_context(nc.sbuf_tensor("q_ti", [128, WJ], I32))
    q_p2 = _ES.enter_context(nc.sbuf_tensor("q_p2", [128, WJ], F32))
    q_bt = _ES.enter_context(nc.sbuf_tensor("q_bt", [128, WJ], F32))
    wm_st = _ES.enter_context(nc.sbuf_tensor("wm_st", [128, 128], F32))
    psw0 = _ES.enter_context(nc.psum_tensor("psw0", [128, 512], F32))
    psw1 = _ES.enter_context(nc.psum_tensor("psw1", [128, 1024], BF16))
    tl = dict(amax=q_amax, sc8=q_sc8, scl=q_scl, clp=q_clp, nclp=q_nclp,
              ti=q_ti, p2=q_p2, bt=q_bt)
    p = Prog(S)

    @p.on("sync")
    def _(e):
        e.dma_start(out=idf_t[:, :], in_=io["idf"][:, :]).then_inc(sco.h, 16)
        e.dma_start(out=idb_t[:, :], in_=io["idb"][:, :]).then_inc(sco.h, 16)
        for m in range(4):
            e.dma_start(out=bias_t[:, m * c.nD:(m + 1) * c.nD],
                        in_=io["biases"][m, :].rearrange("(oc p) -> p oc", p=128),
                        allow_slow_non_contiguous=True).then_inc(sco.h, 16)
        e.dma_start(out=wm_st[:, :], in_=io["wmaps"][:, :]).then_inc(sco.h, 16)
    sco.inc(16 * 7)
    p.wait("scalar", sco)

    @p.on("scalar")
    def _(e):
        e.activation(out=ones_f[:, :], in_=wm_st[:, 0:8], func=AF.Copy,
                     scale=0.0, bias=1.0).then_inc(sa.h, 1)
    sa.inc()
    p.wait("vector", sco)

    @p.on("vector")
    def _(e):
        e.memset(kvacc[:, :], 0.0)
    p.wait("tensor", sco)

    @p.on("tensor")
    def _(e):
        e.matmul(psw0[:, 0:128], wm_st[:, :], idf_t[:, :], is_transpose=True,
                 start=True, stop=True).then_inc(st.h, 1)
    st.inc()
    p.wait("scalar", st)

    @p.on("scalar")
    def _(e):
        e.activation(out=wmT[:, :], in_=psw0[:, 0:128],
                     func=AF.Copy).then_inc(sa.h, 1)
    sa.inc()

    p.wait("tensor", sa)

    jobs = []
    for nm, srcnm in (("qT", "wq_s"), ("kT", "wk_s"), ("vT", "wv_s"),
                      ("oT", "wo_s")):
        for r in range(c.WSH // 128):
            jobs.append(("attn", nm, srcnm, r, 0, c.D))
    for nm, srcnm in (("gq", "wg_s"), ("uq", "wu_s")):
        for r in range(c.FSH // 128):
            jobs.append(("ffn", nm, srcnm, r, 0, c.D))
    for r in range(c.DSH // 128):
        for c0 in range(0, c.I, WJ):
            jobs.append(("ffn", "dq", "wd_s", r, c0, WJ))

    wst_free = [(0, 0), (0, 0)]
    wbf_free = [0, 0]
    wbf2_free = [0, 0]
    wTf_free = 0
    wTb_free = 0
    attn_done = False
    for jidx, (kind, nm, srcnm, r, c0, W) in enumerate(jobs):
        nT = W // 128
        slot = jidx % 2
        sb0 = slot * WJ
        slsem = sl if slot == 0 else sl1
        st_t, sv_t = wst_free[slot]
        p.wait_val("sync", st, st_t)
        p.wait_val("sync", sv, sv_t)

        @p.on("sync")
        def _(e, _s=srcnm, _r=r, _c0=c0, _W=W, _b=sb0, _sem=slsem):
            e.dma_start(out=wstage[:, _b:_b + _W],
                        in_=io[_s][_r * 128:(_r + 1) * 128, _c0:_c0 + _W]
                        ).then_inc(_sem.h, 16)
        slsem.inc(16)
        p.wait("vector", slsem)
        p.wait("tensor", slsem)

        if kind == "attn":
            for j in range(nT):
                p.wait("tensor", sv)

                @p.on("tensor")
                def _(e, _j=j, _b=sb0):
                    e.matmul(psw0[:, 0:128],
                             wstage[:, _b + _j * 128:_b + (_j + 1) * 128],
                             idf_t[:, :], is_transpose=True,
                             start=True, stop=True).then_inc(st.h, 1)
                st.inc()
                if j == 0:
                    p.wait_val("vector", stf, wTf_free)
                p.wait("vector", st)

                @p.on("vector")
                def _(e, _j=j):
                    e.tensor_copy(wTf[:, _j * 128:(_j + 1) * 128], psw0[:, 0:128])
            wst_free[slot] = (st.n, 0)
            p.wait("sync", sv)

            @p.on("sync")
            def _(e, _nm=nm, _r=r):
                base = c.GF_OFF[_nm] + _r * c.D * 128
                e.dma_start(out=io["gin_f"][base:base + c.D * 128]
                            .rearrange("(p x) -> p x", p=128),
                            in_=wTf[:, :]).then_inc(stf.h, 16)
            stf.inc(16)
            wTf_free = stf.n
        else:
            if not attn_done:
                attn_done = True
                p.wait("gpsimd", stf)

                @p.on("gpsimd")
                def _(e):
                    e.collective_compute(
                        "AllGather", OP.bypass,
                        replica_groups=[list(range(c.NC))],
                        ins=[io["gin_f"].ap().opt()],
                        outs=[io["gout_f"].ap().opt()],
                    ).then_inc(scc.h, 1)
                scc.inc(1)
            p.wait_val("vector", sp, wbf_free[slot])
            _wgt_quant(p, c, wstage[:, sb0:sb0 + W], W, tl, wbf[:, sb0:sb0 + W])
            wst_free[slot] = (0, sv.n)
            p.wait("gpsimd", sv)
            p.wait_val("gpsimd", st, wbf2_free[slot])

            @p.on("gpsimd")
            def _(e, _W=W, _b=sb0):
                e.tensor_copy(wbf2[:, _b:_b + _W].bitcast(U16),
                              wbf[:, _b:_b + _W].bitcast(U16)).then_inc(sp.h, 1)
            sp.inc()
            wbf_free[slot] = sp.n
            p.wait("tensor", sp)
            for j in range(nT):
                p.wait("tensor", sa)
                if j == 0:
                    p.wait_val("scalar", stb, wTb_free)

                @p.on("tensor")
                def _(e, _j=j, _b=sb0):
                    e.matmul(psw1[:, 0:128],
                             wbf2[:, _b + _j * 128:_b + (_j + 1) * 128],
                             idb_t[:, :], is_transpose=True,
                             start=True, stop=True).then_inc(st.h, 1)
                st.inc()
                p.wait("scalar", st)

                @p.on("scalar")
                def _(e, _j=j):
                    e.activation(out=wTb[:, _j * 128:(_j + 1) * 128],
                                 in_=psw1[:, 0:128], func=AF.Copy).then_inc(sa.h, 1)
                sa.inc()
            wbf2_free[slot] = st.n
            p.wait("sync", sa)

            @p.on("sync")
            def _(e, _nm=nm, _r=r, _c0=c0, _W=W):
                width = c.FSH if _nm in ("gq", "uq") else c.DSH
                rows = c.D if _nm in ("gq", "uq") else c.I
                base = c.GB_OFF[_nm]
                piece = io["gin_b"][base:base + rows * width].rearrange(
                    "(dd pp w) -> pp dd w", pp=128, w=width)
                pp = piece[:, _c0 // 128:(_c0 + _W) // 128,
                           _r * 128:(_r + 1) * 128]
                e.dma_start(out=pp,
                            in_=wTb[:, 0:_W].rearrange("p (dd o) -> p dd o", o=128)
                            ).then_inc(stb.h, 16)
            stb.inc(16)
            wTb_free = stb.n

    p.wait("gpsimd", stb)

    @p.on("gpsimd")
    def _(e):
        e.collective_compute(
            "AllGather", OP.bypass, replica_groups=[list(range(c.NC))],
            ins=[io["gin_b"].ap().opt()], outs=[io["gout_b"].ap().opt()],
        ).then_inc(scc.h, 1)
    scc.inc(1)
    p.play(nc)
    _ES.close()


def _phase_a(nc, c, io, S, gfw, idf_t, bias_t, wmT, ones_f, kvacc, QM, nKB):
    sv, sa, st, sp, scc = (S[k] for k in ("sv", "sa", "st", "sp", "scc"))
    TB, nTT, nD, H, HP = c.TBLK, c.nTT, c.nD, c.H, c.HP
    _ES = contextlib.ExitStack()
    xst = _ES.enter_context(nc.sbuf_tensor("xst", [128, 2 * c.D], F32))
    xf = _ES.enter_context(nc.sbuf_tensor("xf", [128, nD * TB], F32))
    pf = _ES.enter_context(nc.sbuf_tensor("pf", [128, nD * TB], F32))
    wc = _ES.enter_context(nc.sbuf_tensor("wc", [128, 2 * nD * 128], F32))
    kmB = _ES.enter_context(nc.sbuf_tensor("kmB", [128, nTT * H * 64], F32))
    vt = _ES.enter_context(nc.sbuf_tensor("vt", [128, 2 * c.D], F32))
    etmp1 = _ES.enter_context(nc.sbuf_tensor("etmp1", [128, 1024], F32))
    etmp2 = etmp1
    psA = _ES.enter_context(nc.psum_tensor("psA", [128, 512], F32))
    psB = _ES.enter_context(nc.psum_tensor("psB", [128, 512], F32))
    psTr = _ES.enter_context(nc.psum_tensor("psTr", [128, 512], F32))
    psKS = _ES.enter_context(nc.psum_tensor("psKS", [128, 512], F32))
    psKV = [_ES.enter_context(nc.psum_tensor("psKV%d" % i, [128, 512], F32))
            for i in range(nKB)]

    p = Prog(S)
    barrier(p, [S["sco"], sv, sa, st, sp], ENGS)

    ast = dict(vt_free=[0, 0], x_t={}, xst_free={0: 0, 1: 0})
    for tb in range(c.nB):
        _attn_block(p, c, io, S, tb, gfw, idf_t, bias_t, wmT, ones_f, kvacc,
                    QM, ast, xst, xf, pf, wc, kmB, vt, etmp1, etmp2,
                    psA, psB, psTr, psKS, psKV, nKB)

    barrier(p, [S["w0"], S["w1"], S["x0"], S["x1"], sv, sa, st, sp], ENGS)
    p.play(nc)
    _ES.close()


def _a1_loads(p, c, io, S, tb, ast, xst, tts):
    st = S["st"]
    t0 = tb * c.TBLK
    for tt in tts:
        slot = tt % 2
        sx = S["x0"] if slot == 0 else S["x1"]
        p.wait_val("sync", st, ast["xst_free"][slot])

        @p.on("sync")
        def _(e, _tt=tt, _sx=sx, _slot=slot, _t0=t0):
            e.dma_start(out=xst[:, _slot * c.D:(_slot + 1) * c.D],
                        in_=io["x_own"][_t0 + _tt * 128: _t0 + (_tt + 1) * 128, :]
                        ).then_inc(_sx.h, 16)
        sx.inc(16)
        ast["x_t"][(tb, tt)] = sx.n


def _attn_block(p, c, io, S, tb, gfw, idf_t, bias_t, wmT, ones_f, kvacc, QM,
                ast, xst, xf, pf, wc, kmB, vt, etmp1, etmp2,
                psA, psB, psTr, psKS, psKV, nKB):
    sv, sa, st, sp, scc = (S[k] for k in ("sv", "sa", "st", "sp", "scc"))
    TB, nTT, nD, H, HP = c.TBLK, c.nTT, c.nD, c.H, c.HP
    t0 = tb * TB

    a1_free = {0: sa.n, 1: sa.n}
    for tt in range(nTT):
        if (tb, tt) not in ast["x_t"]:
            _a1_loads(p, c, io, S, tb, ast, xst, [tt])
        slot = tt % 2
        sx = S["x0"] if slot == 0 else S["x1"]
        p.wait_val("tensor", sx, ast["x_t"].pop((tb, tt)))
        p.wait("tensor", sv)
        for dd in range(nD):
            tb2 = dd % 2
            trb = psTr if tb2 == 0 else psKS
            p.wait_val("tensor", sa, a1_free[tb2])

            @p.on("tensor")
            def _(e, _dd=dd, _trb=trb, _slot=slot):
                e.matmul(_trb[:, 0:128],
                         xst[:, _slot * c.D + _dd * 128:
                             _slot * c.D + (_dd + 1) * 128],
                         idf_t[:, :], is_transpose=True,
                         start=True, stop=True).then_inc(st.h, 1)
            st.inc()
            p.wait("scalar", st)

            @p.on("scalar")
            def _(e, _tt=tt, _dd=dd, _trb=trb):
                col = _dd * TB + _tt * 128
                e.activation(out=xf[:, col:col + 128], in_=_trb[:, 0:128],
                             func=AF.Copy).then_inc(sa.h, 1)
            sa.inc()
            a1_free[tb2] = sa.n
        ast["xst_free"][slot] = st.n

    def fm_proj(w_nm, bidx, dest):
        bank_free = {0: (sa.n, sv.n), 1: (sa.n, sv.n)}
        wbuf_free = {0: st.n, 1: st.n}
        for oc in range(nD):
            wb = oc % 2
            sw = S["w0"] if wb == 0 else S["w1"]
            qeng = "sync" if (wb == 0 or not USE_2Q_A) else "scalar"
            p.wait_val(qeng, st, wbuf_free[wb])
            if tb == 0 and oc <= 1 and w_nm == "qT":
                p.wait_val(qeng, scc, 1)

            @p.on(qeng)
            def _(e, _nm=w_nm, _oc=oc, _wb=wb, _sw=sw):
                e.dma_start(out=wc[:, _wb * nD * 128:(_wb + 1) * nD * 128]
                            .rearrange("p (dd o) -> p dd o", o=128),
                            in_=gfw(_nm, _oc)).then_inc(_sw.h, 16)
            sw.inc(16)
            ps = psA if wb == 0 else psB
            p.wait("tensor", sw)
            sa_t, sv_t = bank_free[wb]
            p.wait_val("tensor", sa, sa_t)
            p.wait_val("tensor", sv, sv_t)

            @p.on("tensor")
            def _(e, _wb=wb, _ps=ps):
                for dd in range(nD):
                    w = wc[:, _wb * nD * 128 + dd * 128: _wb * nD * 128 + (dd + 1) * 128]
                    _li = e.matmul(_ps[:, 0:TB], w, xf[:, dd * TB:(dd + 1) * TB],
                                   start=(dd == 0), stop=(dd == nD - 1))
                _li.then_inc(st.h, 1)
            st.inc()
            wbuf_free[wb] = st.n
            p.wait("scalar", st)

            @p.on("scalar")
            def _(e, _oc=oc, _ps=ps, _bi=bidx):
                e.activation(out=dest[:, _oc * TB:(_oc + 1) * TB],
                             in_=_ps[:, 0:TB], func=AF.Identity,
                             bias=bias_t[:, _bi * c.nD + _oc: _bi * c.nD + _oc + 1]
                             ).then_inc(sa.h, 1)
            sa.inc()
            bank_free[wb] = (sa.n, 0)

    fm_proj("qT", 0, pf)
    a3_free = {0: (sv.n, sa.n), 1: (sv.n, sa.n)}
    for h in range(H):
        hp, par = h // 2, h % 2
        ps = psA if par == 0 else psB
        rows = slice(0, 64) if par == 0 else slice(64, 128)
        tpos = None if par == 0 else (0, 64)
        sv_t, sa_t = a3_free[par]
        p.wait_val("tensor", sv, sv_t)
        p.wait_val("tensor", sa, sa_t)

        @p.on("tensor")
        def _(e, _h=h, _ps=ps, _rows=rows, _tpos=tpos):
            q = pf[:, _h * TB:(_h + 1) * TB]
            e.matmul(_ps[_rows, 0:TB], wmT[:, 0:64], q, start=True, stop=True,
                     tile_position=_tpos).then_inc(st.h, 1)
        st.inc()
        p.wait("vector", st)

        eb = par * TB

        @p.on("vector")
        def _(e, _ps=ps, _rows=rows, _eb=eb):
            e.tensor_scalar(out=etmp1[_rows, _eb:_eb + TB], in0=_ps[_rows, 0:TB],
                            scalar1=0.0, scalar2=None, op0=OP.min)
        p.wait("scalar", sv)

        @p.on("scalar")
        def _(e, _rows=rows, _eb=eb):
            e.activation(out=etmp2[_rows, _eb:_eb + TB], in_=etmp1[_rows, _eb:_eb + TB],
                         func=AF.Exp).then_inc(sa.h, 1)
        sa.inc()
        p.wait("vector", sa)

        @p.on("vector")
        def _(e, _ps=ps, _rows=rows, _hp=hp, _eb=eb):
            e.scalar_tensor_tensor(out=QM[_rows, _hp * c.T + t0: _hp * c.T + t0 + TB],
                                   in0=_ps[_rows, 0:TB],
                                   scalar=0.0, in1=etmp2[_rows, _eb:_eb + TB],
                                   op0=OP.max, op1=OP.add)
        a3_free[par] = (sv.n, 0)

    fm_proj("kT", 1, pf)
    nhalf = max(H // 8, 1)
    nh = min(H, 8)
    a5_free = {0: (sv.n, sa.n), 1: (sv.n, sa.n)}
    for tt in range(nTT):
        for half in range(nhalf):
            par5 = (tt * nhalf + half) % 2
            ps = psA if par5 == 0 else psB
            eb = par5 * 512
            sv_t, sa_t = a5_free[par5]
            p.wait_val("tensor", sv, sv_t)
            p.wait_val("tensor", sa, sa_t)

            @p.on("tensor")
            def _(e, _tt=tt, _half=half, _ps=ps):
                for hh in range(nh):
                    h = _half * 8 + hh
                    k_ = pf[:, h * TB + _tt * 128: h * TB + (_tt + 1) * 128]
                    _li = e.matmul(_ps[:, hh * 64:(hh + 1) * 64], k_,
                                   wmT[:, 64:128], start=(hh == 0),
                                   stop=(hh == nh - 1), skip_group_check=True)
                _li.then_inc(st.h, 1)
            st.inc()
            W = nh * 64
            p.wait("vector", st)

            @p.on("vector")
            def _(e, _ps=ps, _eb=eb):
                e.tensor_scalar(out=etmp1[:, _eb:_eb + W], in0=_ps[:, 0:W],
                                scalar1=0.0, scalar2=None, op0=OP.min)
            p.wait("scalar", sv)

            @p.on("scalar")
            def _(e, _eb=eb):
                e.activation(out=etmp2[:, _eb:_eb + W], in_=etmp1[:, _eb:_eb + W],
                             func=AF.Exp).then_inc(sa.h, 1)
            sa.inc()
            p.wait("vector", sa)
            cb = tt * H * 64 + half * 8 * 64

            @p.on("vector")
            def _(e, _ps=ps, _cb=cb, _eb=eb):
                e.scalar_tensor_tensor(out=kmB[:, _cb:_cb + W], in0=_ps[:, 0:W],
                                       scalar=0.0, in1=etmp2[:, _eb:_eb + W],
                                       op0=OP.max, op1=OP.add)
            a5_free[par5] = (sv.n, 0)

    fm_proj("vT", 2, pf)

    if tb + 1 < c.nB:
        _a1_loads(p, c, io, S, tb + 1, ast, xst, list(range(min(2, c.nTT))))

    for tt in range(nTT):
        vs = tt % 2
        first_evac = True
        for dd in range(nD):
            p.wait("tensor", sa)

            @p.on("tensor")
            def _(e, _tt=tt, _dd=dd):
                e.matmul(psTr[:, 0:128],
                         pf[:, _dd * TB + _tt * 128: _dd * TB + (_tt + 1) * 128],
                         idf_t[:, :], is_transpose=True,
                         start=True, stop=True).then_inc(st.h, 1)
            st.inc()
            if first_evac:
                first_evac = False
                p.wait_val("scalar", st, ast["vt_free"][vs])
            p.wait("scalar", st)

            @p.on("scalar")
            def _(e, _dd=dd, _vs=vs):
                e.activation(out=vt[:, _vs * c.D + _dd * 128: _vs * c.D + (_dd + 1) * 128],
                             in_=psTr[:, 0:128], func=AF.Copy).then_inc(sa.h, 1)
            sa.inc()
        vt_ready = sa.n

        p.wait_val("tensor", sa, vt_ready)
        p.wait("tensor", sv)

        @p.on("tensor")
        def _(e, _tt=tt, _tb=tb, _vs=vs):
            for hp in range(HP):
                bank = psKV[hp // 2]
                cb0 = (hp % 2) * 256
                km = kmB[:, _tt * H * 64 + hp * 128: _tt * H * 64 + hp * 128 + 128]
                vA = vt[:, _vs * c.D + (2 * hp) * 128: _vs * c.D + (2 * hp + 1) * 128]
                vB = vt[:, _vs * c.D + (2 * hp + 1) * 128: _vs * c.D + (2 * hp + 2) * 128]
                first = _tt == 0 and hp % 2 == 0
                _li = e.matmul(bank[:, cb0:cb0 + 128], km, vA, start=first,
                         stop=False, skip_group_check=True)
                _li = e.matmul(bank[:, cb0 + 128:cb0 + 256], km, vB, start=False,
                         stop=False, skip_group_check=True)
                _li = e.matmul(psKS[:, hp:hp + 1], km, ones_f[:, 0:1],
                         start=(_tt == 0 and hp == 0), stop=False,
                         skip_group_check=True)
            _li.then_inc(st.h, 1)
        st.inc()
        ast["vt_free"][vs] = st.n

    p.wait("vector", st)

    @p.on("vector")
    def _(e):
        for b in range(nKB):
            cc = b * 512
            w = min(512, (HP - 2 * b) * 256)
            e.tensor_tensor(out=kvacc[:, cc:cc + w], in0=psKV[b][:, 0:w],
                            in1=kvacc[:, cc:cc + w], op=OP.add)
        cc = nKB * 512
        e.tensor_tensor(out=kvacc[:, cc:cc + HP], in0=psKS[:, 0:HP],
                        in1=kvacc[:, cc:cc + HP], op=OP.add)


def _phase_b(nc, c, io, S, gfw, idf_t, idb_t, bias_t, kvacc, QM, nKB):
    sv, sa, st, sp = (S[k] for k in ("sv", "sa", "st", "sp"))
    sg = S["sg"]
    TB, nTT, nD, H, HP = c.TBLK, c.nTT, c.nD, c.H, c.HP
    _ES = contextlib.ExitStack()
    kv2 = _ES.enter_context(nc.sbuf_tensor("kv2", [128, H * 128], F32))
    ks2 = _ES.enter_context(nc.sbuf_tensor("ks2", [128, 2 * HP], F32))
    rinvT = _ES.enter_context(nc.sbuf_tensor("rinvT", [128, c.nTT * H], F32))
    oat = _ES.enter_context(nc.sbuf_tensor("oat", [128, H * TB], F32))
    oa_f = _ES.enter_context(nc.sbuf_tensor("oa_f", [128, 1024], F32))
    resid = _ES.enter_context(nc.sbuf_tensor("resid", [128, nTT * c.D], F32))
    wc2 = _ES.enter_context(nc.sbuf_tensor("wc2", [128, 2 * nD * 128], F32))
    yf = _ES.enter_context(nc.sbuf_tensor("yf", [128, 2 * TB], F32))
    xres = _ES.enter_context(nc.sbuf_tensor("xres", [128, 256], F32))
    x1w = _ES.enter_context(nc.sbuf_tensor("x1w", [128, c.D], F32))
    lnm = _ES.enter_context(nc.sbuf_tensor("lnm", [128, 8], F32))
    x1q = _ES.enter_context(nc.sbuf_tensor("x1q", [128, c.D], BF16))
    x1q2 = _ES.enter_context(nc.sbuf_tensor("x1q2", [128, c.D], BF16))
    b_amax = _ES.enter_context(nc.sbuf_tensor("b_amax", [128, c.D // 32], I32))
    b_sc8 = _ES.enter_context(nc.sbuf_tensor("b_sc8", [128, c.D // 32], I32))
    b_inv = _ES.enter_context(nc.sbuf_tensor("b_inv", [128, c.D // 32], I32))
    b_m64 = _ES.enter_context(nc.sbuf_tensor("b_m64", [128, c.D // 32], I32))
    b_z = _ES.enter_context(nc.sbuf_tensor("b_z", [128, c.D], F32))
    b_q8 = _ES.enter_context(nc.sbuf_tensor("b_q8", [128, c.D], FP8))
    psA2 = _ES.enter_context(nc.psum_tensor("psA2", [128, 512], F32))
    psB2 = _ES.enter_context(nc.psum_tensor("psB2", [128, 512], F32))
    psN = _ES.enter_context(nc.psum_tensor("psN", [128, 512], F32))
    psT2 = _ES.enter_context(nc.psum_tensor("psT2", [128, 512], F32))
    psT2b = _ES.enter_context(nc.psum_tensor("psT2b", [128, 1024], BF16))
    tlB = dict(amax=b_amax, sc8=b_sc8, inv=b_inv, m64=b_m64, z=b_z, q8=b_q8)

    p = Prog(S)
    barrier(p, [sg], ENGS)

    @p.on("vector")
    def _(e):
        e.memset(kv2[:, :], 0.0)
        e.memset(ks2[:, :], 0.0)
        ks = nKB * 512
        for hp in range(HP):
            b, j = hp // 2, hp % 2
            cc = b * 512 + j * 256
            e.tensor_copy(kv2[0:64, (2 * hp) * 128:(2 * hp + 1) * 128],
                          kvacc[0:64, cc:cc + 128])
            e.tensor_copy(kv2[64:128, (2 * hp + 1) * 128:(2 * hp + 2) * 128],
                          kvacc[64:128, cc + 128:cc + 256])
            e.tensor_copy(ks2[0:64, 2 * hp:2 * hp + 1],
                          kvacc[0:64, ks + hp:ks + hp + 1])
            e.tensor_copy(ks2[64:128, 2 * hp + 1:2 * hp + 2],
                          kvacc[64:128, ks + hp:ks + hp + 1])

    bst = dict(oa_free={0: 0, 1: 0}, psN_free=S["sv"].n,
               psAB_free={0: [0, 0], 1: [0, 0]}, psT2_free=[0, 0],
               pend_ln1=[])
    for tb in range(c.nB):
        _attn_out_block(p, c, io, S, tb, gfw, idf_t, idb_t, bias_t, QM,
                        kv2, ks2, rinvT, oat, oa_f, resid, wc2, yf, xres,
                        x1w, lnm, x1q, x1q2, tlB, bst,
                        psA2, psB2, psN, psT2, psT2b)

    barrier(p, [S["w0"], S["w1"], S["r0"], S["r1"], S["ssp"], sg, sv, sa,
                st, sp], ENGS)
    p.play(nc)
    _ES.close()


def _attn_out_block(p, c, io, S, tb, gfw, idf_t, idb_t, bias_t, QM,
                    kv2, ks2, rinvT, oat, oa_f, resid, wc2, yf, xres,
                    x1w, lnm, x1q, x1q2, tlB, bst,
                    psA2, psB2, psN, psT2, psT2b):
    sv, sa, st, sp = (S[k] for k in ("sv", "sa", "st", "sp"))
    TB, nTT, nD, H, HP = c.TBLK, c.nTT, c.nD, c.H, c.HP
    t0 = tb * TB
    ssp = S["ssp"]

    pend_tr = None
    for tt in range(nTT):
        p.wait_val("tensor", sv, bst["psN_free"])

        @p.on("tensor")
        def _(e, _tt=tt):
            for hp in range(HP):
                cc = hp * c.T + t0 + _tt * 128
                _li = e.matmul(psN[:, 2 * hp:2 * hp + 2], QM[:, cc:cc + 128],
                               ks2[:, 2 * hp:2 * hp + 2], start=(hp == 0),
                               stop=(hp == HP - 1), skip_group_check=True)
            _li.then_inc(st.h, 1)
        st.inc()
        p.wait("vector", st)

        @p.on("vector")
        def _(e, _tt=tt):
            e.tensor_scalar(out=rinvT[:, _tt * H:(_tt + 1) * H], in0=psN[:, 0:H],
                            scalar1=EPS_ATT, scalar2=None, op0=OP.add)
            e.reciprocal(out=rinvT[:, _tt * H:(_tt + 1) * H],
                         in_=rinvT[:, _tt * H:(_tt + 1) * H])
        bst["psN_free"] = sv.n
        rinv_t = sv.n

        GW = min(4, H)
        ps_free = bst["psAB_free"]
        oa_free = bst["oa_free"]
        for grp in range(H // GW):
            par = grp % 2
            ob = par * 512
            ps = psA2 if par == 0 else psB2
            sv_t, sa_t = ps_free[par]
            p.wait_val("tensor", sv, sv_t)
            p.wait_val("tensor", sa, sa_t)

            @p.on("tensor")
            def _(e, _g=grp, _ps=ps, _tt=tt):
                for j in range(GW):
                    h = _g * GW + j
                    cc = (h // 2) * c.T + t0 + _tt * 128
                    _li = e.matmul(_ps[:, j * 128:(j + 1) * 128],
                                   QM[:, cc:cc + 128],
                                   kv2[:, h * 128:(h + 1) * 128],
                                   start=True, stop=True, skip_group_check=True)
                _li.then_inc(st.h, 1)
            st.inc()
            if pend_tr is not None:
                pend_tr()
                pend_tr = None
            mm_t = st.n
            p.wait_val("scalar", st, mm_t)
            p.wait_val("scalar", st, oa_free[par])
            p.wait_val("scalar", sv, rinv_t)

            @p.on("scalar")
            def _(e, _ps=ps, _tt=tt, _g=grp, _ob=ob):
                for j in range(GW):
                    hh = _tt * H + _g * GW + j
                    e.activation(out=oa_f[:, _ob + j * 128:_ob + (j + 1) * 128],
                                 in_=_ps[:, j * 128:(j + 1) * 128],
                                 func=AF.Identity,
                                 scale=rinvT[:, hh:hh + 1]).then_inc(sa.h, 1)
            sa.inc(GW)
            ps_free[par][1] = sa.n
            dv_t = sa.n

            def emit_tr(_g=grp, _tt=tt, _ob=ob, _par=par, _dt=dv_t):
                p.wait_val("tensor", sa, _dt)
                for j in range(GW):
                    p.wait_val("tensor", sa, bst["psT2_free"][0])
                    p.wait_val("tensor", sv, bst["psT2_free"][1])

                    @p.on("tensor")
                    def _(e, _j=j):
                        e.matmul(psT2[:, 0:128],
                                 oa_f[:, _ob + _j * 128:_ob + (_j + 1) * 128],
                                 idf_t[:, :], is_transpose=True,
                                 start=True, stop=True).then_inc(st.h, 1)
                    st.inc()
                    p.wait("scalar", st)

                    @p.on("scalar")
                    def _(e, _j=j):
                        col = (_g * GW + _j) * TB + _tt * 128
                        e.activation(out=oat[:, col:col + 128], in_=psT2[:, 0:128],
                                     func=AF.Copy).then_inc(sa.h, 1)
                    sa.inc()
                    bst["psT2_free"][0] = sa.n
                oa_free[_par] = st.n
            pend_tr = emit_tr
        if bst["pend_ln1"]:
            bst["pend_ln1"].pop(0)()
    if pend_tr is not None:
        pend_tr()

    bank_free = bst["psAB_free"]
    wbuf_free = {0: st.n, 1: st.n}
    yf_free = {0: st.n, 1: st.n}
    p.wait("tensor", sa)
    for oc in range(nD):
        wb = oc % 2
        sw = S["w0"] if wb == 0 else S["w1"]
        qeng = "sync" if (wb == 0 or not USE_2Q_B) else "scalar"
        p.wait_val(qeng, st, wbuf_free[wb])

        @p.on(qeng)
        def _(e, _oc=oc, _wb=wb, _sw=sw):
            e.dma_start(out=wc2[:, _wb * nD * 128:(_wb + 1) * nD * 128]
                        .rearrange("p (dd o) -> p dd o", o=128),
                        in_=gfw("oT", _oc)).then_inc(_sw.h, 16)
        sw.inc(16)
        ps = psA2 if wb == 0 else psB2
        p.wait("tensor", sw)
        sv_t, sa_t = bank_free[wb]
        p.wait_val("tensor", sa, sa_t)
        p.wait_val("tensor", sv, sv_t)

        @p.on("tensor")
        def _(e, _wb=wb, _ps=ps):
            for dd in range(nD):
                w = wc2[:, _wb * nD * 128 + dd * 128: _wb * nD * 128 + (dd + 1) * 128]
                _li = e.matmul(_ps[:, 0:TB], w, oat[:, dd * TB:(dd + 1) * TB],
                               start=(dd == 0), stop=(dd == nD - 1))
            _li.then_inc(st.h, 1)
        st.inc()
        wbuf_free[wb] = st.n
        p.wait("scalar", st)
        p.wait_val("scalar", st, yf_free[wb])

        @p.on("scalar")
        def _(e, _oc=oc, _ps=ps, _wb=wb):
            e.activation(out=yf[:, _wb * TB:_wb * TB + TB], in_=_ps[:, 0:TB],
                         func=AF.Identity,
                         bias=bias_t[:, 3 * c.nD + _oc: 3 * c.nD + _oc + 1]
                         ).then_inc(sa.h, 1)
        sa.inc()
        bank_free[wb][1] = sa.n
        yf_t = sa.n
        for tt in range(nTT):
            sr = S["r0"] if tt % 2 == 0 else S["r1"]
            p.wait("sync", sv)

            @p.on("sync")
            def _(e, _oc=oc, _tt=tt, _sr=sr):
                e.dma_start(out=xres[:, (_tt % 2) * 128:(_tt % 2) * 128 + 128],
                            in_=io["x_own"][t0 + _tt * 128: t0 + (_tt + 1) * 128,
                                            _oc * 128:(_oc + 1) * 128]
                            ).then_inc(_sr.h, 16)
            sr.inc(16)
            p.wait_val("tensor", sa, max(bst["psT2_free"][0], yf_t))
            p.wait_val("tensor", sv, bst["psT2_free"][1])

            @p.on("tensor")
            def _(e, _tt=tt, _wb=wb):
                e.matmul(psT2[:, 0:128],
                         yf[:, _wb * TB + _tt * 128: _wb * TB + (_tt + 1) * 128],
                         idf_t[:, :], is_transpose=True,
                         start=True, stop=True).then_inc(st.h, 1)
            st.inc()
            p.wait("vector", st)
            p.wait("vector", sr)

            @p.on("vector")
            def _(e, _oc=oc, _tt=tt):
                e.scalar_tensor_tensor(
                    out=resid[:, _tt * c.D + _oc * 128: _tt * c.D + (_oc + 1) * 128],
                    in0=psT2[:, 0:128], scalar=1.0,
                    in1=xres[:, (_tt % 2) * 128:(_tt % 2) * 128 + 128],
                    op0=OP.mult, op1=OP.add)
            bst["psT2_free"][1] = sv.n
        yf_free[wb] = st.n

    for tt in range(nTT):
        def ln1_tt(tt=tt):
            p.wait("vector", ssp)
            p.wait("vector", sp)

            @p.on("vector")
            def _(e, _tt=tt):
                r = resid[:, _tt * c.D:(_tt + 1) * c.D]
                e.tensor_reduce(out=lnm[:, 0:1], in_=r, axis=AX, op=OP.add)
                e.tensor_scalar(out=lnm[:, 0:1], in0=lnm[:, 0:1],
                                scalar1=1.0 / c.D, scalar2=None, op0=OP.mult)
                e.tensor_scalar(out=x1w[:, :], in0=r, scalar1=lnm[:, 0:1],
                                scalar2=None, op0=OP.subtract)
            p.wait("scalar", sv)

            @p.on("scalar")
            def _(e):
                e.activation(out=tlB["z"][:, 0:c.D], in_=x1w[:, :], func=AF.Square,
                             accum_out=lnm[:, 1:2]).then_inc(sa.h, 1)
            sa.inc()
            p.wait("vector", sa)

            @p.on("vector")
            def _(e):
                e.tensor_scalar(out=lnm[:, 1:2], in0=lnm[:, 1:2],
                                scalar1=1.0 / c.D, scalar2=EPS_LN,
                                op0=OP.mult, op1=OP.add)
            p.wait("scalar", sv)

            @p.on("scalar")
            def _(e):
                e.activation(out=lnm[:, 2:3], in_=lnm[:, 1:2],
                             func=AF.Sqrt).then_inc(sa.h, 1)
            sa.inc()
            p.wait("vector", sa)

            @p.on("vector")
            def _(e):
                e.reciprocal(out=lnm[:, 3:4], in_=lnm[:, 2:3])
                e.tensor_scalar(out=x1w[:, :], in0=x1w[:, :], scalar1=lnm[:, 3:4],
                                scalar2=None, op0=OP.mult)
            p.wait("sync", sv)

            @p.on("sync")
            def _(e, _tt=tt):
                e.dma_start(out=io["x1_sp"][t0 + _tt * 128: t0 + (_tt + 1) * 128, :],
                            in_=x1w[:, :]).then_inc(ssp.h, 16)
            ssp.inc(16)
            _act_quant(p, c, x1w[:, :], c.D, tlB, x1q)
            p.wait("gpsimd", sv)
            p.wait("gpsimd", ssp)

            @p.on("gpsimd")
            def _(e):
                e.tensor_copy(x1q2[:, :].bitcast(U16), x1q[:, :].bitcast(U16)
                              ).then_inc(sp.h, 1)
            sp.inc()
            p.wait("sync", sp)

            @p.on("sync")
            def _(e, _tt=tt):
                e.dma_start(out=io["x1q_sp"][t0 + _tt * 128: t0 + (_tt + 1) * 128, :],
                            in_=x1q2[:, :]).then_inc(ssp.h, 16)
            ssp.inc(16)
        bst["pend_ln1"].append(ln1_tt)
    if tb == c.nB - 1:
        while bst["pend_ln1"]:
            bst["pend_ln1"].pop(0)()


def _phase_f(nc, c, io, S, idb_t):
    NRES = min(21, c.nI // 2)
    fctx = {"nres": NRES, "wres_t": 0}
    with contextlib.ExitStack() as fes:
        wres = (fes.enter_context(
            nc.sbuf_tensor("wres", [128, NRES * c.D], BF16)) if NRES else None)
        _phase_f1(nc, c, io, S, idb_t, wres, fctx)
        _phase_f2(nc, c, io, S, wres, fctx)


def _phase_f1(nc, c, io, S, idb_t, wres, fctx):
    sv, sa, st, sp, scc = (S[k] for k in ("sv", "sa", "st", "sp", "scc"))
    sg = S["sg"]
    SB, nST, nD, nI, nOS, OSW = c.SB, c.nST, c.nD, c.nI, c.nOS, c.OSW
    TH = max(c.T // 2, 128)
    nTC = TH // 128
    _ES = contextlib.ExitStack()
    wgs = _ES.enter_context(nc.sbuf_tensor("wgs", [128, 2 * nD * OSW], BF16))
    wus = _ES.enter_context(nc.sbuf_tensor("wus", [128, 2 * nD * OSW], BF16))
    x1qTh = _ES.enter_context(nc.sbuf_tensor("x1qTh", [128, nD * TH], BF16))
    intw = _ES.enter_context(nc.sbuf_tensor("intw", [128, 2 * OSW], F32))
    iq = _ES.enter_context(nc.sbuf_tensor("iq", [128, 2 * OSW], BF16))
    iq2 = _ES.enter_context(nc.sbuf_tensor("iq2", [128, 2 * OSW], BF16))
    f_amax = _ES.enter_context(nc.sbuf_tensor("f_amax", [128, OSW // 16], I32))
    f_sc8 = _ES.enter_context(nc.sbuf_tensor("f_sc8", [128, OSW // 16], I32))
    f_inv = _ES.enter_context(nc.sbuf_tensor("f_inv", [128, OSW // 16], I32))
    f_m64 = _ES.enter_context(nc.sbuf_tensor("f_m64", [128, OSW // 16], I32))
    f_z = _ES.enter_context(nc.sbuf_tensor("f_z", [128, 2 * OSW], F32))
    f_q8 = _ES.enter_context(nc.sbuf_tensor("f_q8", [128, 2 * OSW], FP8))
    iqs = _ES.enter_context(nc.sbuf_tensor("iqs", [128, 2 * OSW], BF16))
    psb = [_ES.enter_context(nc.psum_tensor("psf%d" % i, [128, 512], F32))
           for i in range(6)]
    psT = [_ES.enter_context(nc.psum_tensor("psq%d" % i, [128, 1024], BF16))
           for i in range(2)]

    def _emit_wres(p):
        stf = S["stf"]
        p.wait("scalar", S["scc"])

        @p.on("scalar")
        def _(e):
            for ii in range(fctx["nres"]):
                base = c.GB_OFF["dq"] + ii * 128 * c.DSH
                view = io["gout_b"].rearrange("(rk x) -> rk x", x=c.GBN)
                ap = view[:, base:base + 128 * c.DSH].rearrange(
                    "rk (pp w) -> pp rk w", pp=128)
                e.dma_start(out=wres[:, ii * c.D:(ii + 1) * c.D]
                            .rearrange("p (rk w) -> p rk w", w=c.DSH),
                            in_=ap).then_inc(stf.h, 16)
        stf.inc(16 * fctx["nres"])
        fctx["wres_t"] = stf.n

    p = Prog(S)
    barrier(p, [S["ssp"], S["w0"], S["w1"], S["r0"], S["r1"], sg, sv, sa,
                st, sp], ENGS)
    p.wait("sync", scc)
    p.wait("tensor", scc)
    if fctx["nres"]:
        _emit_wres(p)

    tls = []
    for s in range(2):
        nb = OSW // 32
        tls.append(dict(
            amax=f_amax[:, s * nb:(s + 1) * nb],
            sc8=f_sc8[:, s * nb:(s + 1) * nb],
            inv=f_inv[:, s * nb:(s + 1) * nb],
            m64=f_m64[:, s * nb:(s + 1) * nb],
            z=f_z[:, s * OSW:(s + 1) * OSW],
            q8=f_q8[:, s * OSW:(s + 1) * OSW]))

    fst = dict(ps_free=[(0, 0), (0, 0), (0, 0)],
               intw_free=[0, 0], iq_free=[0, 0], iq2_free=[0, 0],
               xh_free=0, gu_free=[0, 0], k=0, pend=[], psT_free=[0, 0],
               stg_free=[0, 0],
               wd_sem=[S["wd0"], S["wd1"]], wd_buf_free=[0, 0], wd_k=0)
    sgu = S["gu0"]
    sgu2 = S["gu1"]
    sxq = S["sxq"]

    for th in range(c.T // TH):
        ht0 = th * TH
        p.wait_val("sync", st, fst["xh_free"])
        p.wait("sync", S["ssp"])

        @p.on("sync")
        def _(e, _ht0=ht0):
            for dd in range(nD):
                e.dma_start(out=x1qTh[:, dd * TH:(dd + 1) * TH],
                            in_=io["x1q_sp"][_ht0:_ht0 + TH,
                                             dd * 128:(dd + 1) * 128],
                            transpose=True).then_inc(sxq.h, 16)
        sxq.inc(16 * nD)
        xq_t = sxq.n

        for osx in range(nOS):
            gpar = osx % 2
            gsem = sgu if gpar == 0 else sgu2
            p.wait_val("sync", st, fst["gu_free"][gpar])

            @p.on("sync")
            def _(e, _osx=osx, _gpar=gpar, _gs=gsem):
                rk = (_osx * OSW) // c.FSH
                lo = (_osx * OSW) % c.FSH
                for nm, buf in (("gq", wgs), ("uq", wus)):
                    base = rk * c.GBN + c.GB_OFF[nm]
                    piece = io["gout_b"][base:base + c.D * c.FSH].rearrange(
                        "(dd pp w) -> pp dd w", pp=128, w=c.FSH)
                    e.dma_start(
                        out=buf[:, _gpar * nD * OSW:(_gpar + 1) * nD * OSW]
                        .rearrange("p (dd o) -> p dd o", o=OSW),
                        in_=piece[:, :, lo:lo + OSW]).then_inc(_gs.h, 16)
            gsem.inc(16 * 2)

            for tc in range(nTC):
                k = fst["k"]
                kset = k % 3
                sl = k % 2
                pg, pu = psb[2 * kset], psb[2 * kset + 1]
                while len(fst["pend"]) >= 2:
                    fst["pend"].pop(0)()
                p.wait_val("tensor", sxq, xq_t)
                p.wait("tensor", gsem)
                sa_t, sv_t = fst["ps_free"][kset]
                p.wait_val("tensor", sa, sa_t)
                p.wait_val("tensor", sv, sv_t)

                @p.on("tensor")
                def _(e, _tc=tc, _gpar=gpar, _pg=pg, _pu=pu):
                    for dd in range(nD):
                        xw = x1qTh[:, dd * TH + _tc * 128: dd * TH + (_tc + 1) * 128]
                        gw = wgs[:, _gpar * nD * OSW + dd * OSW:
                                 _gpar * nD * OSW + (dd + 1) * OSW]
                        uw = wus[:, _gpar * nD * OSW + dd * OSW:
                                 _gpar * nD * OSW + (dd + 1) * OSW]
                        _li = e.matmul(_pg[:, 0:OSW], xw, gw, start=(dd == 0),
                                       stop=(dd == nD - 1))
                        _li = e.matmul(_pu[:, 0:OSW], xw, uw, start=(dd == 0),
                                       stop=(dd == nD - 1))
                    _li.then_inc(st.h, 1)
                st.inc()
                mm_t = st.n
                fst["gu_free"][gpar] = st.n

                p.wait_val("scalar", st, mm_t)
                p.wait_val("scalar", sv, fst["intw_free"][sl])

                @p.on("scalar")
                def _(e, _sl=sl, _pg=pg):
                    e.activation(out=intw[:, _sl * OSW:(_sl + 1) * OSW],
                                 in_=_pg[:, 0:OSW],
                                 func=AF.Silu).then_inc(sa.h, 1)
                sa.inc()
                silu_t = sa.n
                p.wait_val("vector", sa, silu_t)

                @p.on("vector")
                def _(e, _sl=sl, _pu=pu):
                    e.tensor_tensor(out=intw[:, _sl * OSW:(_sl + 1) * OSW],
                                    in0=intw[:, _sl * OSW:(_sl + 1) * OSW],
                                    in1=_pu[:, 0:OSW], op=OP.mult)
                fst["ps_free"][kset] = (silu_t, sv.n)

                p.wait_val("vector", sp, fst["iq_free"][sl])
                _act_quant(p, c, intw[:, sl * OSW:(sl + 1) * OSW], OSW,
                           tls[sl], iq[:, sl * OSW:(sl + 1) * OSW])
                fst["intw_free"][sl] = sv.n
                quant_t = sv.n

                p.wait_val("gpsimd", sv, quant_t)
                p.wait_val("gpsimd", st, fst["iq2_free"][sl])

                @p.on("gpsimd")
                def _(e, _sl=sl):
                    e.tensor_copy(iq2[:, _sl * OSW:(_sl + 1) * OSW].bitcast(U16),
                                  iq[:, _sl * OSW:(_sl + 1) * OSW].bitcast(U16)
                                  ).then_inc(sp.h, 1)
                sp.inc()
                fst["iq_free"][sl] = sp.n
                laund_t = sp.n

                def emit_tr(_sl=sl, _osx=osx, _tcol=ht0 + tc * 128,
                            _pslot=fst["k"] % 2, _lt=laund_t):
                    _psq = psT[_pslot]
                    _ssi = S["si0"] if _pslot == 0 else S["si1"]
                    nj = OSW // 128
                    p.wait_val("tensor", sp, _lt)
                    p.wait_val("tensor", sa, fst["psT_free"][_pslot])

                    @p.on("tensor")
                    def _(e):
                        for j in range(nj):
                            _li = e.matmul(_psq[:, j * 128:(j + 1) * 128],
                                           iq2[:, _sl * OSW + j * 128:
                                               _sl * OSW + (j + 1) * 128],
                                           idb_t[:, :], is_transpose=True,
                                           start=True, stop=True)
                        _li.then_inc(st.h, 1)
                    st.inc()
                    fst["iq2_free"][_sl] = st.n
                    p.wait_val("scalar", st, st.n)
                    p.wait_val("scalar", _ssi, fst["stg_free"][_pslot])

                    @p.on("scalar")
                    def _(e):
                        e.activation(out=iqs[:, _pslot * OSW:(_pslot + 1) * OSW],
                                     in_=_psq[:, 0:OSW],
                                     func=AF.Copy).then_inc(sa.h, 1)
                    sa.inc()
                    fst["psT_free"][_pslot] = sa.n
                    p.wait_val("sync", sa, sa.n)

                    @p.on("sync")
                    def _(e):
                        e.dma_start(
                            out=io["iq_sp"][_osx * OSW:(_osx + 1) * OSW,
                                            _tcol:_tcol + 128]
                            .rearrange("(j p) t -> p j t", p=128),
                            in_=iqs[:, _pslot * OSW:(_pslot + 1) * OSW]
                            .rearrange("p (j t) -> p j t", t=128)
                            ).then_inc(_ssi.h, 16)
                    _ssi.inc(16)
                    fst["stg_free"][_pslot] = _ssi.n
                fst["pend"].append(emit_tr)
                fst["k"] += 1
        fst["xh_free"] = st.n
    while fst["pend"]:
        fst["pend"].pop(0)()

    p.play(nc)
    _ES.close()


def _phase_f2(nc, c, io, S, wres, fctx):
    sv, sa, st, sp = (S[k] for k in ("sv", "sa", "st", "sp"))
    SB, nST, nD, nI = c.SB, c.nST, c.nD, c.nI
    _ES = contextlib.ExitStack()
    iqT = _ES.enter_context(nc.sbuf_tensor("iqT", [128, 2 * nI * SB], BF16))
    wds = _ES.enter_context(nc.sbuf_tensor("wds", [128, 2 * c.D], BF16))
    res2 = _ES.enter_context(nc.sbuf_tensor("res2", [128, nST * c.D], F32))
    x1res = _ES.enter_context(nc.sbuf_tensor("x1res", [128, 2 * c.D], F32))
    f_lnm = _ES.enter_context(nc.sbuf_tensor("f_lnm", [128, 8], F32))
    x2w = _ES.enter_context(nc.sbuf_tensor("x2w", [128, c.D], F32))
    psb = [_ES.enter_context(nc.psum_tensor("psg%d" % i, [128, 512], F32))
           for i in range(8)]
    fst = dict(wd_sem=[S["wd0"], S["wd1"]], wd_buf_free=[0, 0], wd_k=0)
    iq_sems = [(S["sxq"], S["skv"]), (S["sj0"], S["sj1"])]
    p = Prog(S)
    barrier(p, [S["ssp"], S["si0"], S["si1"], S["sxq"], S["skv"],
                S["sj0"], S["sj1"], S["gu0"], S["gu1"], sv, sa, st, sp], ENGS)
    dn_state = dict(iqs_free=[0, 0], iqT_t=[0, 0], dn_free=(sa.n, sv.n))

    def iqt_prefetch(t0p, qs):
        pair = iq_sems[qs]

        @p.on("sync")
        def _(e, _t0=t0p, _qs=qs, _ss=pair[0]):
            e.dma_start(out=iqT[:, _qs * nI * SB:(_qs + 1) * nI * SB]
                        .rearrange("p (ii t) -> p ii t", t=SB),
                        in_=io["iq_sp"].rearrange("(ii p) t -> p ii t", p=128)
                        [:, :, _t0:_t0 + SB]).then_inc(_ss.h, 16)
        pair[0].inc(16)
        dn_state["iqT_t"][qs] = (pair[0].n, pair[1].n)

    iqt_prefetch(0, 0)
    for sb in range(c.nSB):
        t0 = sb * SB
        qs = sb % 2
        if sb + 1 < c.nSB:
            nqs = (sb + 1) % 2
            p.wait_val("sync", st, dn_state["iqs_free"][nqs])
            iqt_prefetch(t0 + SB, nqs)
        iqT_t, iqT_t2 = dn_state["iqT_t"][qs]
        sxq, sxq2 = iq_sems[qs]

        SPW = min(512, c.D)
        nSpan = c.D // SPW
        banks = psb[:nST * nSpan]
        sa_t, sv_t = dn_state["dn_free"]
        p.wait_val("tensor", sa, sa_t)
        p.wait_val("tensor", sv, sv_t)
        p.wait_val("tensor", sxq, iqT_t)
        p.wait_val("tensor", sxq2, iqT_t2)
        nres = fctx["nres"]
        if sb == 0 and nres:
            p.wait_val("tensor", S["stf"], fctx["wres_t"])
        for ii in range(nI):
            if ii < nres:
                @p.on("tensor")
                def _(e, _ii=ii, _qs=qs):
                    for t1 in range(nST):
                        for spn in range(nSpan):
                            bank = banks[t1 * nSpan + spn]
                            w = wres[:, _ii * c.D + spn * SPW:
                                     _ii * c.D + (spn + 1) * SPW]
                            _li = e.matmul(bank[:, 0:SPW],
                                           iqT[:, _qs * nI * SB + _ii * SB + t1 * 128:
                                               _qs * nI * SB + _ii * SB + (t1 + 1) * 128],
                                           w, start=(_ii == 0), stop=(_ii == nI - 1),
                                           skip_group_check=True)
                    _li.then_inc(st.h, 1)
                st.inc()
                continue
            wpar = fst["wd_k"] % 2
            swd = fst["wd_sem"][wpar]
            qeng = "sync" if (wpar == 0 or not USE_2Q_F) else "scalar"
            p.wait_val(qeng, st, fst["wd_buf_free"][wpar])

            @p.on(qeng)
            def _(e, _ii=ii, _wpar=wpar, _swd=swd):
                base = c.GB_OFF["dq"] + _ii * 128 * c.DSH
                view = io["gout_b"].rearrange("(rk x) -> rk x", x=c.GBN)
                ap = view[:, base:base + 128 * c.DSH].rearrange(
                    "rk (pp w) -> pp rk w", pp=128)
                e.dma_start(out=wds[:, _wpar * c.D:(_wpar + 1) * c.D]
                            .rearrange("p (rk w) -> p rk w", w=c.DSH),
                            in_=ap).then_inc(_swd.h, 16)
            swd.inc(16)
            p.wait("tensor", swd)

            @p.on("tensor")
            def _(e, _ii=ii, _wpar=wpar, _qs=qs):
                for t1 in range(nST):
                    for spn in range(nSpan):
                        bank = banks[t1 * nSpan + spn]
                        w = wds[:, _wpar * c.D + spn * SPW:
                                _wpar * c.D + (spn + 1) * SPW]
                        _li = e.matmul(bank[:, 0:SPW],
                                       iqT[:, _qs * nI * SB + _ii * SB + t1 * 128:
                                           _qs * nI * SB + _ii * SB + (t1 + 1) * 128],
                                       w, start=(_ii == 0), stop=(_ii == nI - 1),
                                       skip_group_check=True)
                _li.then_inc(st.h, 1)
            st.inc()
            fst["wd_buf_free"][wpar] = st.n
            fst["wd_k"] += 1
        dn_t = st.n
        dn_state["iqs_free"][qs] = st.n

        for t1 in range(nST):
            sf = S["f0"] if t1 % 2 == 0 else S["f1"]
            p.wait("sync", sv)

            @p.on("sync")
            def _(e, _t1=t1, _sf=sf, _t0=t0):
                e.dma_start(out=x1res[:, (_t1 % 2) * c.D:(_t1 % 2) * c.D + c.D],
                            in_=io["x1_sp"][_t0 + _t1 * 128: _t0 + (_t1 + 1) * 128, :]
                            ).then_inc(_sf.h, 16)
            sf.inc(16)
            p.wait_val("vector", st, dn_t)
            p.wait("vector", sf)

            @p.on("vector")
            def _(e, _t1=t1):
                for spn in range(nSpan):
                    bank = banks[_t1 * nSpan + spn]
                    e.scalar_tensor_tensor(
                        out=res2[:, _t1 * c.D + spn * SPW: _t1 * c.D + (spn + 1) * SPW],
                        in0=bank[:, 0:SPW], scalar=1.0,
                        in1=x1res[:, (_t1 % 2) * c.D + spn * SPW:
                                  (_t1 % 2) * c.D + (spn + 1) * SPW],
                        op0=OP.mult, op1=OP.add)
        dn_state["dn_free"] = (0, sv.n)

        so = S["so"]
        for t1 in range(nST):
            p.wait("vector", so)

            @p.on("vector")
            def _(e, _t1=t1):
                r = res2[:, _t1 * c.D:(_t1 + 1) * c.D]
                e.tensor_reduce(out=f_lnm[:, 0:1], in_=r, axis=AX, op=OP.add)
                e.tensor_scalar(out=f_lnm[:, 0:1], in0=f_lnm[:, 0:1],
                                scalar1=1.0 / c.D, scalar2=None, op0=OP.mult)
                e.tensor_scalar(out=x2w[:, :], in0=r, scalar1=f_lnm[:, 0:1],
                                scalar2=None, op0=OP.subtract)
            p.wait("scalar", sv)

            @p.on("scalar")
            def _(e, _t1=t1):
                e.activation(out=x1res[:, (_t1 % 2) * c.D:(_t1 % 2) * c.D + c.D],
                             in_=x2w[:, :],
                             func=AF.Square, accum_out=f_lnm[:, 1:2]).then_inc(sa.h, 1)
            sa.inc()
            p.wait("vector", sa)

            @p.on("vector")
            def _(e):
                e.tensor_scalar(out=f_lnm[:, 1:2], in0=f_lnm[:, 1:2],
                                scalar1=1.0 / c.D, scalar2=EPS_LN,
                                op0=OP.mult, op1=OP.add)
            p.wait("scalar", sv)

            @p.on("scalar")
            def _(e):
                e.activation(out=f_lnm[:, 2:3], in_=f_lnm[:, 1:2],
                             func=AF.Sqrt).then_inc(sa.h, 1)
            sa.inc()
            p.wait("vector", sa)

            @p.on("vector")
            def _(e):
                e.reciprocal(out=f_lnm[:, 3:4], in_=f_lnm[:, 2:3])
                e.tensor_scalar(out=x2w[:, :], in0=x2w[:, :], scalar1=f_lnm[:, 3:4],
                                scalar2=None, op0=OP.mult)
            p.wait("sync", sv)

            @p.on("sync")
            def _(e, _t1=t1, _t0=t0):
                e.dma_start(out=io["out"][_t0 + _t1 * 128: _t0 + (_t1 + 1) * 128, :],
                            in_=x2w[:, :]).then_inc(so.h, 16)
            so.inc(16)

    barrier(p, [S["so"], S["f0"], S["f1"], S["wd0"], S["wd1"], S["gu0"],
                S["sxq"], S["skv"], S["sj0"], S["sj1"], S["ssp"],
                sv, sa, st, sp], ["sync"])
    p.play(nc)
    _ES.close()


_NC_CACHE = {}


def _get_nc(full):
    if full not in _NC_CACHE:
        _NC_CACHE[full] = build_nc(CFG(full))
    return _NC_CACHE[full]


def kernel(L0_attn_hidden_states, L0_attn_wq, L0_attn_bq, L0_attn_wk,
           L0_attn_bk, L0_attn_wv, L0_attn_bv, L0_attn_wo, L0_attn_bo,
           L0_attn_wmap_q, L0_attn_wmap_k, L0_ffn_w_gate, L0_ffn_w_up,
           L0_ffn_w_down):
    x = np.ascontiguousarray(np.asarray(L0_attn_hidden_states, dtype=np.float32))
    B, Sq, D = x.shape
    c = CFG(full=True)
    assert (B, Sq, D) == (4, 4096, 2048)
    wq = np.ascontiguousarray(np.asarray(L0_attn_wq, dtype=np.float32))
    wk = np.ascontiguousarray(np.asarray(L0_attn_wk, dtype=np.float32))
    wv = np.ascontiguousarray(np.asarray(L0_attn_wv, dtype=np.float32))
    wo = np.ascontiguousarray(np.asarray(L0_attn_wo, dtype=np.float32))
    wg = np.ascontiguousarray(np.asarray(L0_ffn_w_gate, dtype=np.float32))
    wu = np.ascontiguousarray(np.asarray(L0_ffn_w_up, dtype=np.float32))
    wd = np.ascontiguousarray(np.asarray(L0_ffn_w_down, dtype=np.float32))
    biases = np.stack([np.asarray(L0_attn_bq, dtype=np.float32),
                       np.asarray(L0_attn_bk, dtype=np.float32),
                       np.asarray(L0_attn_bv, dtype=np.float32),
                       np.asarray(L0_attn_bo, dtype=np.float32)], axis=0)
    wmaps = np.concatenate([np.asarray(L0_attn_wmap_q, dtype=np.float32),
                            np.asarray(L0_attn_wmap_k, dtype=np.float32)], axis=0)
    idf = np.eye(128, dtype=np.float32)
    idb = np.eye(128, dtype=np.float32).astype(ml_dtypes.bfloat16)

    nc = _get_nc(True)
    in_maps = []
    for core in range(c.NC):
        b, half = core // 2, core % 2
        in_maps.append({
            "x_own": np.ascontiguousarray(x[b, half * c.T:(half + 1) * c.T, :]),
            "wq_s": np.ascontiguousarray(wq[core * c.WSH:(core + 1) * c.WSH, :]),
            "wk_s": np.ascontiguousarray(wk[core * c.WSH:(core + 1) * c.WSH, :]),
            "wv_s": np.ascontiguousarray(wv[core * c.WSH:(core + 1) * c.WSH, :]),
            "wo_s": np.ascontiguousarray(wo[core * c.WSH:(core + 1) * c.WSH, :]),
            "wg_s": np.ascontiguousarray(wg[core * c.FSH:(core + 1) * c.FSH, :]),
            "wu_s": np.ascontiguousarray(wu[core * c.FSH:(core + 1) * c.FSH, :]),
            "wd_s": np.ascontiguousarray(wd[core * c.DSH:(core + 1) * c.DSH, :]),
            "biases": biases, "wmaps": wmaps, "idf": idf, "idb": idb,
        })
    res = run_bass_kernel_spmd(nc, in_maps, core_ids=list(range(c.NC)))
    out = np.empty((B, Sq, D), dtype=np.float32)
    for core in range(c.NC):
        b, half = core // 2, core % 2
        out[b, half * c.T:(half + 1) * c.T, :] = res.results[core]["out"]
    return out
